# version 19
# baseline (speedup 1.0000x reference)
import math
import contextlib
import numpy as np
import concourse.bass as bass
import concourse.mybir as mybir
from concourse.bass_utils import run_bass_kernel_spmd

F32 = mybir.dt.float32
BF16 = mybir.dt.bfloat16
U8 = mybir.dt.uint8
AF = mybir.ActivationFunctionType
ALU = mybir.AluOpType
AX = mybir.AxisListType.X

C = 1024
NH = 16
HD = 64
E = 8
HF = 2048
EPS = 1e-5
BLK = 2048
CCH = C // 128
FCH = HF // 128
RW = 160

QW = [128, 128, 256, 256] + [512] * 12


def _acol_map(T):
    m = {}
    col = 0
    for h in range(NH):
        qw = QW[h]
        for qb in range(T // qw):
            qe = qb * qw + qw - 1
            for kb in range(qe // 128 + 1):
                m[(h, qb, kb)] = col
                col += 1
    return m, col

AR_A = 0
AR_B = 32768
AR_C = 65536
AR_D = 98304
AR_E = 131072
AR_F = 147456
AR_TOTAL = 163840


def _slopes(n):
    def pow2(m):
        start = 2 ** (-(2 ** (-(math.log2(m) - 3))))
        return [start * start ** i for i in range(m)]
    if math.log2(n).is_integer():
        return pow2(n)
    c = 2 ** math.floor(math.log2(n))
    return pow2(c) + _slopes(2 * c)[0::2][:n - c]


class Sem:
    def __init__(self, nc, es, name):
        self.h = es.enter_context(nc.semaphore(name))
        self.n = 0


class Sched:
    ENGS = ("tensor", "scalar", "vector", "gpsimd", "sync")

    def __init__(self):
        self.ops = {e: [] for e in self.ENGS}
        self.last_wait = {e: {} for e in self.ENGS}

    def add(self, eng, fn, waits=(), inc=None, amt=1):
        if inc is not None and eng in ("gpsimd", "sync") and inc.n > 0:
            waits = list(waits) + [(inc, inc.n)]
        wl = []
        lw = self.last_wait[eng]
        for (s, v) in waits:
            if v <= 0 or lw.get(id(s), 0) >= v:
                continue
            lw[id(s)] = v
            wl.append((s, v))
        self.ops[eng].append((wl, fn, inc, amt))
        if inc is not None:
            inc.n += amt
            return inc.n
        return None

    def emit(self, block):
        for name in self.ENGS:
            oplist = self.ops[name]
            if not oplist:
                continue
            dec = getattr(block, name)

            def section(eng, _ops=oplist):
                for (wl, fn, inc, amt) in _ops:
                    for (s, v) in wl:
                        eng.wait_ge(s.h, v)
                    ins = fn(eng)
                    if inc is not None:
                        ins.then_inc(inc.h, amt)
            dec(section)


def build_core_program(T=2048):
    assert T % 512 == 0
    TC = T // 128
    QC = T // 512
    THC = TC // 2
    NPAIR_D = min(102, TC * 8)
    nc = bass.Bass("TRN2", target_bir_lowering=False)

    x_in = nc.dram_tensor("x", [T, C], F32, kind="ExternalInput").ap()
    Wq = nc.dram_tensor("Wq", [C, C], F32, kind="ExternalInput").ap()
    Wk = nc.dram_tensor("Wk", [C, C], F32, kind="ExternalInput").ap()
    Wv = nc.dram_tensor("Wv", [C, C], F32, kind="ExternalInput").ap()
    Wo = nc.dram_tensor("Wo", [C, C], F32, kind="ExternalInput").ap()
    W1 = nc.dram_tensor("W1", [E, C, HF], F32, kind="ExternalInput").ap()
    W2 = nc.dram_tensor("W2", [E, HF, C], F32, kind="ExternalInput").ap()
    b2v = nc.dram_tensor("b2", [E, C], F32, kind="ExternalInput").ap()
    rWv = nc.dram_tensor("rW", [C, E], F32, kind="ExternalInput").ap()
    lgb = nc.dram_tensor("lgb", [128, 2 * CCH], F32, kind="ExternalInput").ap()
    b1r = nc.dram_tensor("b1r", [128, E * FCH], F32, kind="ExternalInput").ap()
    rbc = nc.dram_tensor("rbc", [128, E], F32, kind="ExternalInput").ap()
    amap, acols = _acol_map(T)
    alibi_in = nc.dram_tensor("alibi", [128, acols], F32, kind="ExternalInput").ap()
    masks_in = nc.dram_tensor("masks", [128, 4 * 512], BF16, kind="ExternalInput").ap()
    ident_in = nc.dram_tensor("ident", [128, 128], BF16, kind="ExternalInput").ap()
    bcon_in = nc.dram_tensor("bcon", [128, 128], BF16, kind="ExternalInput").ap()
    y_out = nc.dram_tensor("y", [T, C], F32, kind="ExternalOutput").ap()
    xuscr = nc.dram_tensor("xuscr", [T, 2 * C], F32, kind="Internal").ap()

    es = contextlib.ExitStack()
    with es:
        arena = es.enter_context(nc.sbuf_tensor("arena", [128, AR_TOTAL], U8))
        ar_base = nc.lookup_mloc(arena).addr

        def vt(name, shape, dtype, byte_off):
            return nc.alloc_sbuf_tensor_at(name, shape, dtype, offset=ar_base + byte_off)

        Trz1 = vt("Trz1", [128, CCH * T], BF16, AR_A)
        TrAttn = vt("TrAttn", [128, CCH * T], BF16, AR_A)
        hidT = vt("hidT", [128, FCH * (T // 2)], BF16, AR_A)
        qT = vt("qT", [128, CCH * T], BF16, AR_B)
        fsl0 = vt("fsl0", [128, 8192], BF16, AR_B + 16384)
        fsl1 = vt("fsl1", [128, 8192], BF16, AR_B)
        kT = vt("kT", [128, CCH * T], BF16, AR_C)
        fsl2 = vt("fsl2", [128, 8192], BF16, AR_C)
        x0_t = vt("x0t", [128, 3072], F32, AR_C + 16384)
        u_t = vt("ut", [128, 4096], F32, AR_B)
        x1_t = vt("x1t", [128, 3072], F32, AR_F)
        accA = vt("accA", [128, 4096], F32, AR_E)
        accB = vt("accB", [128, 4096], F32, AR_C + 16384)
        vgD = vt("vgD", [128, NPAIR_D * RW], BF16, AR_D)
        Trz = vt("Trz", [128, CCH * T], BF16, AR_D)
        wsl = [vt("wsl0", [128, 4096], BF16, AR_E),
               vt("wsl1", [128, 4096], BF16, AR_E + 8192)]
        xpp = vt("xpp", [128, 4096], F32, AR_F)
        z1pp = vt("z1pp", [128, 3072], BF16, AR_D)
        slabs = vt("slabs", [128, 2048], BF16, AR_F)
        vgF = vt("vgF", [128, max(0, TC * 8 - NPAIR_D) * RW + 16], BF16, AR_F + 4096)
        zp_t = vt("zp", [128, 2048], BF16, AR_F + 12288)
        x1r_t = vt("x1r", [128, 2048], F32, AR_F)
        outb_t = vt("outb", [128, 2048], F32, AR_F + 8192)

        lgb_s = es.enter_context(nc.sbuf_tensor("lgb_s", [128, 2 * CCH], F32))
        b1_s = es.enter_context(nc.sbuf_tensor("b1_s", [128, E * FCH], F32))
        rbc_s = es.enter_context(nc.sbuf_tensor("rbc_s", [128, E], F32))
        alibi_s = es.enter_context(nc.sbuf_tensor("alibi_s", [128, acols], F32))
        masks_s = es.enter_context(nc.sbuf_tensor("masks_s", [128, 4 * 512], BF16))
        ident_s = es.enter_context(nc.sbuf_tensor("ident_s", [128, 128], BF16))
        bcon_s = es.enter_context(nc.sbuf_tensor("bcon_s", [128, 128], BF16))
        rW_s = es.enter_context(nc.sbuf_tensor("rW_s", [128, CCH * E], BF16))
        b2_s = es.enter_context(nc.sbuf_tensor("b2_s", [8, C], BF16))
        wT_s = es.enter_context(nc.sbuf_tensor("wT_s", [8, T], BF16))
        recb_s = es.enter_context(nc.sbuf_tensor("recb_s", [128, 2048], BF16))
        stg_s = es.enter_context(nc.sbuf_tensor("stg_s", [128, 2048], BF16))
        recf_s = es.enter_context(nc.sbuf_tensor("recf_s", [128, 512], F32))
        wS = es.enter_context(nc.sbuf_tensor("wS", [128, TC * E], F32))
        stP = es.enter_context(nc.sbuf_tensor("stP", [128, 128], F32))
        st3 = es.enter_context(nc.sbuf_tensor("st3", [128, 128], F32))
        lgS = es.enter_context(nc.sbuf_tensor("lgS", [128, 128], F32))
        stF = es.enter_context(nc.sbuf_tensor("stF", [128, 64], F32))
        epsv = es.enter_context(nc.sbuf_tensor("epsv", [128, 1], F32))
        wbf_t = es.enter_context(nc.sbuf_tensor("wbf_t", [128, 8 * 16], BF16))

        PB = [es.enter_context(nc.psum_tensor("pb%d" % i, [128, 512], F32)) for i in range(6)]
        PT = [es.enter_context(nc.psum_tensor("pt%d" % i, [128, 1024], BF16)) for i in range(2)]

        sTE = Sem(nc, es, "sTE")
        sACT = Sem(nc, es, "sACT")
        sDVE = Sem(nc, es, "sDVE")
        sGD = Sem(nc, es, "sGD")
        sSD = Sem(nc, es, "sSD")

        S = Sched()
        block = es.enter_context(nc.Block())

        wr = {}
        rd = {}

        def bank_w(bid):
            return [rd[bid]] if bid in rd else []

        def bank_r(bid):
            return [wr[bid]] if bid in wr else []

        def vgp(pair):
            if pair < NPAIR_D:
                return vgD[:, pair * RW:(pair + 1) * RW]
            p = pair - NPAIR_D
            return vgF[:, p * RW:(p + 1) * RW]

        for dst, src in ((lgb_s, lgb), (b1_s, b1r), (rbc_s, rbc), (alibi_s, alibi_in),
                         (masks_s, masks_in), (ident_s, ident_in), (bcon_s, bcon_in)):
            S.add("sync", lambda e, d=dst, s=src: e.dma_start(out=d[:, :], in_=s[:, :]),
                  inc=sSD, amt=16)
        SD_CONST = sSD.n
        S.add("gpsimd", lambda e: e.dma_start(
            out=rW_s[:, :].rearrange("p (cc e) -> p cc e", cc=CCH),
            in_=rWv.rearrange("(cc p) e -> p cc e", p=128)), inc=sGD, amt=16)
        S.add("gpsimd", lambda e: e.dma_start(out=b2_s[:, :], in_=b2v[:, :]), inc=sGD, amt=16)
        GD_SMALL = sGD.n
        S.add("vector", lambda e: e.memset(epsv[:, :], EPS), inc=sDVE)
        S.add("vector", lambda e: e.memset(recf_s[:, :], 0.0), inc=sDVE)
        S.add("vector", lambda e: e.memset(recb_s[:, :], 0.0), inc=sDVE)
        DVE_INIT = sDVE.n

        def w_half_load(Wm, hf):
            def f(e, Wm=Wm, hf=hf, slot=wsl[hf % 2]):
                return e.dma_start(
                    out=slot[:, :].rearrange("p (cc d) -> p cc d", cc=CCH),
                    in_=Wm[:, hf * 512:(hf + 1) * 512].rearrange("(cc p) d -> p cc d", p=128))
            return f

        GD_W = {}
        S.add("gpsimd", w_half_load(Wq, 0), inc=sGD, amt=16); GD_W[(0, 0)] = sGD.n
        S.add("gpsimd", w_half_load(Wq, 1), inc=sGD, amt=16); GD_W[(0, 1)] = sGD.n

        p0_norm_act = {}
        p0_tr_te = {}
        p0_sd = {}
        p0_pend = []
        for tc in range(TC):
            xb = xpp[:, (tc % 4) * 1024:(tc % 4) * 1024 + 1024]
            z1 = z1pp[:, (tc % 3) * 1024:(tc % 3) * 1024 + 1024]
            if tc % 2 == 0:
                s4 = (tc % 4) * 1024
                p0_sd[tc] = S.add("sync", lambda e, tc=tc, s4=s4: e.dma_start(
                    out=xpp[:, s4:s4 + 2048].rearrange("p (two q) -> p two q", two=2),
                    in_=x_in[tc * 128:(tc + 2) * 128, :].rearrange("(two p) c -> p two c", p=128)),
                    waits=[(sDVE, p0_norm_act.get(tc - 3, 0))], inc=sSD, amt=16)
            sd_x = p0_sd[tc - tc % 2]

            sp = stP[:, (tc % 2) * 64:(tc % 2) * 64 + 64]

            def stats0(e, xb=xb, sp=sp):
                e.drain()
                e.bn_stats(sp[:, 0:6], xb[:, 0:512])
                e.bn_stats(sp[:, 6:12], xb[:, 512:1024])
                e.drain()
                return e.bn_aggr(sp[:, 12:14], sp[:, 0:12])
            dve_st = S.add("vector", stats0,
                           waits=[(sSD, sd_x)], inc=sDVE)

            def rsq0(e, sp=sp):
                e.activation(sp[:, 14:15], sp[:, 13:14], AF.Ln, bias=epsv[:, 0:1], scale=1.0)
                e.drain()
                return e.activation(sp[:, 15:16], sp[:, 14:15], AF.Exp, bias=0.0, scale=-0.5)
            act_rq = S.add("scalar", rsq0, waits=[(sDVE, dve_st)], inc=sACT)

            def nrm0(e, sp=sp, z1=z1, xb=xb):
                e.drain()
                return e.tensor_scalar(z1, xb, sp[:, 12:13], sp[:, 15:16],
                                       ALU.subtract, ALU.mult)
            act_nm = S.add("vector", nrm0,
                           waits=[(sACT, act_rq), (sTE, p0_tr_te.get(tc - 3, 0))], inc=sDVE)
            p0_norm_act[tc] = act_nm

            def p0_tr(tc=tc, z1=z1):
                bid = ("pt", tc % 2)
                pt = PT[tc % 2]
                for cc in range(CCH):
                    te_tr = S.add("tensor", lambda e, z1=z1, cc=cc, pt=pt: e.transpose(
                        pt[:, cc * 128:(cc + 1) * 128], z1[:, cc * 128:(cc + 1) * 128],
                        ident_s[:, :]),
                        waits=[(sDVE, p0_norm_act[tc]), (sSD, SD_CONST)] +
                              (bank_w(bid) if cc == 0 else []), inc=sTE)
                wr[bid] = (sTE, sTE.n)
                dst = Trz1[:, 0:CCH * T].rearrange(
                    "p (cc t) -> p cc t", cc=CCH)[:, :, tc * 128:(tc + 1) * 128]
                d_ev = S.add("vector", lambda e, pt=pt, dst=dst: (
                    e.drain(), e.tensor_copy(dst, pt[:, 0:1024].rearrange(
                        "p (cc q) -> p cc q", cc=CCH)))[-1],
                    waits=[(sTE, sTE.n)], inc=sDVE)
                rd[bid] = (sDVE, d_ev)
                p0_tr_te[tc] = sTE.n
            p0_pend.append(p0_tr)
            if len(p0_pend) > 2:
                p0_pend.pop(0)()
        while p0_pend:
            p0_pend.pop(0)()
        P0_TEN = sTE.n
        P0_ACT = sACT.n
        P0_DVE = sDVE.n

        half_te = {}

        def qk_proj(wi, DST):
            for hf in range(2):
                gd = GD_W[(wi, hf)]
                if hf == 1:
                    half_te[(wi, 0)] = sTE.n
                for dc in range(4):
                    for tp in range(max(1, QC // 2)):
                        nb = min(2, QC)
                        bids = [("pb", 2 * (tp % 2) + i) for i in range(nb)]
                        for cc in range(CCH):
                            lw = wsl[hf % 2][:, cc * 512 + dc * 128: cc * 512 + dc * 128 + 128]
                            S.add("tensor", lambda e, lw=lw: e.ldweights(lw),
                                  waits=[(sGD, gd), (sACT, P0_ACT), (sDVE, P0_DVE)])
                            for half in range(nb):
                                ts4 = tp * 2 + half
                                rhs = Trz1[:, cc * T + ts4 * 512: cc * T + ts4 * 512 + 512]
                                te_mm = S.add("tensor", lambda e, b=PB[2 * (tp % 2) + half],
                                              lw=lw, rhs=rhs, cc=cc:
                                              e.matmul(b[:, 0:512], lw, rhs,
                                                       start=(cc == 0), stop=(cc == CCH - 1)),
                                              waits=bank_w(bids[half]) if cc == 0 else [], inc=sTE)
                                if cc == CCH - 1:
                                    wr[bids[half]] = (sTE, te_mm)
                        for half in range(nb):
                            ts4 = tp * 2 + half
                            d = DST[:, (hf * 4 + dc) * T + ts4 * 512:
                                    (hf * 4 + dc) * T + ts4 * 512 + 512]
                            d_ev = S.add("vector", lambda e, b=PB[2 * (tp % 2) + half], d=d: (
                                e.drain(), e.tensor_copy(d, b[:, 0:512]))[-1],
                                waits=[wr[bids[half]]], inc=sDVE)
                            rd[bids[half]] = (sDVE, d_ev)
        qk_proj(0, qT)
        te_q_done = sTE.n
        S.add("gpsimd", w_half_load(Wk, 0), waits=[(sTE, half_te[(0, 0)])], inc=sGD, amt=16)
        GD_W[(1, 0)] = sGD.n
        S.add("gpsimd", w_half_load(Wk, 1), waits=[(sTE, te_q_done)], inc=sGD, amt=16)
        GD_W[(1, 1)] = sGD.n
        qk_proj(1, kT)
        te_k_done = sTE.n
        P1_ACT_QK = sACT.n
        S.add("gpsimd", w_half_load(Wv, 0), waits=[(sTE, half_te[(1, 0)])], inc=sGD, amt=16)
        GD_W[(2, 0)] = sGD.n
        S.add("gpsimd", w_half_load(Wv, 1), waits=[(sTE, te_k_done)], inc=sGD, amt=16)
        GD_W[(2, 1)] = sGD.n

        def vg_ones(e):
            e.drain()
            e.memset(vgD[:, :], 0.0)
            if TC * 8 > NPAIR_D:
                e.memset(vgF[:, :], 0.0)
            e.drain()
            for pair in range(TC * 8):
                e.memset(vgp(pair)[:, 64:65], 1.0)
            e.drain()
            return e.engine_nop()
        dve_ones = S.add("vector", vg_ones, waits=[(sTE, P0_TEN)], inc=sDVE)

        for tc in range(TC):
            for ds in range(2):
                gd = GD_W[(2, ds)]
                bid = ("pb", 4 + ds)
                for cc in range(CCH):
                    lw = Trz1[:, cc * T + tc * 128: cc * T + tc * 128 + 128]
                    S.add("tensor", lambda e, lw=lw: e.ldweights(lw), waits=[(sGD, gd)])
                    rhs = wsl[ds % 2][:, cc * 512: cc * 512 + 512]
                    te_mm = S.add("tensor", lambda e, b=PB[4 + ds], lw=lw, rhs=rhs, cc=cc:
                                  e.matmul(b[:, 0:512], lw, rhs,
                                           start=(cc == 0), stop=(cc == CCH - 1)),
                                  waits=(bank_w(bid) + [(sDVE, dve_ones)]) if cc == 0 else [],
                                  inc=sTE)
                wr[bid] = (sTE, te_mm)

                def vevac(e, b=PB[4 + ds], tc=tc, ds=ds):
                    from concourse.ap import AP
                    e.drain()
                    pair0 = tc * 8 + ds * 4
                    if pair0 + 4 <= NPAIR_D:
                        base = vgD.ap()
                        part = list(base.ap[0])
                        dst = AP(tensor=base.tensor, offset=base.offset + pair0 * RW,
                                 ap=[part, [RW, 4], [96, 2], [1, 64]])
                        e.tensor_copy(dst, b[:, 0:512].rearrange(
                            "p (pr two d) -> p pr two d", pr=4, two=2))
                    else:
                        for j in range(8):
                            h = ds * 8 + j
                            pr = tc * 8 + h // 2
                            off = 0 if h % 2 == 0 else 96
                            e.tensor_copy(vgp(pr)[:, off:off + 64], b[:, j * 64:(j + 1) * 64])
                    e.drain()
                    return e.engine_nop()
                d_ev = S.add("vector", vevac, waits=[wr[bid]], inc=sDVE)
                rd[bid] = (sDVE, d_ev)
        P1_TE = sTE.n
        P1_DVE = sDVE.n
        S.add("gpsimd", w_half_load(Wo, 0), waits=[(sTE, P1_TE)], inc=sGD, amt=16)
        GD_WO0 = sGD.n
        S.add("gpsimd", w_half_load(Wo, 1), waits=[(sTE, P1_TE)], inc=sGD, amt=16)
        GD_WO1 = sGD.n

        slab_free = [0, 0, 0, 0]
        stg_free = [0, 0, 0, 0]
        gi = 0
        pend_evac = [None, None]
        for pi in range(NH // 2):
            qw = QW[2 * pi]
            dc = pi
            for qb in range(T // qw):
                qs = qb * qw
                qe = qs + qw - 1
                K = qe // 128 + 1
                gpar = gi % 2
                pv_bid = {0: ("pb", 2 + 2 * gpar), 1: ("pb", 3 + 2 * gpar)}
                pvb_i = {0: 2 + 2 * gpar, 1: 3 + 2 * gpar}
                exp_done = {}
                mask_done = {}
                if pend_evac[0] is not None:
                    for f in pend_evac[0][0]:
                        f(d_rc=pend_evac[0][1][f])
                    pend_evac[0] = None
                if pend_evac[1] is not None:
                    rfs, efs = pend_evac[1]
                    drcs = {}
                    for rf, ef in zip(rfs, efs):
                        drcs[ef] = rf()
                    pend_evac[0] = (efs, drcs)
                    pend_evac[1] = None

                def emit_score(kb, hp):
                    h = 2 * pi + hp
                    sbi = hp
                    s_bid = ("pb", sbi)
                    r0 = hp * 64
                    tile = None if hp == 0 else (64, 0)
                    lwk = kT[r0:r0 + 64, dc * T + kb * 128: dc * T + kb * 128 + 128]
                    qrhs = qT[r0:r0 + 64, dc * T + qs: dc * T + qs + qw]
                    S.add("tensor", lambda e, lwk=lwk, tile=tile: e.ldweights(lwk, tile_position=tile),
                          waits=[(sACT, P1_ACT_QK)] + bank_w(s_bid))
                    te_s = S.add("tensor", lambda e, sb=PB[sbi], lwk=lwk, qrhs=qrhs, tile=tile,
                                 qw=qw: e.matmul(sb[:, 0:qw], lwk, qrhs, start=True, stop=True,
                                                 tile_position=tile), inc=sTE)
                    wr[s_bid] = (sTE, te_s)
                    slot = (kb % 2) * 2 + hp
                    slab = slabs[:, slot * 512: slot * 512 + qw]
                    ac = amap[(h, qb, kb)]
                    bias = alibi_s[:, ac:ac + 1]
                    a_ex = S.add("scalar", lambda e, slab=slab, sb=PB[sbi], bias=bias, qw=qw:
                                 e.activation(slab, sb[:, 0:qw], AF.Exp, bias=bias, scale=0.125),
                                 waits=[(sTE, te_s), (sTE, slab_free[slot]), (sSD, SD_CONST)],
                                 inc=sACT)
                    rd[s_bid] = (sACT, a_ex)
                    exp_done[(kb, hp)] = a_ex
                    if kb * 128 + 127 > qs:
                        oi = (kb * 128 - qs) // 128
                        d_mk = S.add("vector", lambda e, slab=slab, oi=oi, qw=qw: (
                            e.drain(), e.tensor_mul(slab, slab,
                                                    masks_s[:, oi * 512:oi * 512 + qw]))[-1],
                            waits=[(sACT, a_ex), (sSD, SD_CONST)], inc=sDVE)
                        mask_done[(kb, hp)] = d_mk

                def emit_pv(kb, hp):
                    slot = (kb % 2) * 2 + hp
                    slab = slabs[:, slot * 512: slot * 512 + qw]
                    pv_w = ([(sDVE, mask_done[(kb, hp)])] if (kb, hp) in mask_done
                            else [(sACT, exp_done[(kb, hp)])])
                    pr = kb * 8 + pi
                    stat = vgp(pr)[:, 0:65] if hp == 0 else vgp(pr)[:, 32:160]
                    if kb == 0:
                        pv_w += bank_w(pv_bid[hp]) + [(sDVE, P1_DVE)]
                    mtop = 65 if hp == 0 else 128
                    S.add("tensor", lambda e, stat=stat: e.ldweights(stat), waits=pv_w)
                    te_pv = S.add("tensor", lambda e, pvb=PB[pvb_i[hp]], stat=stat, slab=slab,
                                  kb=kb, K=K, mtop=mtop, qw=qw:
                                  e.matmul(pvb[0:mtop, 0:qw], stat, slab,
                                           start=(kb == 0), stop=(kb == K - 1)),
                                  inc=sTE)
                    slab_free[slot] = te_pv
                    return te_pv

                te_pv_d = {}
                for kb in range(K):
                    emit_score(kb, 0)
                    emit_score(kb, 1)
                    if kb >= 1:
                        te_pv_d[0] = emit_pv(kb - 1, 0)
                        te_pv_d[1] = emit_pv(kb - 1, 1)
                te_pv_d[0] = emit_pv(K - 1, 0)
                te_pv_d[1] = emit_pv(K - 1, 1)
                wr[pv_bid[0]] = (sTE, te_pv_d[0])
                wr[pv_bid[1]] = (sTE, te_pv_d[1])

                rfs = []
                efs = []
                for hp in range(2):
                    h = 2 * pi + hp
                    drow = 64 if hp == 0 else 32
                    rslot = gpar * 2 + hp

                    def emit_recip(pvbi=pvb_i[hp], drow=drow, rslot=rslot, qw=qw):
                        def recip_fn(e, pvb=PB[pvbi], drow=drow, rslot=rslot, qw=qw):
                            e.drain()
                            e.reciprocal(recf_s[drow:drow + 1, 0:qw], pvb[drow:drow + 1, 0:qw])
                            e.drain()
                            return e.tensor_copy(
                                recb_s[drow:drow + 1, rslot * 512: rslot * 512 + qw],
                                recf_s[drow:drow + 1, 0:qw])
                        return S.add("vector", recip_fn,
                                     waits=[(sTE, te_pv_d[hp]), (sDVE, DVE_INIT)], inc=sDVE)

                    def emit_evac(pvbi=pvb_i[hp], pvbid=pv_bid[hp], te_pv=te_pv_d[hp],
                                  hp=hp, drow=drow, rslot=rslot, qw=qw, dc=dc, qs=qs,
                                  d_rc=0):
                        r0 = hp * 64
                        tileb = (64, 0) if drow == 64 else (32, 0)
                        stg = stg_s[:, rslot * 512: rslot * 512 + qw]
                        a_st = S.add("scalar", lambda e, stg=stg, pvb=PB[pvbi], r0=r0, qw=qw:
                                     e.copy(stg[r0:r0 + 64, :], pvb[r0:r0 + 64, 0:qw]),
                                     waits=[(sTE, te_pv), (sDVE, stg_free[rslot])], inc=sACT)
                        S.add("tensor", lambda e, tileb=tileb, drow=drow: e.ldweights(
                            bcon_s[drow:drow + 32, :], tile_position=tileb),
                            waits=[(sDVE, d_rc), (sSD, SD_CONST), (sACT, a_st)])
                        te_bc = S.add("tensor", lambda e, bcb=PB[pvbi], tileb=tileb, drow=drow,
                                      rslot=rslot, qw=qw:
                                      e.matmul(bcb[:, 0:qw], bcon_s[drow:drow + 32, :],
                                               recb_s[drow:drow + 32,
                                                      rslot * 512: rslot * 512 + qw],
                                               start=True, stop=True, tile_position=tileb),
                                      inc=sTE)
                        dsta = TrAttn[r0:r0 + 64, dc * T + qs: dc * T + qs + qw]
                        d_ml = S.add("vector", lambda e, dsta=dsta, stg=stg, bcb=PB[pvbi],
                                     r0=r0, qw=qw: (
                            e.drain(), e.tensor_mul(dsta, stg[r0:r0 + 64, :],
                                                    bcb[r0:r0 + 64, 0:qw]))[-1],
                            waits=[(sTE, te_bc), (sACT, a_st), (sTE, P1_TE)], inc=sDVE)
                        rd[pvbid] = (sDVE, d_ml)
                        stg_free[rslot] = d_ml
                    rfs.append(emit_recip)
                    efs.append(emit_evac)
                pend_evac[1] = (rfs, efs)
                gi += 1
        if pend_evac[1] is not None:
            rfs, efs = pend_evac[1]
            drcs = {}
            for rf, ef in zip(rfs, efs):
                drcs[ef] = rf()
            if pend_evac[0] is not None:
                for f in pend_evac[0][0]:
                    f(d_rc=pend_evac[0][1][f])
            pend_evac[0] = (efs, drcs)
            pend_evac[1] = None
        if pend_evac[0] is not None:
            for f in pend_evac[0][0]:
                f(d_rc=pend_evac[0][1][f])
        P2_TE = sTE.n
        P2_DVE = sDVE.n
        P2_ACT = sACT.n

        def acc_ch(i):
            if i < 4:
                return accA[:, i * 1024:(i + 1) * 1024]
            return accB[:, (i - 4) * 1024:(i - 4) * 1024 + 1024]

        fsl = [fsl0, fsl1, fsl2]
        gd_half = {}
        half_slot = {}
        _gctr = [0]

        def ffn_w_load(th, e_, kind, hf, te_guard=None, extra=(), slot=None):
            if th >= 2:
                return
            if slot is None:
                slot = _gctr[0] % 3
            _gctr[0] += 1
            half_slot[(th, e_, kind, hf)] = slot

            def f(e, e_=e_, kind=kind, hf=hf, slot=slot):
                if kind == 1:
                    src = W1[e_, :, hf * 1024:(hf + 1) * 1024].rearrange(
                        "(cc p) f -> p cc f", p=128)
                else:
                    src = W2[e_, hf * 1024:(hf + 1) * 1024, :].rearrange(
                        "(fc p) h -> p fc h", p=128)
                return e.dma_start(out=fsl[slot][:, :].rearrange("p (a b) -> p a b", a=8), in_=src)
            S.add("gpsimd", f, waits=[(sTE, sTE.n if te_guard is None else te_guard)] + list(extra),
                  inc=sGD, amt=16)
            gd_half[(th, e_, kind, hf)] = sGD.n


        ffn_w_load(0, 0, 1, 0, te_guard=P2_TE, slot=0)
        ffn_w_load(0, 0, 2, 0, te_guard=P2_TE, slot=2)
        wo_last_te = 0
        trz_act = {}
        xu_spill_sd = {}
        zp_te = {}
        w_done_dve = {}
        zp_act = {}

        def stats_fn(buf, so, stt):
            def f(e, buf=buf, so=so, stt=stt):
                e.drain()
                e.bn_stats(stt[:, so:so + 6], buf[:, 0:512])
                e.bn_stats(stt[:, so + 6:so + 12], buf[:, 512:1024])
                e.drain()
                return e.bn_aggr(stt[:, so + 12:so + 14], stt[:, so:so + 12])
            return f

        def rsq_fn(so, stt, out=None, src=None, short=False):
            def f(e, so=so, stt=stt, out=out, src=src, short=short):
                e.activation(stt[:, so + 14:so + 15], stt[:, so + 13:so + 14], AF.Ln,
                             bias=epsv[:, 0:1], scale=1.0)
                e.drain()
                r = e.activation(stt[:, so + 15:so + 16], stt[:, so + 14:so + 15], AF.Exp,
                                 bias=0.0, scale=-0.5)
                if short:
                    return r
                e.drain()
                e.activation(stt[:, so + 16:so + 17], stt[:, so + 12:so + 13], AF.Copy,
                             bias=0.0, scale=-1.0)
                e.drain()
                r = e.activation(stt[:, so + 17:so + 18], stt[:, so + 16:so + 17], AF.Copy,
                                 bias=0.0, scale=stt[:, so + 15:so + 16])
                if out is not None:
                    e.drain()
                    r = e.activation(out, src, AF.Identity, bias=stt[:, so + 17:so + 18],
                                     scale=stt[:, so + 15:so + 16])
                return r
            return f

        x0_sd = {}
        d_x1_d = {}
        x1sp_sd = {}
        usp_sd = {}
        a_ev_d = {}
        r1_d = {}
        r2u_d = {}

        def p3_A1(tc):
            sl3 = (tc % 3) * 1024
            x0_sd[tc] = S.add("sync", lambda e, tc=tc, sl3=sl3: e.dma_start(
                out=x0_t[:, sl3:sl3 + 1024], in_=x_in[tc * 128:(tc + 1) * 128, :]),
                waits=[(sTE, P2_TE), (sDVE, d_x1_d.get(tc - 3, 0))], inc=sSD, amt=16)
            bids = [("pb", 0), ("pb", 1)]
            for cc in range(CCH):
                lw = TrAttn[:, cc * T + tc * 128: cc * T + tc * 128 + 128]
                S.add("tensor", lambda e, lw=lw: e.ldweights(lw),
                      waits=[(sDVE, P2_DVE), (sGD, GD_WO1)] +
                            (bank_w(bids[0]) + bank_w(bids[1]) if cc == 0 else []))
                for cs in range(2):
                    rhs = wsl[cs][:, cc * 512: cc * 512 + 512]
                    te_mm = S.add("tensor", lambda e, b=PB[cs], lw=lw, rhs=rhs, cc=cc:
                                  e.matmul(b[:, 0:512], lw, rhs,
                                           start=(cc == 0), stop=(cc == CCH - 1)), inc=sTE)
            wr[bids[0]] = (sTE, te_mm)
            wr[bids[1]] = (sTE, te_mm)
            x1b = x1_t[:, sl3:sl3 + 1024]

            def wo_ev(e, x1b=x1b):
                e.drain()
                e.copy(x1b[:, 0:512], PB[0][:, 0:512])
                e.drain()
                return e.copy(x1b[:, 512:1024], PB[1][:, 0:512])
            a_ev = S.add("scalar", wo_ev,
                         waits=[(sTE, te_mm), (sSD, x1sp_sd.get(tc - 3, 0))], inc=sACT)
            a_ev_d[tc] = a_ev
            rd[bids[0]] = (sACT, a_ev)
            rd[bids[1]] = (sACT, a_ev)
            return te_mm

        def p3_A2(tc):
            par = tc % 2
            sl3 = (tc % 3) * 1024
            x1b = x1_t[:, sl3:sl3 + 1024]
            st = st3[:, par * 64: par * 64 + 64]
            d_s1 = S.add("vector", stats_fn(x1b, 0, st), waits=[(sACT, a_ev_d[tc])], inc=sDVE)
            r1_d[tc] = S.add("scalar", rsq_fn(0, st, short=True), waits=[(sDVE, d_s1)], inc=sACT)

        def p3_A3(tc):
            par = tc % 2
            sl3 = (tc % 3) * 1024
            x1b = x1_t[:, sl3:sl3 + 1024]
            x0v = x0_t[:, sl3:sl3 + 1024]
            ub = u_t[:, (tc % 4) * 1024:(tc % 4) * 1024 + 1024]
            st = st3[:, par * 64: par * 64 + 64]

            def x1_fn(e, x1b=x1b, x0v=x0v, st=st):
                e.drain()
                e.tensor_scalar(x1b, x1b, st[:, 12:13], st[:, 15:16], ALU.subtract, ALU.mult)
                e.drain()
                return e.tensor_add(x1b, x1b, x0v)
            d_x1 = S.add("vector", x1_fn, waits=[(sACT, r1_d[tc]), (sSD, x0_sd[tc])], inc=sDVE)
            d_x1_d[tc] = d_x1
            x1sp_sd[tc] = S.add("sync", lambda e, x1b=x1b, tc=tc: e.dma_start(
                out=xuscr[tc * 128:(tc + 1) * 128, 1024:2048], in_=x1b),
                waits=[(sDVE, d_x1)], inc=sSD, amt=16)
            d_s2 = S.add("vector", stats_fn(x1b, 18, st), inc=sDVE)
            r2u_d[tc] = S.add("scalar", rsq_fn(18, st, out=ub, src=x1b),
                              waits=[(sDVE, d_s2), (sSD, usp_sd.get(tc - 3, 0))], inc=sACT)

        def p3_A4(tc):
            par = tc % 2
            ub = u_t[:, (tc % 4) * 1024:(tc % 4) * 1024 + 1024]
            st = st3[:, par * 64: par * 64 + 64]
            d_s3 = S.add("vector", stats_fn(ub, 36, st), waits=[(sACT, r2u_d[tc])], inc=sDVE)
            zp = zp_t[:, par * 1024: par * 1024 + 1024]
            a_r3 = S.add("scalar", rsq_fn(36, st, out=zp, src=ub),
                         waits=[(sDVE, d_s3), (sTE, zp_te.get(tc - 2, 0))], inc=sACT)
            zp_act[tc] = a_r3
            usp_sd[tc] = S.add("sync", lambda e, tc=tc, ub=ub: e.dma_start(
                out=xuscr[tc * 128:(tc + 1) * 128, 0:1024], in_=ub),
                waits=[(sACT, r2u_d[tc])], inc=sSD, amt=16)
            xu_spill_sd[tc] = usp_sd[tc]

        def p3_B1(tc):
            par = tc % 2
            zp = zp_t[:, par * 1024: par * 1024 + 1024]
            bid = ("pt", tc % 2)
            pt = PT[tc % 2]
            for cc in range(CCH):
                te_tr = S.add("tensor", lambda e, zp=zp, cc=cc, pt=pt: e.transpose(
                    pt[:, cc * 128:(cc + 1) * 128], zp[:, cc * 128:(cc + 1) * 128],
                    ident_s[:, :]),
                    waits=[(sACT, zp_act[tc])] + (bank_w(bid) if cc == 0 else []), inc=sTE)
            wr[bid] = (sTE, sTE.n)
            for cc in range(CCH):
                dst = Trz[:, cc * T + tc * 128: cc * T + tc * 128 + 128]
                a_tz = S.add("scalar", lambda e, pt=pt, dst=dst, cc=cc: e.activation(
                    dst, pt[:, cc * 128:(cc + 1) * 128], AF.Identity,
                    bias=lgb_s[:, CCH + cc:CCH + cc + 1],
                    scale=lgb_s[:, cc:cc + 1]),
                    waits=[(sTE, sTE.n), (sTE, P2_TE)], inc=sACT)
            rd[bid] = (sACT, sACT.n)
            zp_te[tc] = sTE.n
            trz_act[tc] = sACT.n

        def p3_B23(tc):
            par = tc % 2
            lg = lgS[:, par * 64: par * 64 + 64]
            lg_bid = ("pb", 2)
            for cc in range(CCH):
                lw = Trz[:, cc * T + tc * 128: cc * T + tc * 128 + 128]
                S.add("tensor", lambda e, lw=lw: e.ldweights(lw),
                      waits=[(sACT, trz_act[tc]), (sGD, GD_SMALL)] +
                            (bank_w(lg_bid) if cc == 0 else []))
                te_lg = S.add("tensor", lambda e, lw=lw, cc=cc:
                              e.matmul(PB[2][:, 0:E], lw, rW_s[:, cc * E:(cc + 1) * E],
                                       start=(cc == 0), stop=(cc == CCH - 1)), inc=sTE)
            wr[lg_bid] = (sTE, te_lg)

            def rt1(e, lg=lg):
                e.drain()
                e.tensor_add(lg[:, 0:8], PB[2][:, 0:E], rbc_s[:, :])
                e.drain()
                return e.reduce_max(lg[:, 8:9], lg[:, 0:8], axis=AX, negate=True)
            d_r1 = S.add("vector", rt1, waits=[(sTE, te_lg), (sSD, SD_CONST)], inc=sDVE)
            rd[lg_bid] = (sDVE, d_r1)
            a_re_d[tc] = S.add("scalar", lambda e, lg=lg: e.activation(
                lg[:, 9:17], lg[:, 0:8], AF.Exp, bias=lg[:, 8:9], scale=1.0),
                waits=[(sDVE, d_r1)], inc=sACT)

        def p3_B3(tc):
            par = tc % 2
            lg = lgS[:, par * 64: par * 64 + 64]
            wv = wS[:, tc * E:(tc + 1) * E]
            wb = wbf_t[:, (tc % 16) * 8:(tc % 16) * 8 + 8]

            def rt2(e, wv=wv, wb=wb, lg=lg):
                ex = lg[:, 9:17]
                e.drain()
                e.reduce_max(lg[:, 17:18], ex, axis=AX)
                e.drain()
                e.tensor_scalar(lg[:, 18:26], ex, lg[:, 17:18], None, ALU.is_equal)
                e.drain()
                e.tensor_tensor(lg[:, 26:34], ex, lg[:, 18:26], ALU.mult)
                e.drain()
                e.tensor_sub(lg[:, 34:42], ex, lg[:, 26:34])
                e.drain()
                e.reduce_max(lg[:, 42:43], lg[:, 34:42], axis=AX)
                e.drain()
                e.tensor_scalar(lg[:, 43:51], ex, lg[:, 42:43], None, ALU.is_ge)
                e.drain()
                e.tensor_tensor(lg[:, 51:59], ex, lg[:, 43:51], ALU.mult)
                e.drain()
                e.tensor_add(lg[:, 59:60], lg[:, 17:18], lg[:, 42:43])
                e.drain()
                e.tensor_scalar(lg[:, 60:61], lg[:, 59:60], 1e-8, None, ALU.max)
                e.drain()
                e.reciprocal(lg[:, 61:62], lg[:, 60:61])
                e.drain()
                e.tensor_scalar(wv, lg[:, 51:59], lg[:, 61:62], None, ALU.mult)
                e.drain()
                return e.tensor_copy(wb, wv)
            d_w = S.add("vector", rt2, waits=[(sACT, a_re_d[tc])], inc=sDVE)
            w_done_dve[tc] = d_w

        a_re_d = {}
        for i in range(TC + 8):
            if 0 <= i - 7 < TC:
                p3_B3(i - 7)
            if 0 <= i - 6 < TC:
                p3_B23(i - 6)
            if 0 <= i - 5 < TC:
                p3_B1(i - 5)
            if i < TC:
                wo_last_te = p3_A1(i)
            if 0 <= i - 1 < TC:
                p3_A2(i - 1)
            if 0 <= i - 2 < TC:
                p3_A3(i - 2)
            if 0 <= i - 3 < TC:
                p3_A4(i - 3)
        ffn_w_load(0, 0, 1, 1, te_guard=P2_TE, slot=1,
                   extra=[(sSD, usp_sd[TC - 1]), (sDVE, sDVE.n)])
        P3_TE = sTE.n
        P3_DVE = sDVE.n
        P3_SD = sSD.n

        acc_reload_sd = {}
        final_stt = {}
        acc_b2d = {}
        final_read_dve = {}
        u_rel_guard = []

        def emit_acc_chunk(th, i):
            tc = th * THC + i
            wl = [(sSD, P3_SD), (sDVE, P3_DVE)]
            if i < 4:
                wl.append((sTE, P3_TE))
            if th == 1:
                wl.append((sDVE, final_read_dve.get((0, i), 0)))
            acc_reload_sd[(th, i)] = S.add("sync", lambda e, tc=tc, i=i: e.dma_start(
                out=acc_ch(i), in_=xuscr[tc * 128:(tc + 1) * 128, 0:1024]), waits=wl,
                inc=sSD, amt=16)
            bidw = ("pt", 0)
            wb = wbf_t[:, (tc % 16) * 8:(tc % 16) * 8 + 8]
            te_wt = S.add("tensor", lambda e, wb=wb: e.transpose(
                PT[0][0:8, 0:128], wb, ident_s[:, :]),
                waits=[(sDVE, w_done_dve[tc]), (sTE, P3_TE)] + bank_w(bidw), inc=sTE)
            wr[bidw] = (sTE, te_wt)
            a_wt = S.add("scalar", lambda e, tc=tc: e.copy(
                wT_s[0:8, tc * 128:(tc + 1) * 128], PT[0][0:8, 0:128]),
                waits=[(sTE, te_wt)], inc=sACT)
            rd[bidw] = (sACT, a_wt)
            b2_bids = [("pb", 4), ("pb", 5)]
            S.add("tensor", lambda e, tc=tc: e.ldweights(wT_s[0:8, tc * 128:(tc + 1) * 128]),
                  waits=[(sACT, a_wt), (sGD, GD_SMALL)] +
                        bank_w(b2_bids[0]) + bank_w(b2_bids[1]))
            for hs in range(2):
                te_b2 = S.add("tensor", lambda e, tc=tc, hs=hs:
                              e.matmul(PB[4 + hs][:, 0:512],
                                       wT_s[0:8, tc * 128:(tc + 1) * 128],
                                       b2_s[0:8, hs * 512:(hs + 1) * 512],
                                       start=True, stop=True), inc=sTE)
                wr[b2_bids[hs]] = (sTE, te_b2)

            def b2add(e, i=i):
                ac = acc_ch(i)
                e.drain()
                e.tensor_add(ac[:, 0:512], ac[:, 0:512], PB[4][:, 0:512])
                e.drain()
                return e.tensor_add(ac[:, 512:1024], ac[:, 512:1024], PB[5][:, 0:512])
            d_b2 = S.add("vector", b2add,
                         waits=[(sTE, te_b2), (sSD, acc_reload_sd[(th, i)])], inc=sDVE)
            rd[b2_bids[0]] = (sDVE, d_b2)
            rd[b2_bids[1]] = (sDVE, d_b2)
            acc_b2_dve = d_b2
            acc_reload_sd[(th, i)] = acc_reload_sd[(th, i)]
            acc_b2d[(th, i)] = acc_b2_dve

        gelu_act = {}
        final_store_sd = {}

        def emit_final(th, i):
            tc = th * THC + i
            par = i % 2
            so = par * 32
            x1r = x1r_t[:, par * 1024: par * 1024 + 1024]
            ob = outb_t[:, par * 1024: par * 1024 + 1024]
            if par == 0:
                wl = [(sDVE, final_read_dve.get((th, i - 2), 0)),
                      (sDVE, final_read_dve.get((th, i - 1), 0)),
                      (sDVE, final_read_dve.get((th - 1, THC - 2), 0)),
                      (sDVE, final_read_dve.get((th - 1, THC - 1), 0)),
                      (sSD, xu_spill_sd[TC - 1]), (sTE, zp_te.get(TC - 1, 0))]
                S.add("sync", lambda e, tc=tc: e.dma_start(
                    out=x1r_t[:, 0:2048].rearrange("p (two q) -> p two q", two=2),
                    in_=xuscr[tc * 128:(tc + 2) * 128, 1024:2048].rearrange(
                        "(two p) c -> p two c", p=128)), waits=wl, inc=sSD, amt=16)
            sd_r = sSD.n

            def fstat(e, i=i, so=so):
                ac = acc_ch(i)
                e.drain()
                e.bn_stats(stF[:, so:so + 6], ac[:, 0:512])
                e.bn_stats(stF[:, so + 6:so + 12], ac[:, 512:1024])
                e.drain()
                return e.bn_aggr(stF[:, so + 12:so + 14], stF[:, so:so + 12])
            d_fs = S.add("vector", fstat,
                         waits=[(sDVE, final_stt.get((th, i), 0))], inc=sDVE)

            def frsq(e, so=so):
                e.activation(stF[:, so + 14:so + 15], stF[:, so + 13:so + 14], AF.Ln,
                             bias=epsv[:, 0:1], scale=1.0)
                e.drain()
                e.activation(stF[:, so + 15:so + 16], stF[:, so + 14:so + 15], AF.Exp,
                             bias=0.0, scale=-0.5)
                e.drain()
                e.activation(stF[:, so + 16:so + 17], stF[:, so + 12:so + 13], AF.Copy,
                             bias=0.0, scale=-1.0)
                e.drain()
                return e.activation(stF[:, so + 17:so + 18], stF[:, so + 16:so + 17], AF.Copy,
                                    bias=0.0, scale=stF[:, so + 15:so + 16])
            a_fr = S.add("scalar", frsq, waits=[(sDVE, d_fs)], inc=sACT)

            def fout(e, i=i, x1r=x1r, ob=ob, so=so):
                ac = acc_ch(i)
                e.drain()
                e.scalar_tensor_tensor(ob, ac, stF[:, so + 15:so + 16], x1r, ALU.mult, ALU.add)
                e.drain()
                return e.tensor_scalar(ob, ob, stF[:, so + 17:so + 18], None, ALU.add)
            d_fo = S.add("vector", fout,
                         waits=[(sACT, a_fr), (sSD, sd_r),
                                (sSD, final_store_sd.get((th, i - 2), 0))], inc=sDVE)
            final_read_dve[(th, i)] = d_fo
            if par == 1:
                final_store_sd[(th, i - 1)] = final_store_sd[(th, i)] = S.add(
                    "sync", lambda e, tc=tc: e.dma_start(
                        out=y_out[(tc - 1) * 128:(tc + 1) * 128, :].rearrange(
                            "(two p) c -> p two c", p=128),
                        in_=outb_t[:, 0:2048].rearrange("p (two q) -> p two q", two=2)),
                    waits=[(sDVE, final_read_dve[(th, i - 1)]), (sDVE, d_fo)],
                    inc=sSD, amt=16)

        for th in range(2):
            for e_ in range(E):
                nth, ne = (th, e_ + 1) if e_ < E - 1 else (th + 1, 0)
                HL = T // 2
                n_tsl = max(1, HL // 512)
                tw = min(512, HL)
                for fc in range(FCH):
                    hf = fc // 8
                    gd = gd_half[(th, e_, 1, hf)]
                    slot = half_slot[(th, e_, 1, hf)]
                    bk = [2 * (fc % 2), 2 * (fc % 2) + 1]
                    w1_bids = [("pb", bk[0]), ("pb", bk[1])]
                    for cc in range(CCH):
                        lw = fsl[slot][:, cc * 1024 + (fc % 8) * 128:
                                       cc * 1024 + (fc % 8) * 128 + 128]
                        S.add("tensor", lambda e, lw=lw: e.ldweights(lw),
                              waits=[(sGD, gd), (sACT, trz_act.get(th * THC + THC - 1, 0))] +
                                    (bank_w(w1_bids[0]) + bank_w(w1_bids[1]) if cc == 0 else []))
                        for tsl in range(n_tsl):
                            rhs = Trz[:, cc * T + th * HL + tsl * tw:
                                      cc * T + th * HL + tsl * tw + tw]
                            te_mm = S.add("tensor", lambda e, b=PB[bk[tsl]], lw=lw, rhs=rhs, cc=cc,
                                          tw=tw: e.matmul(b[:, 0:tw], lw, rhs,
                                                          start=(cc == 0), stop=(cc == CCH - 1)),
                                          inc=sTE)
                    wr[w1_bids[0]] = (sTE, te_mm)
                    wr[w1_bids[1]] = (sTE, te_mm)
                    for tsl in range(n_tsl):
                        dst = hidT[:, fc * HL + tsl * tw: fc * HL + tsl * tw + tw]
                        a_g = S.add("scalar", lambda e, b=PB[bk[tsl]], dst=dst, e_=e_, fc=fc, tw=tw:
                                    e.activation(dst, b[:, 0:tw], AF.Gelu,
                                                 bias=b1_s[:, e_ * FCH + fc: e_ * FCH + fc + 1],
                                                 scale=1.0),
                                    waits=[(sTE, te_mm), (sTE, wo_last_te)], inc=sACT)
                        rd[w1_bids[tsl]] = (sACT, a_g)
                    gelu_act[fc] = a_g
                    if e_ == 0 and fc % 2 == 1 and fc // 2 < THC:
                        emit_acc_chunk(th, fc // 2)
                    if fc == 7:
                        ffn_w_load(th, e_, 2, 1)
                ffn_w_load(nth, ne, 1, 0)
                for t8 in range(THC):
                    bk = [2 * (t8 % 2), 2 * (t8 % 2) + 1]
                    w2_bids = [("pb", bk[0]), ("pb", bk[1])]
                    for fc in range(FCH):
                        hf = fc // 8
                        gd = gd_half[(th, e_, 2, hf)]
                        slot = half_slot[(th, e_, 2, hf)]
                        lw = hidT[:, fc * (T // 2) + t8 * 128: fc * (T // 2) + t8 * 128 + 128]
                        S.add("tensor", lambda e, lw=lw: e.ldweights(lw),
                              waits=[(sGD, gd), (sACT, gelu_act[fc])] +
                                    (bank_w(w2_bids[0]) + bank_w(w2_bids[1]) if fc == 0 else []))
                        for hs in range(2):
                            rhs = fsl[slot][:, (fc % 8) * 1024 + hs * 512:
                                            (fc % 8) * 1024 + hs * 512 + 512]
                            te_mm = S.add("tensor", lambda e, b=PB[bk[hs]], lw=lw, rhs=rhs, fc=fc:
                                          e.matmul(b[:, 0:512], lw, rhs,
                                                   start=(fc == 0), stop=(fc == FCH - 1)),
                                          inc=sTE)
                    wr[w2_bids[0]] = (sTE, te_mm)
                    wr[w2_bids[1]] = (sTE, te_mm)
                    tc = th * THC + t8
                    wcol = wS[:, tc * E + e_: tc * E + e_ + 1]

                    def stt_fn(e, t8=t8, wcol=wcol, bk=bk):
                        ac = acc_ch(t8)
                        e.drain()
                        e.scalar_tensor_tensor(ac[:, 0:512], PB[bk[0]][:, 0:512], wcol,
                                               ac[:, 0:512], ALU.mult, ALU.add)
                        e.drain()
                        return e.scalar_tensor_tensor(ac[:, 512:1024], PB[bk[1]][:, 0:512], wcol,
                                                      ac[:, 512:1024], ALU.mult, ALU.add)
                    d_st = S.add("vector", stt_fn,
                                 waits=[(sTE, te_mm), (sSD, acc_reload_sd[(th, t8)]),
                                        (sDVE, max(acc_b2d[(th, t8)],
                                                   w_done_dve.get(TC - 1, 0)))], inc=sDVE)
                    rd[w2_bids[0]] = (sDVE, d_st)
                    rd[w2_bids[1]] = (sDVE, d_st)
                    if e_ == E - 1:
                        final_stt[(th, t8)] = d_st
                ffn_w_load(nth, ne, 1, 1)
                ffn_w_load(nth, ne, 2, 0)
                if e_ == E - 1:
                    for i in range(THC):
                        emit_final(th, i)

        SD_ALL = sSD.n
        GD_ALL = sGD.n
        S.add("sync", lambda e: e.nop(), waits=[(sSD, SD_ALL), (sGD, GD_ALL)])

        S.emit(block)
    return nc


def _host_consts(T):
    import ml_dtypes
    BF = ml_dtypes.bfloat16
    TC = T // 128
    sl = np.asarray(_slopes(NH), dtype=np.float32)
    amap, ncol = _acol_map(T)
    karr = np.arange(BLK, dtype=np.float32) - np.float32(BLK - 1)
    bias_ref = (sl[:, None] * karr[None, :]).astype(np.float32)
    alibi = np.zeros((128, ncol), np.float32)
    for (h, qb, kb), col in amap.items():
        qw = QW[h]
        qcen = qb * qw + qw // 2
        ks = kb * 128
        alibi[:, col] = (bias_ref[h, ks:ks + 128].astype(np.float64)
                         - np.float64(bias_ref[h, qcen])).astype(np.float32)
    masks = np.zeros((128, 4 * 512), np.float32)
    p = np.arange(128, dtype=np.float32)
    j = np.arange(512, dtype=np.float32)
    for oi in range(4):
        masks[:, oi * 512:(oi + 1) * 512] = (oi * 128 + p[:, None]) <= j[None, :]
    ident = np.eye(128, dtype=np.float32)
    bcon = np.zeros((128, 128), np.float32)
    bcon[32, :] = 1.0
    bcon[64, :] = 1.0
    return {
        "alibi": alibi,
        "masks": masks.astype(BF),
        "ident": ident.astype(BF),
        "bcon": bcon.astype(BF),
    }


def kernel(L0_attn_x, L0_attn_Wq, L0_attn_Wk, L0_attn_Wv, L0_attn_Wo,
           L0_ffn_ln_g, L0_ffn_ln_b, L0_ffn_rW, L0_ffn_rb, L0_ffn_W1,
           L0_ffn_b1, L0_ffn_W2, L0_ffn_b2):
    T = 2048
    x = np.ascontiguousarray(np.asarray(L0_attn_x, dtype=np.float32))
    consts = _host_consts(T)
    lg = np.asarray(L0_ffn_ln_g, np.float32).reshape(CCH, 128).T
    lb = np.asarray(L0_ffn_ln_b, np.float32).reshape(CCH, 128).T
    lgb = np.ascontiguousarray(np.concatenate([lg, lb], axis=1))
    b1rr = np.ascontiguousarray(
        np.asarray(L0_ffn_b1, np.float32).reshape(E, FCH, 128).transpose(2, 0, 1).reshape(128, E * FCH))
    rbc = np.ascontiguousarray(np.broadcast_to(np.asarray(L0_ffn_rb, np.float32), (128, E)))
    common = {
        "Wq": np.ascontiguousarray(L0_attn_Wq, np.float32),
        "Wk": np.ascontiguousarray(L0_attn_Wk, np.float32),
        "Wv": np.ascontiguousarray(L0_attn_Wv, np.float32),
        "Wo": np.ascontiguousarray(L0_attn_Wo, np.float32),
        "W1": np.ascontiguousarray(L0_ffn_W1, np.float32),
        "W2": np.ascontiguousarray(L0_ffn_W2, np.float32),
        "b2": np.ascontiguousarray(L0_ffn_b2, np.float32),
        "rW": np.ascontiguousarray(L0_ffn_rW, np.float32),
        "lgb": lgb, "b1r": b1rr, "rbc": rbc,
        "alibi": consts["alibi"], "masks": consts["masks"],
        "ident": consts["ident"], "bcon": consts["bcon"],
    }
    nc = build_core_program(T)
    in_maps = [dict(common, x=np.ascontiguousarray(x[c])) for c in range(8)]
    res = run_bass_kernel_spmd(nc, in_maps, core_ids=list(range(8)))
    return np.stack([np.asarray(res.results[c]["y"]) for c in range(8)], axis=0)


# version 20
# speedup vs baseline: 1.0362x; 1.0362x over previous
import math
import contextlib
import numpy as np
import concourse.bass as bass
import concourse.mybir as mybir
from concourse.bass_utils import run_bass_kernel_spmd

F32 = mybir.dt.float32
BF16 = mybir.dt.bfloat16
U8 = mybir.dt.uint8
AF = mybir.ActivationFunctionType
ALU = mybir.AluOpType
AX = mybir.AxisListType.X

C = 1024
NH = 16
HD = 64
E = 8
HF = 2048
EPS = 1e-5
BLK = 2048
CCH = C // 128
FCH = HF // 128
RW = 160

QW = [128, 128, 256, 256] + [512] * 12


def _acol_map(T):
    m = {}
    col = 0
    for h in range(NH):
        qw = QW[h]
        for qb in range(T // qw):
            qe = qb * qw + qw - 1
            for kb in range(qe // 128 + 1):
                m[(h, qb, kb)] = col
                col += 1
    return m, col

AR_A = 0
AR_B = 32768
AR_C = 65536
AR_D = 98304
AR_E = 131072
AR_F = 147456
AR_TOTAL = 163840


def _slopes(n):
    def pow2(m):
        start = 2 ** (-(2 ** (-(math.log2(m) - 3))))
        return [start * start ** i for i in range(m)]
    if math.log2(n).is_integer():
        return pow2(n)
    c = 2 ** math.floor(math.log2(n))
    return pow2(c) + _slopes(2 * c)[0::2][:n - c]


class Sem:
    def __init__(self, nc, es, name):
        self.h = es.enter_context(nc.semaphore(name))
        self.n = 0


class Sched:
    ENGS = ("tensor", "scalar", "vector", "gpsimd", "sync")

    def __init__(self):
        self.ops = {e: [] for e in self.ENGS}
        self.last_wait = {e: {} for e in self.ENGS}

    def add(self, eng, fn, waits=(), inc=None, amt=1):
        if inc is not None and eng in ("gpsimd", "sync") and inc.n > 0:
            waits = list(waits) + [(inc, inc.n)]
        wl = []
        lw = self.last_wait[eng]
        for (s, v) in waits:
            if v <= 0 or lw.get(id(s), 0) >= v:
                continue
            lw[id(s)] = v
            wl.append((s, v))
        self.ops[eng].append((wl, fn, inc, amt))
        if inc is not None:
            inc.n += amt
            return inc.n
        return None

    def emit(self, block):
        for name in self.ENGS:
            oplist = self.ops[name]
            if not oplist:
                continue
            dec = getattr(block, name)

            def section(eng, _ops=oplist):
                for (wl, fn, inc, amt) in _ops:
                    for (s, v) in wl:
                        eng.wait_ge(s.h, v)
                    ins = fn(eng)
                    if inc is not None:
                        ins.then_inc(inc.h, amt)
            dec(section)


def build_core_program(T=2048):
    assert T % 512 == 0
    TC = T // 128
    QC = T // 512
    THC = TC // 2
    NPAIR_D = min(102, TC * 8)
    nc = bass.Bass("TRN2", target_bir_lowering=False)

    x_in = nc.dram_tensor("x", [T, C], F32, kind="ExternalInput").ap()
    Wq = nc.dram_tensor("Wq", [C, C], F32, kind="ExternalInput").ap()
    Wk = nc.dram_tensor("Wk", [C, C], F32, kind="ExternalInput").ap()
    Wv = nc.dram_tensor("Wv", [C, C], F32, kind="ExternalInput").ap()
    Wo = nc.dram_tensor("Wo", [C, C], F32, kind="ExternalInput").ap()
    W1 = nc.dram_tensor("W1", [E, C, HF], F32, kind="ExternalInput").ap()
    W2 = nc.dram_tensor("W2", [E, HF, C], F32, kind="ExternalInput").ap()
    b2v = nc.dram_tensor("b2", [E, C], F32, kind="ExternalInput").ap()
    rWv = nc.dram_tensor("rW", [C, E], F32, kind="ExternalInput").ap()
    lgb = nc.dram_tensor("lgb", [128, 2 * CCH], F32, kind="ExternalInput").ap()
    b1r = nc.dram_tensor("b1r", [128, E * FCH], F32, kind="ExternalInput").ap()
    rbc = nc.dram_tensor("rbc", [128, E], F32, kind="ExternalInput").ap()
    amap, acols = _acol_map(T)
    alibi_in = nc.dram_tensor("alibi", [128, acols], F32, kind="ExternalInput").ap()
    masks_in = nc.dram_tensor("masks", [128, 4 * 512], BF16, kind="ExternalInput").ap()
    ident_in = nc.dram_tensor("ident", [128, 128], BF16, kind="ExternalInput").ap()
    bcon_in = nc.dram_tensor("bcon", [128, 128], BF16, kind="ExternalInput").ap()
    y_out = nc.dram_tensor("y", [T, C], F32, kind="ExternalOutput").ap()
    xuscr = nc.dram_tensor("xuscr", [T, 2 * C], F32, kind="Internal").ap()

    es = contextlib.ExitStack()
    with es:
        arena = es.enter_context(nc.sbuf_tensor("arena", [128, AR_TOTAL], U8))
        ar_base = nc.lookup_mloc(arena).addr

        def vt(name, shape, dtype, byte_off):
            return nc.alloc_sbuf_tensor_at(name, shape, dtype, offset=ar_base + byte_off)

        Trz1 = vt("Trz1", [128, CCH * T], BF16, AR_A)
        TrAttn = vt("TrAttn", [128, CCH * T], BF16, AR_A)
        hidT = vt("hidT", [128, FCH * (T // 2)], BF16, AR_A)
        qT = vt("qT", [128, CCH * T], BF16, AR_B)
        fsl0 = vt("fsl0", [128, 8192], BF16, AR_B + 16384)
        fsl1 = vt("fsl1", [128, 8192], BF16, AR_B)
        kT = vt("kT", [128, CCH * T], BF16, AR_C)
        fsl2 = vt("fsl2", [128, 8192], BF16, AR_C)
        x0_t = vt("x0t", [128, 3072], F32, AR_C + 16384)
        u_t = vt("ut", [128, 4096], F32, AR_B)
        x1_t = vt("x1t", [128, 3072], F32, AR_F)
        accA = vt("accA", [128, 4096], F32, AR_E)
        accB = vt("accB", [128, 4096], F32, AR_C + 16384)
        vgD = vt("vgD", [128, NPAIR_D * RW], BF16, AR_D)
        Trz = vt("Trz", [128, CCH * T], BF16, AR_D)
        wsl = [vt("wsl0", [128, 4096], BF16, AR_E),
               vt("wsl1", [128, 4096], BF16, AR_E + 8192)]
        xpp = vt("xpp", [128, 4096], F32, AR_F)
        z1pp = vt("z1pp", [128, 3072], BF16, AR_D)
        slabs = vt("slabs", [128, 2048], BF16, AR_F)
        vgF = vt("vgF", [128, max(0, TC * 8 - NPAIR_D) * RW + 16], BF16, AR_F + 4096)
        zp_t = vt("zp", [128, 2048], BF16, AR_F + 12288)
        x1r_t = vt("x1r", [128, 2048], F32, AR_F)
        outb_t = vt("outb", [128, 2048], F32, AR_F + 8192)

        lgb_s = es.enter_context(nc.sbuf_tensor("lgb_s", [128, 2 * CCH], F32))
        b1_s = es.enter_context(nc.sbuf_tensor("b1_s", [128, E * FCH], F32))
        rbc_s = es.enter_context(nc.sbuf_tensor("rbc_s", [128, E], F32))
        alibi_s = es.enter_context(nc.sbuf_tensor("alibi_s", [128, acols], F32))
        masks_s = es.enter_context(nc.sbuf_tensor("masks_s", [128, 4 * 512], BF16))
        ident_s = es.enter_context(nc.sbuf_tensor("ident_s", [128, 128], BF16))
        bcon_s = es.enter_context(nc.sbuf_tensor("bcon_s", [128, 128], BF16))
        rW_s = es.enter_context(nc.sbuf_tensor("rW_s", [128, CCH * E], BF16))
        b2_s = es.enter_context(nc.sbuf_tensor("b2_s", [8, C], BF16))
        wT_s = es.enter_context(nc.sbuf_tensor("wT_s", [8, T], BF16))
        recb_s = es.enter_context(nc.sbuf_tensor("recb_s", [128, 2048], BF16))
        stg_s = es.enter_context(nc.sbuf_tensor("stg_s", [128, 2048], BF16))
        recf_s = es.enter_context(nc.sbuf_tensor("recf_s", [128, 512], F32))
        wS = es.enter_context(nc.sbuf_tensor("wS", [128, TC * E], F32))
        stP = es.enter_context(nc.sbuf_tensor("stP", [128, 128], F32))
        st3 = es.enter_context(nc.sbuf_tensor("st3", [128, 128], F32))
        lgS = es.enter_context(nc.sbuf_tensor("lgS", [128, 128], F32))
        stF = es.enter_context(nc.sbuf_tensor("stF", [128, 64], F32))
        epsv = es.enter_context(nc.sbuf_tensor("epsv", [128, 1], F32))
        wbf_t = es.enter_context(nc.sbuf_tensor("wbf_t", [128, 8 * 16], BF16))
        wtscr = es.enter_context(nc.sbuf_tensor("wtscr", [128, 128], BF16))

        PB = [es.enter_context(nc.psum_tensor("pb%d" % i, [128, 512], F32)) for i in range(6)]
        PT = [es.enter_context(nc.psum_tensor("pt%d" % i, [128, 1024], BF16)) for i in range(2)]

        sTE = Sem(nc, es, "sTE")
        sACT = Sem(nc, es, "sACT")
        sDVE = Sem(nc, es, "sDVE")
        sGD = Sem(nc, es, "sGD")
        sSD = Sem(nc, es, "sSD")

        S = Sched()
        block = es.enter_context(nc.Block())

        wr = {}
        rd = {}

        def bank_w(bid):
            return [rd[bid]] if bid in rd else []

        def bank_r(bid):
            return [wr[bid]] if bid in wr else []

        def vgp(pair):
            if pair < NPAIR_D:
                return vgD[:, pair * RW:(pair + 1) * RW]
            p = pair - NPAIR_D
            return vgF[:, p * RW:(p + 1) * RW]

        S.add("sync", lambda e: e.dma_start(
            out=xpp[:, 0:2048].rearrange("p (two q) -> p two q", two=2),
            in_=x_in[0:256, :].rearrange("(two p) c -> p two c", p=128)), inc=sSD, amt=16)
        SD_X0 = sSD.n
        for dst, src in ((ident_s, ident_in), (bcon_s, bcon_in), (lgb_s, lgb), (b1_s, b1r),
                         (rbc_s, rbc), (alibi_s, alibi_in), (masks_s, masks_in)):
            S.add("sync", lambda e, d=dst, s=src: e.dma_start(out=d[:, :], in_=s[:, :]),
                  inc=sSD, amt=16)
        SD_CONST = sSD.n
        S.add("gpsimd", lambda e: e.dma_start(
            out=rW_s[:, :].rearrange("p (cc e) -> p cc e", cc=CCH),
            in_=rWv.rearrange("(cc p) e -> p cc e", p=128)), inc=sGD, amt=16)
        S.add("gpsimd", lambda e: e.dma_start(out=b2_s[:, :], in_=b2v[:, :]), inc=sGD, amt=16)
        GD_SMALL = sGD.n
        S.add("vector", lambda e: e.memset(epsv[:, :], EPS), inc=sDVE)
        S.add("vector", lambda e: e.memset(recf_s[:, :], 0.0), inc=sDVE)
        S.add("vector", lambda e: e.memset(recb_s[:, :], 0.0), inc=sDVE)
        DVE_INIT = sDVE.n

        def w_half_load(Wm, hf):
            def f(e, Wm=Wm, hf=hf, slot=wsl[hf % 2]):
                return e.dma_start(
                    out=slot[:, :].rearrange("p (cc d) -> p cc d", cc=CCH),
                    in_=Wm[:, hf * 512:(hf + 1) * 512].rearrange("(cc p) d -> p cc d", p=128))
            return f

        GD_W = {}
        S.add("gpsimd", w_half_load(Wq, 0), inc=sGD, amt=16); GD_W[(0, 0)] = sGD.n
        S.add("gpsimd", w_half_load(Wq, 1), inc=sGD, amt=16); GD_W[(0, 1)] = sGD.n

        p0_norm_act = {}
        p0_tr_te = {}
        p0_sd = {}
        p0_pend = []
        for tc in range(TC):
            xb = xpp[:, (tc % 4) * 1024:(tc % 4) * 1024 + 1024]
            z1 = z1pp[:, (tc % 3) * 1024:(tc % 3) * 1024 + 1024]
            if tc == 0:
                p0_sd[0] = SD_X0
            elif tc % 2 == 0:
                s4 = (tc % 4) * 1024
                p0_sd[tc] = S.add("sync", lambda e, tc=tc, s4=s4: e.dma_start(
                    out=xpp[:, s4:s4 + 2048].rearrange("p (two q) -> p two q", two=2),
                    in_=x_in[tc * 128:(tc + 2) * 128, :].rearrange("(two p) c -> p two c", p=128)),
                    waits=[(sDVE, p0_norm_act.get(tc - 3, 0))], inc=sSD, amt=16)
            sd_x = p0_sd[tc - tc % 2]

            sp = stP[:, (tc % 2) * 64:(tc % 2) * 64 + 64]

            def stats0(e, xb=xb, sp=sp):
                e.drain()
                e.bn_stats(sp[:, 0:6], xb[:, 0:512])
                e.bn_stats(sp[:, 6:12], xb[:, 512:1024])
                e.drain()
                return e.bn_aggr(sp[:, 12:14], sp[:, 0:12])
            dve_st = S.add("vector", stats0,
                           waits=[(sSD, sd_x)], inc=sDVE)

            def rsq0(e, sp=sp):
                e.activation(sp[:, 14:15], sp[:, 13:14], AF.Ln, bias=epsv[:, 0:1], scale=1.0)
                e.drain()
                return e.activation(sp[:, 15:16], sp[:, 14:15], AF.Exp, bias=0.0, scale=-0.5)
            act_rq = S.add("scalar", rsq0, waits=[(sDVE, dve_st)], inc=sACT)

            def nrm0(e, sp=sp, z1=z1, xb=xb):
                e.drain()
                return e.tensor_scalar(z1, xb, sp[:, 12:13], sp[:, 15:16],
                                       ALU.subtract, ALU.mult)
            act_nm = S.add("vector", nrm0,
                           waits=[(sACT, act_rq), (sTE, p0_tr_te.get(tc - 3, 0))], inc=sDVE)
            p0_norm_act[tc] = act_nm

            def p0_tr(tc=tc, z1=z1):
                bid = ("pt", tc % 2)
                pt = PT[tc % 2]
                for cc in range(CCH):
                    te_tr = S.add("tensor", lambda e, z1=z1, cc=cc, pt=pt: e.transpose(
                        pt[:, cc * 128:(cc + 1) * 128], z1[:, cc * 128:(cc + 1) * 128],
                        ident_s[:, :]),
                        waits=[(sDVE, p0_norm_act[tc]), (sSD, SD_CONST)] +
                              (bank_w(bid) if cc == 0 else []), inc=sTE)
                wr[bid] = (sTE, sTE.n)
                dst = Trz1[:, 0:CCH * T].rearrange(
                    "p (cc t) -> p cc t", cc=CCH)[:, :, tc * 128:(tc + 1) * 128]
                d_ev = S.add("vector", lambda e, pt=pt, dst=dst: (
                    e.drain(), e.tensor_copy(dst, pt[:, 0:1024].rearrange(
                        "p (cc q) -> p cc q", cc=CCH)))[-1],
                    waits=[(sTE, sTE.n)], inc=sDVE)
                rd[bid] = (sDVE, d_ev)
                p0_tr_te[tc] = sTE.n
            p0_pend.append(p0_tr)
            if len(p0_pend) > 2:
                p0_pend.pop(0)()
        while p0_pend:
            p0_pend.pop(0)()
        P0_TEN = sTE.n
        P0_ACT = sACT.n
        P0_DVE = sDVE.n

        half_te = {}

        def qk_proj(wi, DST):
            for hf in range(2):
                gd = GD_W[(wi, hf)]
                if hf == 1:
                    half_te[(wi, 0)] = sTE.n
                for dc in range(4):
                    for tp in range(max(1, QC // 2)):
                        nb = min(2, QC)
                        bids = [("pb", 2 * (tp % 2) + i) for i in range(nb)]
                        for cc in range(CCH):
                            lw = wsl[hf % 2][:, cc * 512 + dc * 128: cc * 512 + dc * 128 + 128]
                            S.add("tensor", lambda e, lw=lw: e.ldweights(lw),
                                  waits=[(sGD, gd), (sACT, P0_ACT), (sDVE, P0_DVE)])
                            for half in range(nb):
                                ts4 = tp * 2 + half
                                rhs = Trz1[:, cc * T + ts4 * 512: cc * T + ts4 * 512 + 512]
                                te_mm = S.add("tensor", lambda e, b=PB[2 * (tp % 2) + half],
                                              lw=lw, rhs=rhs, cc=cc:
                                              e.matmul(b[:, 0:512], lw, rhs,
                                                       start=(cc == 0), stop=(cc == CCH - 1)),
                                              waits=bank_w(bids[half]) if cc == 0 else [], inc=sTE)
                                if cc == CCH - 1:
                                    wr[bids[half]] = (sTE, te_mm)
                        for half in range(nb):
                            ts4 = tp * 2 + half
                            d = DST[:, (hf * 4 + dc) * T + ts4 * 512:
                                    (hf * 4 + dc) * T + ts4 * 512 + 512]
                            d_ev = S.add("vector", lambda e, b=PB[2 * (tp % 2) + half], d=d: (
                                e.drain(), e.tensor_copy(d, b[:, 0:512]))[-1],
                                waits=[wr[bids[half]]], inc=sDVE)
                            rd[bids[half]] = (sDVE, d_ev)
        qk_proj(0, qT)
        te_q_done = sTE.n
        S.add("gpsimd", w_half_load(Wk, 0), waits=[(sTE, half_te[(0, 0)])], inc=sGD, amt=16)
        GD_W[(1, 0)] = sGD.n
        S.add("gpsimd", w_half_load(Wk, 1), waits=[(sTE, te_q_done)], inc=sGD, amt=16)
        GD_W[(1, 1)] = sGD.n
        qk_proj(1, kT)
        te_k_done = sTE.n
        P1_ACT_QK = sACT.n
        S.add("gpsimd", w_half_load(Wv, 0), waits=[(sTE, half_te[(1, 0)])], inc=sGD, amt=16)
        GD_W[(2, 0)] = sGD.n
        S.add("gpsimd", w_half_load(Wv, 1), waits=[(sTE, te_k_done)], inc=sGD, amt=16)
        GD_W[(2, 1)] = sGD.n

        def vg_ones(e):
            e.drain()
            e.memset(vgD[:, :], 0.0)
            if TC * 8 > NPAIR_D:
                e.memset(vgF[:, :], 0.0)
            e.drain()
            for pair in range(TC * 8):
                e.memset(vgp(pair)[:, 64:65], 1.0)
            e.drain()
            return e.engine_nop()
        dve_ones = S.add("vector", vg_ones, waits=[(sTE, P0_TEN)], inc=sDVE)

        for tc in range(TC):
            for ds in range(2):
                gd = GD_W[(2, ds)]
                bid = ("pb", 4 + ds)
                for cc in range(CCH):
                    lw = Trz1[:, cc * T + tc * 128: cc * T + tc * 128 + 128]
                    S.add("tensor", lambda e, lw=lw: e.ldweights(lw), waits=[(sGD, gd)])
                    rhs = wsl[ds % 2][:, cc * 512: cc * 512 + 512]
                    te_mm = S.add("tensor", lambda e, b=PB[4 + ds], lw=lw, rhs=rhs, cc=cc:
                                  e.matmul(b[:, 0:512], lw, rhs,
                                           start=(cc == 0), stop=(cc == CCH - 1)),
                                  waits=(bank_w(bid) + [(sDVE, dve_ones)]) if cc == 0 else [],
                                  inc=sTE)
                wr[bid] = (sTE, te_mm)

                def vevac(e, b=PB[4 + ds], tc=tc, ds=ds):
                    from concourse.ap import AP
                    e.drain()
                    pair0 = tc * 8 + ds * 4
                    if pair0 + 4 <= NPAIR_D:
                        base = vgD.ap()
                        part = list(base.ap[0])
                        dst = AP(tensor=base.tensor, offset=base.offset + pair0 * RW,
                                 ap=[part, [RW, 4], [96, 2], [1, 64]])
                        e.tensor_copy(dst, b[:, 0:512].rearrange(
                            "p (pr two d) -> p pr two d", pr=4, two=2))
                    else:
                        for j in range(8):
                            h = ds * 8 + j
                            pr = tc * 8 + h // 2
                            off = 0 if h % 2 == 0 else 96
                            e.tensor_copy(vgp(pr)[:, off:off + 64], b[:, j * 64:(j + 1) * 64])
                    e.drain()
                    return e.engine_nop()
                d_ev = S.add("vector", vevac, waits=[wr[bid]], inc=sDVE)
                rd[bid] = (sDVE, d_ev)
        P1_TE = sTE.n
        P1_DVE = sDVE.n
        S.add("gpsimd", w_half_load(Wo, 0), waits=[(sTE, P1_TE)], inc=sGD, amt=16)
        GD_WO0 = sGD.n
        S.add("gpsimd", w_half_load(Wo, 1), waits=[(sTE, P1_TE)], inc=sGD, amt=16)
        GD_WO1 = sGD.n

        slab_free = [0, 0, 0, 0]
        stg_free = [0, 0, 0, 0]
        gi = 0
        pend_evac = [None, None]
        for pi in range(NH // 2):
            qw = QW[2 * pi]
            dc = pi
            for qb in range(T // qw):
                qs = qb * qw
                qe = qs + qw - 1
                K = qe // 128 + 1
                gpar = gi % 2
                pv_bid = {0: ("pb", 2 + 2 * gpar), 1: ("pb", 3 + 2 * gpar)}
                pvb_i = {0: 2 + 2 * gpar, 1: 3 + 2 * gpar}
                exp_done = {}
                mask_done = {}
                if pend_evac[0] is not None:
                    for f in pend_evac[0][0]:
                        f(d_rc=pend_evac[0][1][f])
                    pend_evac[0] = None
                if pend_evac[1] is not None:
                    rfs, efs = pend_evac[1]
                    drcs = {}
                    for rf, ef in zip(rfs, efs):
                        drcs[ef] = rf()
                    pend_evac[0] = (efs, drcs)
                    pend_evac[1] = None

                def emit_score(kb, hp):
                    h = 2 * pi + hp
                    sbi = hp
                    s_bid = ("pb", sbi)
                    r0 = hp * 64
                    tile = None if hp == 0 else (64, 0)
                    lwk = kT[r0:r0 + 64, dc * T + kb * 128: dc * T + kb * 128 + 128]
                    qrhs = qT[r0:r0 + 64, dc * T + qs: dc * T + qs + qw]
                    S.add("tensor", lambda e, lwk=lwk, tile=tile: e.ldweights(lwk, tile_position=tile),
                          waits=[(sACT, P1_ACT_QK)] + bank_w(s_bid))
                    te_s = S.add("tensor", lambda e, sb=PB[sbi], lwk=lwk, qrhs=qrhs, tile=tile,
                                 qw=qw: e.matmul(sb[:, 0:qw], lwk, qrhs, start=True, stop=True,
                                                 tile_position=tile), inc=sTE)
                    wr[s_bid] = (sTE, te_s)
                    slot = (kb % 2) * 2 + hp
                    slab = slabs[:, slot * 512: slot * 512 + qw]
                    ac = amap[(h, qb, kb)]
                    bias = alibi_s[:, ac:ac + 1]
                    a_ex = S.add("scalar", lambda e, slab=slab, sb=PB[sbi], bias=bias, qw=qw:
                                 e.activation(slab, sb[:, 0:qw], AF.Exp, bias=bias, scale=0.125),
                                 waits=[(sTE, te_s), (sTE, slab_free[slot]), (sSD, SD_CONST)],
                                 inc=sACT)
                    rd[s_bid] = (sACT, a_ex)
                    exp_done[(kb, hp)] = a_ex
                    if kb * 128 + 127 > qs:
                        oi = (kb * 128 - qs) // 128
                        d_mk = S.add("vector", lambda e, slab=slab, oi=oi, qw=qw: (
                            e.drain(), e.tensor_mul(slab, slab,
                                                    masks_s[:, oi * 512:oi * 512 + qw]))[-1],
                            waits=[(sACT, a_ex), (sSD, SD_CONST)], inc=sDVE)
                        mask_done[(kb, hp)] = d_mk

                def emit_pv(kb, hp):
                    slot = (kb % 2) * 2 + hp
                    slab = slabs[:, slot * 512: slot * 512 + qw]
                    pv_w = ([(sDVE, mask_done[(kb, hp)])] if (kb, hp) in mask_done
                            else [(sACT, exp_done[(kb, hp)])])
                    pr = kb * 8 + pi
                    stat = vgp(pr)[:, 0:65] if hp == 0 else vgp(pr)[:, 32:160]
                    if kb == 0:
                        pv_w += bank_w(pv_bid[hp]) + [(sDVE, P1_DVE)]
                    mtop = 65 if hp == 0 else 128
                    S.add("tensor", lambda e, stat=stat: e.ldweights(stat), waits=pv_w)
                    te_pv = S.add("tensor", lambda e, pvb=PB[pvb_i[hp]], stat=stat, slab=slab,
                                  kb=kb, K=K, mtop=mtop, qw=qw:
                                  e.matmul(pvb[0:mtop, 0:qw], stat, slab,
                                           start=(kb == 0), stop=(kb == K - 1)),
                                  inc=sTE)
                    slab_free[slot] = te_pv
                    return te_pv

                te_pv_d = {}
                for kb in range(K):
                    emit_score(kb, 0)
                    emit_score(kb, 1)
                    if kb >= 1:
                        te_pv_d[0] = emit_pv(kb - 1, 0)
                        te_pv_d[1] = emit_pv(kb - 1, 1)
                te_pv_d[0] = emit_pv(K - 1, 0)
                te_pv_d[1] = emit_pv(K - 1, 1)
                wr[pv_bid[0]] = (sTE, te_pv_d[0])
                wr[pv_bid[1]] = (sTE, te_pv_d[1])

                rfs = []
                efs = []
                for hp in range(2):
                    h = 2 * pi + hp
                    drow = 64 if hp == 0 else 32
                    rslot = gpar * 2 + hp

                    def emit_recip(pvbi=pvb_i[hp], drow=drow, rslot=rslot, qw=qw):
                        def recip_fn(e, pvb=PB[pvbi], drow=drow, rslot=rslot, qw=qw):
                            e.drain()
                            e.reciprocal(recf_s[drow:drow + 1, 0:qw], pvb[drow:drow + 1, 0:qw])
                            e.drain()
                            return e.tensor_copy(
                                recb_s[drow:drow + 1, rslot * 512: rslot * 512 + qw],
                                recf_s[drow:drow + 1, 0:qw])
                        return S.add("vector", recip_fn,
                                     waits=[(sTE, te_pv_d[hp]), (sDVE, DVE_INIT)], inc=sDVE)

                    def emit_evac(pvbi=pvb_i[hp], pvbid=pv_bid[hp], te_pv=te_pv_d[hp],
                                  hp=hp, drow=drow, rslot=rslot, qw=qw, dc=dc, qs=qs,
                                  d_rc=0):
                        r0 = hp * 64
                        tileb = (64, 0) if drow == 64 else (32, 0)
                        stg = stg_s[:, rslot * 512: rslot * 512 + qw]
                        a_st = S.add("scalar", lambda e, stg=stg, pvb=PB[pvbi], r0=r0, qw=qw:
                                     e.copy(stg[r0:r0 + 64, :], pvb[r0:r0 + 64, 0:qw]),
                                     waits=[(sTE, te_pv), (sDVE, stg_free[rslot])], inc=sACT)
                        S.add("tensor", lambda e, tileb=tileb, drow=drow: e.ldweights(
                            bcon_s[drow:drow + 32, :], tile_position=tileb),
                            waits=[(sDVE, d_rc), (sSD, SD_CONST), (sACT, a_st)])
                        te_bc = S.add("tensor", lambda e, bcb=PB[pvbi], tileb=tileb, drow=drow,
                                      rslot=rslot, qw=qw:
                                      e.matmul(bcb[:, 0:qw], bcon_s[drow:drow + 32, :],
                                               recb_s[drow:drow + 32,
                                                      rslot * 512: rslot * 512 + qw],
                                               start=True, stop=True, tile_position=tileb),
                                      inc=sTE)
                        dsta = TrAttn[r0:r0 + 64, dc * T + qs: dc * T + qs + qw]
                        d_ml = S.add("vector", lambda e, dsta=dsta, stg=stg, bcb=PB[pvbi],
                                     r0=r0, qw=qw: (
                            e.drain(), e.tensor_mul(dsta, stg[r0:r0 + 64, :],
                                                    bcb[r0:r0 + 64, 0:qw]))[-1],
                            waits=[(sTE, te_bc), (sACT, a_st), (sTE, P1_TE)], inc=sDVE)
                        rd[pvbid] = (sDVE, d_ml)
                        stg_free[rslot] = d_ml
                    rfs.append(emit_recip)
                    efs.append(emit_evac)
                pend_evac[1] = (rfs, efs)
                gi += 1
        if pend_evac[1] is not None:
            rfs, efs = pend_evac[1]
            drcs = {}
            for rf, ef in zip(rfs, efs):
                drcs[ef] = rf()
            if pend_evac[0] is not None:
                for f in pend_evac[0][0]:
                    f(d_rc=pend_evac[0][1][f])
            pend_evac[0] = (efs, drcs)
            pend_evac[1] = None
        if pend_evac[0] is not None:
            for f in pend_evac[0][0]:
                f(d_rc=pend_evac[0][1][f])
        P2_TE = sTE.n
        P2_DVE = sDVE.n
        P2_ACT = sACT.n

        def acc_ch(i):
            if i < 4:
                return accA[:, i * 1024:(i + 1) * 1024]
            return accB[:, (i - 4) * 1024:(i - 4) * 1024 + 1024]

        fsl = [fsl0, fsl1, fsl2]
        gd_half = {}
        half_slot = {}
        _gctr = [0]

        def ffn_w_load(th, e_, kind, hf, te_guard=None, extra=(), slot=None):
            if th >= 2:
                return
            if slot is None:
                slot = _gctr[0] % 3
            _gctr[0] += 1
            half_slot[(th, e_, kind, hf)] = slot

            def f(e, e_=e_, kind=kind, hf=hf, slot=slot):
                if kind == 1:
                    src = W1[e_, :, hf * 1024:(hf + 1) * 1024].rearrange(
                        "(cc p) f -> p cc f", p=128)
                else:
                    src = W2[e_, hf * 1024:(hf + 1) * 1024, :].rearrange(
                        "(fc p) h -> p fc h", p=128)
                return e.dma_start(out=fsl[slot][:, :].rearrange("p (a b) -> p a b", a=8), in_=src)
            S.add("gpsimd", f, waits=[(sTE, sTE.n if te_guard is None else te_guard)] + list(extra),
                  inc=sGD, amt=16)
            gd_half[(th, e_, kind, hf)] = sGD.n


        ffn_w_load(0, 0, 1, 0, te_guard=P2_TE, slot=0)
        ffn_w_load(0, 0, 2, 0, te_guard=P2_TE, slot=2)
        wo_last_te = 0
        trz_act = {}
        xu_spill_sd = {}
        zp_te = {}
        w_done_dve = {}
        zp_act = {}

        def stats_fn(buf, so, stt):
            def f(e, buf=buf, so=so, stt=stt):
                e.drain()
                e.bn_stats(stt[:, so:so + 6], buf[:, 0:512])
                e.bn_stats(stt[:, so + 6:so + 12], buf[:, 512:1024])
                e.drain()
                return e.bn_aggr(stt[:, so + 12:so + 14], stt[:, so:so + 12])
            return f

        def rsq_fn(so, stt, out=None, src=None, short=False):
            def f(e, so=so, stt=stt, out=out, src=src, short=short):
                e.activation(stt[:, so + 14:so + 15], stt[:, so + 13:so + 14], AF.Ln,
                             bias=epsv[:, 0:1], scale=1.0)
                e.drain()
                r = e.activation(stt[:, so + 15:so + 16], stt[:, so + 14:so + 15], AF.Exp,
                                 bias=0.0, scale=-0.5)
                if short:
                    return r
                e.drain()
                e.activation(stt[:, so + 16:so + 17], stt[:, so + 12:so + 13], AF.Copy,
                             bias=0.0, scale=-1.0)
                e.drain()
                r = e.activation(stt[:, so + 17:so + 18], stt[:, so + 16:so + 17], AF.Copy,
                                 bias=0.0, scale=stt[:, so + 15:so + 16])
                if out is not None:
                    e.drain()
                    r = e.activation(out, src, AF.Identity, bias=stt[:, so + 17:so + 18],
                                     scale=stt[:, so + 15:so + 16])
                return r
            return f

        x0_sd = {}
        d_x1_d = {}
        x1sp_sd = {}
        usp_sd = {}
        a_ev_d = {}
        r1_d = {}
        r2u_d = {}

        def p3_A1(tc):
            sl3 = (tc % 3) * 1024
            x0_sd[tc] = S.add("sync", lambda e, tc=tc, sl3=sl3: e.dma_start(
                out=x0_t[:, sl3:sl3 + 1024], in_=x_in[tc * 128:(tc + 1) * 128, :]),
                waits=[(sTE, P2_TE), (sDVE, d_x1_d.get(tc - 3, 0))], inc=sSD, amt=16)
            bids = [("pb", 0), ("pb", 1)]
            for cc in range(CCH):
                lw = TrAttn[:, cc * T + tc * 128: cc * T + tc * 128 + 128]
                S.add("tensor", lambda e, lw=lw: e.ldweights(lw),
                      waits=[(sDVE, P2_DVE), (sGD, GD_WO1)] +
                            (bank_w(bids[0]) + bank_w(bids[1]) if cc == 0 else []))
                for cs in range(2):
                    rhs = wsl[cs][:, cc * 512: cc * 512 + 512]
                    te_mm = S.add("tensor", lambda e, b=PB[cs], lw=lw, rhs=rhs, cc=cc:
                                  e.matmul(b[:, 0:512], lw, rhs,
                                           start=(cc == 0), stop=(cc == CCH - 1)), inc=sTE)
            wr[bids[0]] = (sTE, te_mm)
            wr[bids[1]] = (sTE, te_mm)
            x1b = x1_t[:, sl3:sl3 + 1024]

            def wo_ev(e, x1b=x1b):
                e.drain()
                e.copy(x1b[:, 0:512], PB[0][:, 0:512])
                e.drain()
                return e.copy(x1b[:, 512:1024], PB[1][:, 0:512])
            a_ev = S.add("scalar", wo_ev,
                         waits=[(sTE, te_mm), (sSD, x1sp_sd.get(tc - 3, 0))], inc=sACT)
            a_ev_d[tc] = a_ev
            rd[bids[0]] = (sACT, a_ev)
            rd[bids[1]] = (sACT, a_ev)
            return te_mm

        def p3_A2(tc):
            par = tc % 2
            sl3 = (tc % 3) * 1024
            x1b = x1_t[:, sl3:sl3 + 1024]
            st = st3[:, par * 64: par * 64 + 64]
            d_s1 = S.add("vector", stats_fn(x1b, 0, st), waits=[(sACT, a_ev_d[tc])], inc=sDVE)
            r1_d[tc] = S.add("scalar", rsq_fn(0, st, short=True), waits=[(sDVE, d_s1)], inc=sACT)

        def p3_A3(tc):
            par = tc % 2
            sl3 = (tc % 3) * 1024
            x1b = x1_t[:, sl3:sl3 + 1024]
            x0v = x0_t[:, sl3:sl3 + 1024]
            ub = u_t[:, (tc % 4) * 1024:(tc % 4) * 1024 + 1024]
            st = st3[:, par * 64: par * 64 + 64]

            def x1_fn(e, x1b=x1b, x0v=x0v, st=st):
                e.drain()
                e.tensor_scalar(x1b, x1b, st[:, 12:13], st[:, 15:16], ALU.subtract, ALU.mult)
                e.drain()
                return e.tensor_add(x1b, x1b, x0v)
            d_x1 = S.add("vector", x1_fn, waits=[(sACT, r1_d[tc]), (sSD, x0_sd[tc])], inc=sDVE)
            d_x1_d[tc] = d_x1
            x1sp_sd[tc] = S.add("sync", lambda e, x1b=x1b, tc=tc: e.dma_start(
                out=xuscr[tc * 128:(tc + 1) * 128, 1024:2048], in_=x1b),
                waits=[(sDVE, d_x1)], inc=sSD, amt=16)
            d_s2 = S.add("vector", stats_fn(x1b, 18, st), inc=sDVE)
            r2u_d[tc] = S.add("scalar", rsq_fn(18, st, out=ub, src=x1b),
                              waits=[(sDVE, d_s2), (sSD, usp_sd.get(tc - 3, 0))], inc=sACT)

        def p3_A4(tc):
            par = tc % 2
            ub = u_t[:, (tc % 4) * 1024:(tc % 4) * 1024 + 1024]
            st = st3[:, par * 64: par * 64 + 64]
            d_s3 = S.add("vector", stats_fn(ub, 36, st), waits=[(sACT, r2u_d[tc])], inc=sDVE)
            zp = zp_t[:, par * 1024: par * 1024 + 1024]
            a_r3 = S.add("scalar", rsq_fn(36, st, out=zp, src=ub),
                         waits=[(sDVE, d_s3), (sTE, zp_te.get(tc - 2, 0))], inc=sACT)
            zp_act[tc] = a_r3
            usp_sd[tc] = S.add("sync", lambda e, tc=tc, ub=ub: e.dma_start(
                out=xuscr[tc * 128:(tc + 1) * 128, 0:1024], in_=ub),
                waits=[(sACT, r2u_d[tc])], inc=sSD, amt=16)
            xu_spill_sd[tc] = usp_sd[tc]

        def p3_B1(tc):
            par = tc % 2
            zp = zp_t[:, par * 1024: par * 1024 + 1024]
            bid = ("pt", tc % 2)
            pt = PT[tc % 2]
            for cc in range(CCH):
                te_tr = S.add("tensor", lambda e, zp=zp, cc=cc, pt=pt: e.transpose(
                    pt[:, cc * 128:(cc + 1) * 128], zp[:, cc * 128:(cc + 1) * 128],
                    ident_s[:, :]),
                    waits=[(sACT, zp_act[tc])] + (bank_w(bid) if cc == 0 else []), inc=sTE)
            wr[bid] = (sTE, sTE.n)
            for cc in range(CCH):
                dst = Trz[:, cc * T + tc * 128: cc * T + tc * 128 + 128]
                a_tz = S.add("scalar", lambda e, pt=pt, dst=dst, cc=cc: e.activation(
                    dst, pt[:, cc * 128:(cc + 1) * 128], AF.Identity,
                    bias=lgb_s[:, CCH + cc:CCH + cc + 1],
                    scale=lgb_s[:, cc:cc + 1]),
                    waits=[(sTE, sTE.n), (sTE, P2_TE)], inc=sACT)
            rd[bid] = (sACT, sACT.n)
            zp_te[tc] = sTE.n
            trz_act[tc] = sACT.n

        def p3_B23(tc):
            par = tc % 2
            lg = lgS[:, par * 64: par * 64 + 64]
            lg_bid = ("pb", 2)
            for cc in range(CCH):
                lw = Trz[:, cc * T + tc * 128: cc * T + tc * 128 + 128]
                S.add("tensor", lambda e, lw=lw: e.ldweights(lw),
                      waits=[(sACT, trz_act[tc]), (sGD, GD_SMALL)] +
                            (bank_w(lg_bid) if cc == 0 else []))
                te_lg = S.add("tensor", lambda e, lw=lw, cc=cc:
                              e.matmul(PB[2][:, 0:E], lw, rW_s[:, cc * E:(cc + 1) * E],
                                       start=(cc == 0), stop=(cc == CCH - 1)), inc=sTE)
            wr[lg_bid] = (sTE, te_lg)

            def rt1(e, lg=lg):
                e.drain()
                e.tensor_add(lg[:, 0:8], PB[2][:, 0:E], rbc_s[:, :])
                e.drain()
                return e.reduce_max(lg[:, 8:9], lg[:, 0:8], axis=AX, negate=True)
            d_r1 = S.add("vector", rt1, waits=[(sTE, te_lg), (sSD, SD_CONST)], inc=sDVE)
            rd[lg_bid] = (sDVE, d_r1)
            a_re_d[tc] = S.add("scalar", lambda e, lg=lg: e.activation(
                lg[:, 9:17], lg[:, 0:8], AF.Exp, bias=lg[:, 8:9], scale=1.0),
                waits=[(sDVE, d_r1)], inc=sACT)

        def p3_B3(tc):
            par = tc % 2
            lg = lgS[:, par * 64: par * 64 + 64]
            wv = wS[:, tc * E:(tc + 1) * E]
            wb = wbf_t[:, (tc % 16) * 8:(tc % 16) * 8 + 8]

            def rt2(e, wv=wv, wb=wb, lg=lg):
                ex = lg[:, 9:17]
                e.drain()
                e.reduce_max(lg[:, 17:18], ex, axis=AX)
                e.drain()
                e.tensor_scalar(lg[:, 18:26], ex, lg[:, 17:18], None, ALU.is_equal)
                e.drain()
                e.tensor_tensor(lg[:, 26:34], ex, lg[:, 18:26], ALU.mult)
                e.drain()
                e.tensor_sub(lg[:, 34:42], ex, lg[:, 26:34])
                e.drain()
                e.reduce_max(lg[:, 42:43], lg[:, 34:42], axis=AX)
                e.drain()
                e.tensor_scalar(lg[:, 43:51], ex, lg[:, 42:43], None, ALU.is_ge)
                e.drain()
                e.tensor_tensor(lg[:, 51:59], ex, lg[:, 43:51], ALU.mult)
                e.drain()
                e.tensor_add(lg[:, 59:60], lg[:, 17:18], lg[:, 42:43])
                e.drain()
                e.tensor_scalar(lg[:, 60:61], lg[:, 59:60], 1e-8, None, ALU.max)
                e.drain()
                e.reciprocal(lg[:, 61:62], lg[:, 60:61])
                e.drain()
                e.tensor_scalar(wv, lg[:, 51:59], lg[:, 61:62], None, ALU.mult)
                e.drain()
                return e.tensor_copy(wb, wv)
            d_w = S.add("vector", rt2, waits=[(sACT, a_re_d[tc])], inc=sDVE)
            w_done_dve[tc] = d_w

        a_re_d = {}
        for i in range(TC + 8):
            if 0 <= i - 7 < TC:
                p3_B3(i - 7)
            if 0 <= i - 6 < TC:
                p3_B23(i - 6)
            if 0 <= i - 5 < TC:
                p3_B1(i - 5)
            if i < TC:
                wo_last_te = p3_A1(i)
            if 0 <= i - 1 < TC:
                p3_A2(i - 1)
            if 0 <= i - 2 < TC:
                p3_A3(i - 2)
            if 0 <= i - 3 < TC:
                p3_A4(i - 3)
        ffn_w_load(0, 0, 1, 1, te_guard=P2_TE, slot=1,
                   extra=[(sSD, usp_sd[TC - 1]), (sDVE, sDVE.n)])
        P3_TE = sTE.n
        P3_DVE = sDVE.n
        P3_SD = sSD.n

        acc_reload_sd = {}
        acc_b2d = {}
        final_read_dve = {}
        u_rel_guard = []

        bidw = ("pt", 0)
        W8 = TC * 8
        te_wt = S.add("tensor", lambda e: e.transpose(
            PT[0][0:W8, 0:128], wbf_t[:, 0:W8], ident_s[:, :]),
            waits=[(sDVE, w_done_dve[TC - 1]), (sTE, P3_TE)] + bank_w(bidw), inc=sTE)
        wr[bidw] = (sTE, te_wt)
        a_wt = S.add("scalar", lambda e: e.copy(wtscr[0:W8, 0:128], PT[0][0:W8, 0:128]),
                     waits=[(sTE, te_wt)], inc=sACT)
        rd[bidw] = (sACT, a_wt)
        wt_sd = {}

        def emit_acc_chunk(th, i):
            tc = th * THC + i
            wl = [(sSD, P3_SD), (sDVE, P3_DVE)]
            if i < 4:
                wl.append((sTE, P3_TE))
            if th == 1:
                wl.append((sDVE, final_read_dve.get((0, i), 0)))
            acc_reload_sd[(th, i)] = S.add("sync", lambda e, tc=tc, i=i: e.dma_start(
                out=acc_ch(i), in_=xuscr[tc * 128:(tc + 1) * 128, 0:1024]), waits=wl,
                inc=sSD, amt=16)
            wt_sd[tc] = S.add("sync", lambda e, tc=tc: e.dma_start(
                out=wT_s[0:8, tc * 128:(tc + 1) * 128], in_=wtscr[tc * 8:tc * 8 + 8, 0:128]),
                waits=[(sACT, a_wt)], inc=sSD, amt=16)
            b2_bids = [("pb", 4), ("pb", 5)]
            S.add("tensor", lambda e, tc=tc: e.ldweights(wT_s[0:8, tc * 128:(tc + 1) * 128]),
                  waits=[(sSD, wt_sd[tc]), (sGD, GD_SMALL)] +
                        bank_w(b2_bids[0]) + bank_w(b2_bids[1]))
            for hs in range(2):
                te_b2 = S.add("tensor", lambda e, tc=tc, hs=hs:
                              e.matmul(PB[4 + hs][:, 0:512],
                                       wT_s[0:8, tc * 128:(tc + 1) * 128],
                                       b2_s[0:8, hs * 512:(hs + 1) * 512],
                                       start=True, stop=True), inc=sTE)
                wr[b2_bids[hs]] = (sTE, te_b2)

            def b2add(e, i=i):
                ac = acc_ch(i)
                e.drain()
                e.tensor_add(ac[:, 0:512], ac[:, 0:512], PB[4][:, 0:512])
                e.drain()
                return e.tensor_add(ac[:, 512:1024], ac[:, 512:1024], PB[5][:, 0:512])
            d_b2 = S.add("vector", b2add,
                         waits=[(sTE, te_b2), (sSD, acc_reload_sd[(th, i)])], inc=sDVE)
            rd[b2_bids[0]] = (sDVE, d_b2)
            rd[b2_bids[1]] = (sDVE, d_b2)
            acc_b2_dve = d_b2
            acc_reload_sd[(th, i)] = acc_reload_sd[(th, i)]
            acc_b2d[(th, i)] = acc_b2_dve

        gelu_act = {}
        final_store_sd = {}

        def emit_final(th, i):
            tc = th * THC + i
            par = i % 2
            so = par * 32
            x1r = x1r_t[:, par * 1024: par * 1024 + 1024]
            ob = outb_t[:, par * 1024: par * 1024 + 1024]
            if par == 0:
                wl = [(sDVE, final_read_dve.get((th, i - 2), 0)),
                      (sDVE, final_read_dve.get((th, i - 1), 0)),
                      (sDVE, final_read_dve.get((th - 1, THC - 2), 0)),
                      (sDVE, final_read_dve.get((th - 1, THC - 1), 0)),
                      (sSD, xu_spill_sd[TC - 1]), (sTE, zp_te.get(TC - 1, 0))]
                S.add("sync", lambda e, tc=tc: e.dma_start(
                    out=x1r_t[:, 0:2048].rearrange("p (two q) -> p two q", two=2),
                    in_=xuscr[tc * 128:(tc + 2) * 128, 1024:2048].rearrange(
                        "(two p) c -> p two c", p=128)), waits=wl, inc=sSD, amt=16)
            sd_r = sSD.n

            def fstat(e, i=i, so=so):
                ac = acc_ch(i)
                e.drain()
                e.bn_stats(stF[:, so:so + 6], ac[:, 0:512])
                e.bn_stats(stF[:, so + 6:so + 12], ac[:, 512:1024])
                e.drain()
                return e.bn_aggr(stF[:, so + 12:so + 14], stF[:, so:so + 12])
            d_fs = S.add("vector", fstat, inc=sDVE)

            def frsq(e, so=so):
                e.activation(stF[:, so + 14:so + 15], stF[:, so + 13:so + 14], AF.Ln,
                             bias=epsv[:, 0:1], scale=1.0)
                e.drain()
                e.activation(stF[:, so + 15:so + 16], stF[:, so + 14:so + 15], AF.Exp,
                             bias=0.0, scale=-0.5)
                e.drain()
                e.activation(stF[:, so + 16:so + 17], stF[:, so + 12:so + 13], AF.Copy,
                             bias=0.0, scale=-1.0)
                e.drain()
                return e.activation(stF[:, so + 17:so + 18], stF[:, so + 16:so + 17], AF.Copy,
                                    bias=0.0, scale=stF[:, so + 15:so + 16])
            a_fr = S.add("scalar", frsq, waits=[(sDVE, d_fs)], inc=sACT)

            def fout(e, i=i, x1r=x1r, ob=ob, so=so):
                ac = acc_ch(i)
                e.drain()
                e.scalar_tensor_tensor(ob, ac, stF[:, so + 15:so + 16], x1r, ALU.mult, ALU.add)
                e.drain()
                return e.tensor_scalar(ob, ob, stF[:, so + 17:so + 18], None, ALU.add)
            d_fo = S.add("vector", fout,
                         waits=[(sACT, a_fr), (sSD, sd_r),
                                (sSD, final_store_sd.get((th, i - 2), 0))], inc=sDVE)
            final_read_dve[(th, i)] = d_fo
            if par == 1:
                final_store_sd[(th, i - 1)] = final_store_sd[(th, i)] = S.add(
                    "sync", lambda e, tc=tc: e.dma_start(
                        out=y_out[(tc - 1) * 128:(tc + 1) * 128, :].rearrange(
                            "(two p) c -> p two c", p=128),
                        in_=outb_t[:, 0:2048].rearrange("p (two q) -> p two q", two=2)),
                    waits=[(sDVE, final_read_dve[(th, i - 1)]), (sDVE, d_fo)],
                    inc=sSD, amt=16)

        for th in range(2):
            for e_ in range(E):
                nth, ne = (th, e_ + 1) if e_ < E - 1 else (th + 1, 0)
                HL = T // 2
                n_tsl = max(1, HL // 512)
                tw = min(512, HL)
                for fc in range(FCH):
                    hf = fc // 8
                    gd = gd_half[(th, e_, 1, hf)]
                    slot = half_slot[(th, e_, 1, hf)]
                    bk = [2 * (fc % 2), 2 * (fc % 2) + 1]
                    w1_bids = [("pb", bk[0]), ("pb", bk[1])]
                    for cc in range(CCH):
                        lw = fsl[slot][:, cc * 1024 + (fc % 8) * 128:
                                       cc * 1024 + (fc % 8) * 128 + 128]
                        S.add("tensor", lambda e, lw=lw: e.ldweights(lw),
                              waits=[(sGD, gd), (sACT, trz_act.get(th * THC + THC - 1, 0))] +
                                    (bank_w(w1_bids[0]) + bank_w(w1_bids[1]) if cc == 0 else []))
                        for tsl in range(n_tsl):
                            rhs = Trz[:, cc * T + th * HL + tsl * tw:
                                      cc * T + th * HL + tsl * tw + tw]
                            te_mm = S.add("tensor", lambda e, b=PB[bk[tsl]], lw=lw, rhs=rhs, cc=cc,
                                          tw=tw: e.matmul(b[:, 0:tw], lw, rhs,
                                                          start=(cc == 0), stop=(cc == CCH - 1)),
                                          inc=sTE)
                    wr[w1_bids[0]] = (sTE, te_mm)
                    wr[w1_bids[1]] = (sTE, te_mm)
                    for tsl in range(n_tsl):
                        dst = hidT[:, fc * HL + tsl * tw: fc * HL + tsl * tw + tw]
                        a_g = S.add("scalar", lambda e, b=PB[bk[tsl]], dst=dst, e_=e_, fc=fc, tw=tw:
                                    e.activation(dst, b[:, 0:tw], AF.Gelu,
                                                 bias=b1_s[:, e_ * FCH + fc: e_ * FCH + fc + 1],
                                                 scale=1.0),
                                    waits=[(sTE, te_mm), (sTE, wo_last_te)], inc=sACT)
                        rd[w1_bids[tsl]] = (sACT, a_g)
                    gelu_act[fc] = a_g
                    if e_ == 0 and fc % 2 == 1 and fc // 2 < THC:
                        emit_acc_chunk(th, fc // 2)
                    if fc == 7:
                        ffn_w_load(th, e_, 2, 1)
                ffn_w_load(nth, ne, 1, 0)
                for t8 in range(THC):
                    bk = [2 * (t8 % 2), 2 * (t8 % 2) + 1]
                    w2_bids = [("pb", bk[0]), ("pb", bk[1])]
                    for fc in range(FCH):
                        hf = fc // 8
                        gd = gd_half[(th, e_, 2, hf)]
                        slot = half_slot[(th, e_, 2, hf)]
                        lw = hidT[:, fc * (T // 2) + t8 * 128: fc * (T // 2) + t8 * 128 + 128]
                        S.add("tensor", lambda e, lw=lw: e.ldweights(lw),
                              waits=[(sGD, gd), (sACT, gelu_act[fc])] +
                                    (bank_w(w2_bids[0]) + bank_w(w2_bids[1]) if fc == 0 else []))
                        for hs in range(2):
                            rhs = fsl[slot][:, (fc % 8) * 1024 + hs * 512:
                                            (fc % 8) * 1024 + hs * 512 + 512]
                            te_mm = S.add("tensor", lambda e, b=PB[bk[hs]], lw=lw, rhs=rhs, fc=fc:
                                          e.matmul(b[:, 0:512], lw, rhs,
                                                   start=(fc == 0), stop=(fc == FCH - 1)),
                                          inc=sTE)
                    wr[w2_bids[0]] = (sTE, te_mm)
                    wr[w2_bids[1]] = (sTE, te_mm)
                    tc = th * THC + t8
                    wcol = wS[:, tc * E + e_: tc * E + e_ + 1]

                    def stt_fn(e, t8=t8, wcol=wcol, bk=bk):
                        ac = acc_ch(t8)
                        e.drain()
                        e.scalar_tensor_tensor(ac[:, 0:512], PB[bk[0]][:, 0:512], wcol,
                                               ac[:, 0:512], ALU.mult, ALU.add)
                        e.drain()
                        return e.scalar_tensor_tensor(ac[:, 512:1024], PB[bk[1]][:, 0:512], wcol,
                                                      ac[:, 512:1024], ALU.mult, ALU.add)
                    d_st = S.add("vector", stt_fn,
                                 waits=[(sTE, te_mm), (sSD, acc_reload_sd[(th, t8)]),
                                        (sDVE, max(acc_b2d[(th, t8)],
                                                   w_done_dve.get(TC - 1, 0)))], inc=sDVE)
                    rd[w2_bids[0]] = (sDVE, d_st)
                    rd[w2_bids[1]] = (sDVE, d_st)
                    if e_ == E - 1 and t8 % 2 == 1:
                        emit_final(th, t8 - 1)
                        emit_final(th, t8)
                ffn_w_load(nth, ne, 1, 1)
                ffn_w_load(nth, ne, 2, 0)

        SD_ALL = sSD.n
        GD_ALL = sGD.n
        S.add("sync", lambda e: e.nop(), waits=[(sSD, SD_ALL), (sGD, GD_ALL)])

        S.emit(block)
    return nc


def _host_consts(T):
    import ml_dtypes
    BF = ml_dtypes.bfloat16
    TC = T // 128
    sl = np.asarray(_slopes(NH), dtype=np.float32)
    amap, ncol = _acol_map(T)
    karr = np.arange(BLK, dtype=np.float32) - np.float32(BLK - 1)
    bias_ref = (sl[:, None] * karr[None, :]).astype(np.float32)
    alibi = np.zeros((128, ncol), np.float32)
    for (h, qb, kb), col in amap.items():
        qw = QW[h]
        qcen = qb * qw + qw // 2
        ks = kb * 128
        alibi[:, col] = (bias_ref[h, ks:ks + 128].astype(np.float64)
                         - np.float64(bias_ref[h, qcen])).astype(np.float32)
    masks = np.zeros((128, 4 * 512), np.float32)
    p = np.arange(128, dtype=np.float32)
    j = np.arange(512, dtype=np.float32)
    for oi in range(4):
        masks[:, oi * 512:(oi + 1) * 512] = (oi * 128 + p[:, None]) <= j[None, :]
    ident = np.eye(128, dtype=np.float32)
    bcon = np.zeros((128, 128), np.float32)
    bcon[32, :] = 1.0
    bcon[64, :] = 1.0
    return {
        "alibi": alibi,
        "masks": masks.astype(BF),
        "ident": ident.astype(BF),
        "bcon": bcon.astype(BF),
    }


def kernel(L0_attn_x, L0_attn_Wq, L0_attn_Wk, L0_attn_Wv, L0_attn_Wo,
           L0_ffn_ln_g, L0_ffn_ln_b, L0_ffn_rW, L0_ffn_rb, L0_ffn_W1,
           L0_ffn_b1, L0_ffn_W2, L0_ffn_b2):
    T = 2048
    x = np.ascontiguousarray(np.asarray(L0_attn_x, dtype=np.float32))
    consts = _host_consts(T)
    lg = np.asarray(L0_ffn_ln_g, np.float32).reshape(CCH, 128).T
    lb = np.asarray(L0_ffn_ln_b, np.float32).reshape(CCH, 128).T
    lgb = np.ascontiguousarray(np.concatenate([lg, lb], axis=1))
    b1rr = np.ascontiguousarray(
        np.asarray(L0_ffn_b1, np.float32).reshape(E, FCH, 128).transpose(2, 0, 1).reshape(128, E * FCH))
    rbc = np.ascontiguousarray(np.broadcast_to(np.asarray(L0_ffn_rb, np.float32), (128, E)))
    common = {
        "Wq": np.ascontiguousarray(L0_attn_Wq, np.float32),
        "Wk": np.ascontiguousarray(L0_attn_Wk, np.float32),
        "Wv": np.ascontiguousarray(L0_attn_Wv, np.float32),
        "Wo": np.ascontiguousarray(L0_attn_Wo, np.float32),
        "W1": np.ascontiguousarray(L0_ffn_W1, np.float32),
        "W2": np.ascontiguousarray(L0_ffn_W2, np.float32),
        "b2": np.ascontiguousarray(L0_ffn_b2, np.float32),
        "rW": np.ascontiguousarray(L0_ffn_rW, np.float32),
        "lgb": lgb, "b1r": b1rr, "rbc": rbc,
        "alibi": consts["alibi"], "masks": consts["masks"],
        "ident": consts["ident"], "bcon": consts["bcon"],
    }
    nc = build_core_program(T)
    in_maps = [dict(common, x=np.ascontiguousarray(x[c])) for c in range(8)]
    res = run_bass_kernel_spmd(nc, in_maps, core_ids=list(range(8)))
    return np.stack([np.asarray(res.results[c]["y"]) for c in range(8)], axis=0)


# version 23
# speedup vs baseline: 1.0406x; 1.0043x over previous
import math
import contextlib
import numpy as np
import concourse.bass as bass
import concourse.mybir as mybir
from concourse.bass_utils import run_bass_kernel_spmd

F32 = mybir.dt.float32
BF16 = mybir.dt.bfloat16
U8 = mybir.dt.uint8
AF = mybir.ActivationFunctionType
ALU = mybir.AluOpType
AX = mybir.AxisListType.X

C = 1024
NH = 16
HD = 64
E = 8
HF = 2048
EPS = 1e-5
BLK = 2048
CCH = C // 128
FCH = HF // 128
RW = 160

QW = [128, 128, 256, 256] + [512] * 12


def _acol_map(T):
    m = {}
    col = 0
    for h in range(NH):
        qw = QW[h]
        for qb in range(T // qw):
            qe = qb * qw + qw - 1
            for kb in range(qe // 128 + 1):
                m[(h, qb, kb)] = col
                col += 1
    return m, col

AR_A = 0
AR_B = 32768
AR_C = 65536
AR_D = 98304
AR_E = 131072
AR_F = 147456
AR_TOTAL = 163840


def _slopes(n):
    def pow2(m):
        start = 2 ** (-(2 ** (-(math.log2(m) - 3))))
        return [start * start ** i for i in range(m)]
    if math.log2(n).is_integer():
        return pow2(n)
    c = 2 ** math.floor(math.log2(n))
    return pow2(c) + _slopes(2 * c)[0::2][:n - c]


class Sem:
    def __init__(self, nc, es, name):
        self.h = es.enter_context(nc.semaphore(name))
        self.n = 0


class Sched:
    ENGS = ("tensor", "scalar", "vector", "gpsimd", "sync")

    def __init__(self):
        self.ops = {e: [] for e in self.ENGS}
        self.last_wait = {e: {} for e in self.ENGS}

    def add(self, eng, fn, waits=(), inc=None, amt=1):
        if inc is not None and eng in ("gpsimd", "sync") and inc.n > 0:
            waits = list(waits) + [(inc, inc.n)]
        wl = []
        lw = self.last_wait[eng]
        for (s, v) in waits:
            if v <= 0 or lw.get(id(s), 0) >= v:
                continue
            lw[id(s)] = v
            wl.append((s, v))
        self.ops[eng].append((wl, fn, inc, amt))
        if inc is not None:
            inc.n += amt
            return inc.n
        return None

    def emit(self, block):
        for name in self.ENGS:
            oplist = self.ops[name]
            if not oplist:
                continue
            dec = getattr(block, name)

            def section(eng, _ops=oplist):
                for (wl, fn, inc, amt) in _ops:
                    for (s, v) in wl:
                        eng.wait_ge(s.h, v)
                    ins = fn(eng)
                    if inc is not None:
                        ins.then_inc(inc.h, amt)
            dec(section)


def build_core_program(T=2048):
    assert T % 512 == 0
    TC = T // 128
    QC = T // 512
    THC = TC // 2
    NPAIR_D = min(102, TC * 8)
    nc = bass.Bass("TRN2", target_bir_lowering=False)

    x_in = nc.dram_tensor("x", [T, C], F32, kind="ExternalInput").ap()
    Wq = nc.dram_tensor("Wq", [C, C], F32, kind="ExternalInput").ap()
    Wk = nc.dram_tensor("Wk", [C, C], F32, kind="ExternalInput").ap()
    Wv = nc.dram_tensor("Wv", [C, C], F32, kind="ExternalInput").ap()
    Wo = nc.dram_tensor("Wo", [C, C], F32, kind="ExternalInput").ap()
    W1 = nc.dram_tensor("W1", [E, C, HF], F32, kind="ExternalInput").ap()
    W2 = nc.dram_tensor("W2", [E, HF, C], F32, kind="ExternalInput").ap()
    b2v = nc.dram_tensor("b2", [E, C], F32, kind="ExternalInput").ap()
    rWv = nc.dram_tensor("rW", [C, E], F32, kind="ExternalInput").ap()
    amap, acols = _acol_map(T)
    CF32W = 2 * CCH + E * FCH + E + acols
    CB16W = 4 * 512 + 128
    cf32_in = nc.dram_tensor("cf32", [128, CF32W], F32, kind="ExternalInput").ap()
    cb16_in = nc.dram_tensor("cb16", [128, CB16W], BF16, kind="ExternalInput").ap()
    ident_in = nc.dram_tensor("ident", [128, 128], BF16, kind="ExternalInput").ap()
    y_out = nc.dram_tensor("y", [T, C], F32, kind="ExternalOutput").ap()
    xuscr = nc.dram_tensor("xuscr", [T, 2 * C], F32, kind="Internal").ap()

    es = contextlib.ExitStack()
    with es:
        arena = es.enter_context(nc.sbuf_tensor("arena", [128, AR_TOTAL], U8))
        ar_base = nc.lookup_mloc(arena).addr

        def vt(name, shape, dtype, byte_off):
            return nc.alloc_sbuf_tensor_at(name, shape, dtype, offset=ar_base + byte_off)

        Trz1 = vt("Trz1", [128, CCH * T], BF16, AR_A)
        TrAttn = vt("TrAttn", [128, CCH * T], BF16, AR_A)
        hidT = vt("hidT", [128, FCH * (T // 2)], BF16, AR_A)
        qT = vt("qT", [128, CCH * T], BF16, AR_B)
        fsl0 = vt("fsl0", [128, 8192], BF16, AR_B + 16384)
        fsl1 = vt("fsl1", [128, 8192], BF16, AR_B)
        kT = vt("kT", [128, CCH * T], BF16, AR_C)
        fsl2 = vt("fsl2", [128, 8192], BF16, AR_C)
        x0_t = vt("x0t", [128, 3072], F32, AR_C + 16384)
        u_t = vt("ut", [128, 4096], F32, AR_B)
        x1_t = vt("x1t", [128, 3072], F32, AR_F)
        accA = vt("accA", [128, 4096], F32, AR_E)
        accB = vt("accB", [128, 4096], F32, AR_C + 16384)
        vgD = vt("vgD", [128, NPAIR_D * RW], BF16, AR_D)
        Trz = vt("Trz", [128, CCH * T], BF16, AR_D)
        wsl = [vt("wsl0", [128, 4096], BF16, AR_E),
               vt("wsl1", [128, 4096], BF16, AR_E + 8192)]
        xpp = vt("xpp", [128, 4096], F32, AR_F)
        z1pp = vt("z1pp", [128, 3072], BF16, AR_C)
        slabs = vt("slabs", [128, 2048], BF16, AR_F)
        vgF = vt("vgF", [128, max(0, TC * 8 - NPAIR_D) * RW + 16], BF16, AR_F + 4096)
        zp_t = vt("zp", [128, 2048], BF16, AR_F + 12288)
        x1r_t = vt("x1r", [128, 2048], F32, AR_F)
        outb_t = vt("outb", [128, 2048], F32, AR_F + 8192)

        cfa = es.enter_context(nc.sbuf_tensor("cfa", [128, CF32W], F32))
        cfa_base = nc.lookup_mloc(cfa).addr
        lgb_s = nc.alloc_sbuf_tensor_at("lgb_s", [128, 2 * CCH], F32, offset=cfa_base)
        b1_s = nc.alloc_sbuf_tensor_at("b1_s", [128, E * FCH], F32,
                                       offset=cfa_base + 8 * CCH)
        rbc_s = nc.alloc_sbuf_tensor_at("rbc_s", [128, E], F32,
                                        offset=cfa_base + 8 * CCH + 4 * E * FCH)
        alibi_s = nc.alloc_sbuf_tensor_at("alibi_s", [128, acols], F32,
                                          offset=cfa_base + 8 * CCH + 4 * E * FCH + 4 * E)
        cba = es.enter_context(nc.sbuf_tensor("cba", [128, CB16W], BF16))
        cba_base = nc.lookup_mloc(cba).addr
        masks_s = nc.alloc_sbuf_tensor_at("masks_s", [128, 4 * 512], BF16, offset=cba_base)
        bcon_s = nc.alloc_sbuf_tensor_at("bcon_s", [128, 128], BF16,
                                         offset=cba_base + 2 * 4 * 512)
        ident_s = es.enter_context(nc.sbuf_tensor("ident_s", [128, 128], BF16))
        rW_s = es.enter_context(nc.sbuf_tensor("rW_s", [128, CCH * E], BF16))
        b2_s = es.enter_context(nc.sbuf_tensor("b2_s", [8, C], BF16))
        wT_s = es.enter_context(nc.sbuf_tensor("wT_s", [8, T], BF16))
        recb_s = es.enter_context(nc.sbuf_tensor("recb_s", [128, 2048], BF16))
        rtmp_s = es.enter_context(nc.sbuf_tensor("rtmp_s", [128, 512], F32))
        stg_s = es.enter_context(nc.sbuf_tensor("stg_s", [128, 2048], BF16))
        wS = es.enter_context(nc.sbuf_tensor("wS", [128, TC * E], F32))
        stP = es.enter_context(nc.sbuf_tensor("stP", [128, 128], F32))
        st3 = es.enter_context(nc.sbuf_tensor("st3", [128, 128], F32))
        lgS = es.enter_context(nc.sbuf_tensor("lgS", [128, 128], F32))
        stF = es.enter_context(nc.sbuf_tensor("stF", [128, 64], F32))
        epsv = es.enter_context(nc.sbuf_tensor("epsv", [128, 1], F32))
        wbf_t = es.enter_context(nc.sbuf_tensor("wbf_t", [128, 8 * 16], BF16))
        wtscr = es.enter_context(nc.sbuf_tensor("wtscr", [128, 128], BF16))

        PB = [es.enter_context(nc.psum_tensor("pb%d" % i, [128, 512], F32)) for i in range(6)]
        PT = [es.enter_context(nc.psum_tensor("pt%d" % i, [128, 1024], BF16)) for i in range(2)]

        sTE = Sem(nc, es, "sTE")
        sACT = Sem(nc, es, "sACT")
        sDVE = Sem(nc, es, "sDVE")
        sGD = Sem(nc, es, "sGD")
        sSD = Sem(nc, es, "sSD")

        S = Sched()
        block = es.enter_context(nc.Block())

        wr = {}
        rd = {}

        def bank_w(bid):
            return [rd[bid]] if bid in rd else []

        def bank_r(bid):
            return [wr[bid]] if bid in wr else []

        def vgp(pair):
            if pair < NPAIR_D:
                return vgD[:, pair * RW:(pair + 1) * RW]
            p = pair - NPAIR_D
            return vgF[:, p * RW:(p + 1) * RW]

        S.add("sync", lambda e: e.dma_start(out=ident_s[:, :], in_=ident_in[:, :]),
              inc=sSD, amt=16)
        SD_IDENT = sSD.n
        S.add("sync", lambda e: e.dma_start(
            out=xpp[:, 0:2048].rearrange("p (two q) -> p two q", two=2),
            in_=x_in[0:256, :].rearrange("(two p) c -> p two c", p=128)), inc=sSD, amt=16)
        SD_X0 = sSD.n
        S.add("gpsimd", lambda e: e.dma_start(
            out=rW_s[:, :].rearrange("p (cc e) -> p cc e", cc=CCH),
            in_=rWv.rearrange("(cc p) e -> p cc e", p=128)), inc=sGD, amt=16)
        S.add("gpsimd", lambda e: e.dma_start(out=b2_s[:, :], in_=b2v[:, :]), inc=sGD, amt=16)
        GD_SMALL = sGD.n
        S.add("vector", lambda e: e.memset(epsv[:, :], EPS), inc=sDVE)
        S.add("vector", lambda e: e.memset(recb_s[:, :], 0.0), inc=sDVE)
        S.add("vector", lambda e: e.memset(rtmp_s[:, :], 0.0), inc=sDVE)
        def ones_d(e):
            e.drain()
            e.memset(vgD[:, :], 0.0)
            e.drain()
            for pair in range(NPAIR_D):
                e.memset(vgp(pair)[:, 64:65], 1.0)
            e.drain()
            return e.engine_nop()
        S.add("vector", ones_d, inc=sDVE)
        DVE_INIT = sDVE.n

        def w_half_load(Wm, hf):
            def f(e, Wm=Wm, hf=hf, slot=wsl[hf % 2]):
                return e.dma_start(
                    out=slot[:, :].rearrange("p (cc d) -> p cc d", cc=CCH),
                    in_=Wm[:, hf * 512:(hf + 1) * 512].rearrange("(cc p) d -> p cc d", p=128))
            return f

        GD_W = {}
        S.add("gpsimd", w_half_load(Wq, 0), inc=sGD, amt=16); GD_W[(0, 0)] = sGD.n
        S.add("gpsimd", w_half_load(Wq, 1), inc=sGD, amt=16); GD_W[(0, 1)] = sGD.n

        p0_evac = {}

        def qk_part(wi, hf, tp, DST, dcs=(0, 1, 2, 3), defer_evac=False):
            gd = GD_W[(wi, hf)]
            deferred = []
            for dc in dcs:
                bids = [("pb", 2 * (dc % 2)), ("pb", 2 * (dc % 2) + 1)]
                for cc in range(CCH):
                    lw = wsl[hf % 2][:, cc * 512 + dc * 128: cc * 512 + dc * 128 + 128]
                    S.add("tensor", lambda e, lw=lw: e.ldweights(lw),
                          waits=[(sGD, gd), (sDVE, p0_evac[8 * tp + 7])])
                    for half in range(2):
                        ts4 = tp * 2 + half
                        rhs = Trz1[:, cc * T + ts4 * 512: cc * T + ts4 * 512 + 512]
                        te_mm = S.add("tensor", lambda e, b=PB[2 * (dc % 2) + half],
                                      lw=lw, rhs=rhs, cc=cc:
                                      e.matmul(b[:, 0:512], lw, rhs,
                                               start=(cc == 0), stop=(cc == CCH - 1)),
                                      waits=bank_w(bids[half]) if cc == 0 else [], inc=sTE)
                        if cc == CCH - 1:
                            wr[bids[half]] = (sTE, te_mm)

                def ev(dc=dc, bids=bids, hf=hf, tp=tp, wrv=None):
                    wrv = [wr[bids[0]], wr[bids[1]]]
                    for half in range(2):
                        ts4 = tp * 2 + half
                        d = DST[:, (hf * 4 + dc) * T + ts4 * 512:
                                (hf * 4 + dc) * T + ts4 * 512 + 512]
                        d_ev = S.add("vector", lambda e, b=PB[2 * (dc % 2) + half], d=d: (
                            e.drain(), e.tensor_copy(d, b[:, 0:512]))[-1],
                            waits=[wrv[half]], inc=sDVE)
                        rd[bids[half]] = (sDVE, d_ev)
                if defer_evac:
                    deferred.append(ev)
                else:
                    ev()
            return deferred

        p0_norm_act = {}
        p0_tr_te = {}
        p0_sd = {}
        p0_pend = []
        for tc in range(TC):
            xb = xpp[:, (tc % 4) * 1024:(tc % 4) * 1024 + 1024]
            z1 = z1pp[:, (tc % 3) * 1024:(tc % 3) * 1024 + 1024]
            if tc == 0:
                p0_sd[0] = SD_X0
            elif tc % 2 == 0:
                s4 = (tc % 4) * 1024
                p0_sd[tc] = S.add("sync", lambda e, tc=tc, s4=s4: e.dma_start(
                    out=xpp[:, s4:s4 + 2048].rearrange("p (two q) -> p two q", two=2),
                    in_=x_in[tc * 128:(tc + 2) * 128, :].rearrange("(two p) c -> p two c", p=128)),
                    waits=[(sDVE, p0_norm_act.get(tc - 3, 0))], inc=sSD, amt=16)
            sd_x = p0_sd[tc - tc % 2]

            sp = stP[:, (tc % 2) * 64:(tc % 2) * 64 + 64]

            def stats0(e, xb=xb, sp=sp):
                e.drain()
                e.bn_stats(sp[:, 0:6], xb[:, 0:512])
                e.bn_stats(sp[:, 6:12], xb[:, 512:1024])
                e.drain()
                return e.bn_aggr(sp[:, 12:14], sp[:, 0:12])
            dve_st = S.add("vector", stats0,
                           waits=[(sSD, sd_x)], inc=sDVE)

            def rsq0(e, sp=sp):
                e.activation(sp[:, 14:15], sp[:, 13:14], AF.Ln, bias=epsv[:, 0:1], scale=1.0)
                e.drain()
                return e.activation(sp[:, 15:16], sp[:, 14:15], AF.Exp, bias=0.0, scale=-0.5)
            act_rq = S.add("scalar", rsq0, waits=[(sDVE, dve_st)], inc=sACT)

            def nrm0(e, sp=sp, z1=z1, xb=xb):
                e.drain()
                return e.tensor_scalar(z1, xb, sp[:, 12:13], sp[:, 15:16],
                                       ALU.subtract, ALU.mult)
            act_nm = S.add("vector", nrm0,
                           waits=[(sACT, act_rq), (sTE, p0_tr_te.get(tc - 3, 0))], inc=sDVE)
            p0_norm_act[tc] = act_nm

            def p0_tr(tc=tc, z1=z1):
                bid = ("pt", tc % 2)
                pt = PT[tc % 2]
                for cc in range(CCH):
                    te_tr = S.add("tensor", lambda e, z1=z1, cc=cc, pt=pt: e.transpose(
                        pt[:, cc * 128:(cc + 1) * 128], z1[:, cc * 128:(cc + 1) * 128],
                        ident_s[:, :]),
                        waits=[(sDVE, p0_norm_act[tc]), (sSD, SD_IDENT)] +
                              (bank_w(bid) if cc == 0 else []), inc=sTE)
                wr[bid] = (sTE, sTE.n)
                dst = Trz1[:, 0:CCH * T].rearrange(
                    "p (cc t) -> p cc t", cc=CCH)[:, :, tc * 128:(tc + 1) * 128]
                d_ev = S.add("vector", lambda e, pt=pt, dst=dst: (
                    e.drain(), e.tensor_copy(dst, pt[:, 0:1024].rearrange(
                        "p (cc q) -> p cc q", cc=CCH)))[-1],
                    waits=[(sTE, sTE.n)], inc=sDVE)
                rd[bid] = (sDVE, d_ev)
                p0_evac[tc] = d_ev
                p0_tr_te[tc] = sTE.n
            p0_pend.append(p0_tr)
            if len(p0_pend) > 2:
                p0_pend.pop(0)()
            if tc == 9:
                pend_qk_evacs = qk_part(0, 0, 0, qT, dcs=(0, 1), defer_evac=True)
        while p0_pend:
            p0_pend.pop(0)()
        P0_TEN = sTE.n
        P0_ACT = sACT.n
        P0_DVE = sDVE.n
        S.add("sync", lambda e: e.dma_start(out=cfa[:, :], in_=cf32_in[:, :]),
              inc=sSD, amt=16)
        S.add("sync", lambda e: e.dma_start(out=cba[:, :], in_=cb16_in[:, :]),
              inc=sSD, amt=16)
        SD_CONST = sSD.n
        def ones_f(e):
            e.drain()
            e.memset(vgF[:, :], 0.0)
            e.drain()
            for pair in range(NPAIR_D, TC * 8):
                e.memset(vgp(pair)[:, 64:65], 1.0)
            e.drain()
            return e.engine_nop()
        S.add("vector", ones_f, waits=[(sDVE, P0_DVE), (sSD, p0_sd[TC - 2])], inc=sDVE)
        for f in pend_qk_evacs:
            f()
        qk_part(0, 0, 0, qT, dcs=(2, 3))

        qk_part(0, 1, 0, qT)
        qk_part(0, 0, 1, qT)
        S.add("gpsimd", w_half_load(Wk, 0), waits=[(sTE, sTE.n)], inc=sGD, amt=16)
        GD_W[(1, 0)] = sGD.n
        qk_part(0, 1, 1, qT)
        S.add("gpsimd", w_half_load(Wk, 1), waits=[(sTE, sTE.n)], inc=sGD, amt=16)
        GD_W[(1, 1)] = sGD.n
        qk_part(1, 0, 0, kT)
        qk_part(1, 1, 0, kT)
        qk_part(1, 0, 1, kT)
        S.add("gpsimd", w_half_load(Wv, 0), waits=[(sTE, sTE.n)], inc=sGD, amt=16)
        GD_W[(2, 0)] = sGD.n
        qk_part(1, 1, 1, kT)
        S.add("gpsimd", w_half_load(Wv, 1), waits=[(sTE, sTE.n)], inc=sGD, amt=16)
        GD_W[(2, 1)] = sGD.n
        P1_ACT_QK = sACT.n

        for tc in range(TC):
            for ds in range(2):
                gd = GD_W[(2, ds)]
                bid = ("pb", 4 + ds)
                for cc in range(CCH):
                    lw = Trz1[:, cc * T + tc * 128: cc * T + tc * 128 + 128]
                    S.add("tensor", lambda e, lw=lw: e.ldweights(lw), waits=[(sGD, gd)])
                    rhs = wsl[ds % 2][:, cc * 512: cc * 512 + 512]
                    te_mm = S.add("tensor", lambda e, b=PB[4 + ds], lw=lw, rhs=rhs, cc=cc:
                                  e.matmul(b[:, 0:512], lw, rhs,
                                           start=(cc == 0), stop=(cc == CCH - 1)),
                                  waits=bank_w(bid) if cc == 0 else [],
                                  inc=sTE)
                wr[bid] = (sTE, te_mm)

                def vevac(e, b=PB[4 + ds], tc=tc, ds=ds):
                    from concourse.ap import AP
                    e.drain()
                    pair0 = tc * 8 + ds * 4
                    if pair0 + 4 <= NPAIR_D:
                        base = vgD.ap()
                        part = list(base.ap[0])
                        dst = AP(tensor=base.tensor, offset=base.offset + pair0 * RW,
                                 ap=[part, [RW, 4], [96, 2], [1, 64]])
                        e.tensor_copy(dst, b[:, 0:512].rearrange(
                            "p (pr two d) -> p pr two d", pr=4, two=2))
                    else:
                        for j in range(8):
                            h = ds * 8 + j
                            pr = tc * 8 + h // 2
                            off = 0 if h % 2 == 0 else 96
                            e.tensor_copy(vgp(pr)[:, off:off + 64], b[:, j * 64:(j + 1) * 64])
                    e.drain()
                    return e.engine_nop()
                d_ev = S.add("vector", vevac, waits=[wr[bid]], inc=sDVE)
                rd[bid] = (sDVE, d_ev)
        P1_TE = sTE.n
        P1_DVE = sDVE.n
        S.add("gpsimd", w_half_load(Wo, 0), waits=[(sTE, P1_TE)], inc=sGD, amt=16)
        GD_WO0 = sGD.n
        S.add("gpsimd", w_half_load(Wo, 1), waits=[(sTE, P1_TE)], inc=sGD, amt=16)
        GD_WO1 = sGD.n

        slab_free = [0, 0, 0, 0]
        stg_free = [0, 0, 0, 0]
        gi = 0
        pend_evac = [None, None]
        for pi in range(NH // 2):
            qw = QW[2 * pi]
            dc = pi
            for qb in range(T // qw):
                qs = qb * qw
                qe = qs + qw - 1
                K = qe // 128 + 1
                gpar = gi % 2
                pv_bid = {0: ("pb", 2 + 2 * gpar), 1: ("pb", 3 + 2 * gpar)}
                pvb_i = {0: 2 + 2 * gpar, 1: 3 + 2 * gpar}
                exp_done = {}
                mask_done = {}
                if pend_evac[0] is not None:
                    for f in pend_evac[0][0]:
                        f(d_rc=pend_evac[0][1][f])
                    pend_evac[0] = None
                if pend_evac[1] is not None:
                    rfs, efs = pend_evac[1]
                    drcs = {}
                    for rf, ef in zip(rfs, efs):
                        drcs[ef] = rf()
                    pend_evac[0] = (efs, drcs)
                    pend_evac[1] = None

                def emit_score(kb, hp):
                    h = 2 * pi + hp
                    sbi = hp
                    s_bid = ("pb", sbi)
                    r0 = hp * 64
                    tile = None if hp == 0 else (64, 0)
                    lwk = kT[r0:r0 + 64, dc * T + kb * 128: dc * T + kb * 128 + 128]
                    qrhs = qT[r0:r0 + 64, dc * T + qs: dc * T + qs + qw]
                    S.add("tensor", lambda e, lwk=lwk, tile=tile: e.ldweights(lwk, tile_position=tile),
                          waits=[(sACT, P1_ACT_QK)] + bank_w(s_bid))
                    te_s = S.add("tensor", lambda e, sb=PB[sbi], lwk=lwk, qrhs=qrhs, tile=tile,
                                 qw=qw: e.matmul(sb[:, 0:qw], lwk, qrhs, start=True, stop=True,
                                                 tile_position=tile), inc=sTE)
                    wr[s_bid] = (sTE, te_s)
                    slot = (kb % 2) * 2 + hp
                    slab = slabs[:, slot * 512: slot * 512 + qw]
                    ac = amap[(h, qb, kb)]
                    bias = alibi_s[:, ac:ac + 1]
                    a_ex = S.add("scalar", lambda e, slab=slab, sb=PB[sbi], bias=bias, qw=qw:
                                 e.activation(slab, sb[:, 0:qw], AF.Exp, bias=bias, scale=0.125),
                                 waits=[(sTE, te_s), (sTE, slab_free[slot]), (sSD, SD_CONST)],
                                 inc=sACT)
                    rd[s_bid] = (sACT, a_ex)
                    exp_done[(kb, hp)] = a_ex
                    if kb * 128 + 127 > qs:
                        oi = (kb * 128 - qs) // 128
                        d_mk = S.add("vector", lambda e, slab=slab, oi=oi, qw=qw: (
                            e.drain(), e.tensor_mul(slab, slab,
                                                    masks_s[:, oi * 512:oi * 512 + qw]))[-1],
                            waits=[(sACT, a_ex), (sSD, SD_CONST)], inc=sDVE)
                        mask_done[(kb, hp)] = d_mk

                def emit_pv(kb, hp):
                    slot = (kb % 2) * 2 + hp
                    slab = slabs[:, slot * 512: slot * 512 + qw]
                    pv_w = ([(sDVE, mask_done[(kb, hp)])] if (kb, hp) in mask_done
                            else [(sACT, exp_done[(kb, hp)])])
                    pr = kb * 8 + pi
                    stat = vgp(pr)[:, 0:65] if hp == 0 else vgp(pr)[:, 32:160]
                    if kb == 0:
                        pv_w += bank_w(pv_bid[hp]) + [(sDVE, P1_DVE)]
                    mtop = 65 if hp == 0 else 128
                    S.add("tensor", lambda e, stat=stat: e.ldweights(stat), waits=pv_w)
                    te_pv = S.add("tensor", lambda e, pvb=PB[pvb_i[hp]], stat=stat, slab=slab,
                                  kb=kb, K=K, mtop=mtop, qw=qw:
                                  e.matmul(pvb[0:mtop, 0:qw], stat, slab,
                                           start=(kb == 0), stop=(kb == K - 1)),
                                  inc=sTE)
                    slab_free[slot] = te_pv
                    return te_pv

                te_pv_d = {}
                for kb in range(K):
                    emit_score(kb, 0)
                    emit_score(kb, 1)
                    if kb >= 1:
                        te_pv_d[0] = emit_pv(kb - 1, 0)
                        te_pv_d[1] = emit_pv(kb - 1, 1)
                te_pv_d[0] = emit_pv(K - 1, 0)
                te_pv_d[1] = emit_pv(K - 1, 1)
                wr[pv_bid[0]] = (sTE, te_pv_d[0])
                wr[pv_bid[1]] = (sTE, te_pv_d[1])

                rfs = []
                efs = []
                for hp in range(2):
                    h = 2 * pi + hp
                    drow = 64 if hp == 0 else 32
                    rslot = gpar * 2 + hp

                    def emit_recip(pvbi=pvb_i[hp], drow=drow, rslot=rslot, qw=qw):
                        def recip_fn(e, pvb=PB[pvbi], drow=drow, rslot=rslot, qw=qw):
                            e.drain()
                            e.reciprocal(rtmp_s[drow:drow + 1, 0:qw], pvb[drow:drow + 1, 0:qw])
                            e.drain()
                            return e.tensor_copy(
                                recb_s[drow:drow + 1, rslot * 512: rslot * 512 + qw],
                                rtmp_s[drow:drow + 1, 0:qw])
                        return S.add("vector", recip_fn,
                                     waits=[(sTE, te_pv_d[hp]), (sDVE, DVE_INIT)], inc=sDVE)

                    def emit_evac(pvbi=pvb_i[hp], pvbid=pv_bid[hp], te_pv=te_pv_d[hp],
                                  hp=hp, drow=drow, rslot=rslot, qw=qw, dc=dc, qs=qs,
                                  d_rc=0):
                        r0 = hp * 64
                        tileb = (64, 0) if drow == 64 else (32, 0)
                        stg = stg_s[:, rslot * 512: rslot * 512 + qw]
                        a_st = S.add("scalar", lambda e, stg=stg, pvb=PB[pvbi], r0=r0, qw=qw:
                                     e.copy(stg[r0:r0 + 64, :], pvb[r0:r0 + 64, 0:qw]),
                                     waits=[(sTE, te_pv), (sDVE, stg_free[rslot])], inc=sACT)
                        S.add("tensor", lambda e, tileb=tileb, drow=drow: e.ldweights(
                            bcon_s[drow:drow + 32, :], tile_position=tileb),
                            waits=[(sDVE, d_rc), (sSD, SD_CONST), (sACT, a_st)])
                        te_bc = S.add("tensor", lambda e, bcb=PB[pvbi], tileb=tileb, drow=drow,
                                      rslot=rslot, qw=qw:
                                      e.matmul(bcb[:, 0:qw], bcon_s[drow:drow + 32, :],
                                               recb_s[drow:drow + 32,
                                                      rslot * 512: rslot * 512 + qw],
                                               start=True, stop=True, tile_position=tileb),
                                      inc=sTE)
                        dsta = TrAttn[r0:r0 + 64, dc * T + qs: dc * T + qs + qw]
                        d_ml = S.add("vector", lambda e, dsta=dsta, stg=stg, bcb=PB[pvbi],
                                     r0=r0, qw=qw: (
                            e.drain(), e.tensor_mul(dsta, stg[r0:r0 + 64, :],
                                                    bcb[r0:r0 + 64, 0:qw]))[-1],
                            waits=[(sTE, te_bc), (sACT, a_st), (sTE, P1_TE)], inc=sDVE)
                        rd[pvbid] = (sDVE, d_ml)
                        stg_free[rslot] = d_ml
                    rfs.append(emit_recip)
                    efs.append(emit_evac)
                pend_evac[1] = (rfs, efs)
                gi += 1
        if pend_evac[1] is not None:
            rfs, efs = pend_evac[1]
            drcs = {}
            for rf, ef in zip(rfs, efs):
                drcs[ef] = rf()
            if pend_evac[0] is not None:
                for f in pend_evac[0][0]:
                    f(d_rc=pend_evac[0][1][f])
            pend_evac[0] = (efs, drcs)
            pend_evac[1] = None
        if pend_evac[0] is not None:
            for f in pend_evac[0][0]:
                f(d_rc=pend_evac[0][1][f])
        P2_TE = sTE.n
        P2_DVE = sDVE.n
        P2_ACT = sACT.n

        def acc_ch(i):
            if i < 4:
                return accA[:, i * 1024:(i + 1) * 1024]
            return accB[:, (i - 4) * 1024:(i - 4) * 1024 + 1024]

        fsl = [fsl0, fsl1, fsl2]
        gd_half = {}
        half_slot = {}
        _gctr = [0]

        gd_half_a = {}

        def ffn_w_load(th, e_, kind, hf, te_guard=None, extra=(), slot=None,
                       te_guard2=None):
            if th >= 2:
                return
            if slot is None:
                slot = _gctr[0] % 3
            _gctr[0] += 1
            half_slot[(th, e_, kind, hf)] = slot

            def f(e, e_=e_, kind=kind, hf=hf, slot=slot, sub=None):
                dst = fsl[slot][:, :].rearrange("p (a b) -> p a b", a=8)
                if kind == 1:
                    src = W1[e_, :, hf * 1024:(hf + 1) * 1024].rearrange(
                        "(cc p) f -> p cc f", p=128)
                else:
                    src = W2[e_, hf * 1024:(hf + 1) * 1024, :].rearrange(
                        "(fc p) h -> p fc h", p=128)
                if sub is not None:
                    dst = dst[:, :, sub * 512:(sub + 1) * 512]
                    src = src[:, :, sub * 512:(sub + 1) * 512]
                return e.dma_start(out=dst, in_=src)
            if te_guard2 is not None:
                S.add("gpsimd", lambda e: f(e, sub=0),
                      waits=[(sTE, te_guard)] + list(extra), inc=sGD, amt=16)
                gd_half_a[(th, e_, kind, hf)] = sGD.n
                S.add("gpsimd", lambda e: f(e, sub=1),
                      waits=[(sTE, te_guard2)], inc=sGD, amt=16)
            else:
                S.add("gpsimd", f,
                      waits=[(sTE, sTE.n if te_guard is None else te_guard)] + list(extra),
                      inc=sGD, amt=16)
            gd_half[(th, e_, kind, hf)] = sGD.n


        ffn_w_load(0, 0, 1, 0, te_guard=P2_TE, slot=0)
        ffn_w_load(0, 0, 2, 0, te_guard=P2_TE, slot=2)
        wo_last_te = 0
        trz_act = {}
        xu_spill_sd = {}
        zp_te = {}
        w_done_dve = {}
        zp_act = {}

        def stats_fn(buf, so, stt):
            def f(e, buf=buf, so=so, stt=stt):
                e.drain()
                e.bn_stats(stt[:, so:so + 6], buf[:, 0:512])
                e.bn_stats(stt[:, so + 6:so + 12], buf[:, 512:1024])
                e.drain()
                return e.bn_aggr(stt[:, so + 12:so + 14], stt[:, so:so + 12])
            return f

        def rsq_fn(so, stt, out=None, src=None, short=False):
            def f(e, so=so, stt=stt, out=out, src=src, short=short):
                e.activation(stt[:, so + 14:so + 15], stt[:, so + 13:so + 14], AF.Ln,
                             bias=epsv[:, 0:1], scale=1.0)
                e.drain()
                r = e.activation(stt[:, so + 15:so + 16], stt[:, so + 14:so + 15], AF.Exp,
                                 bias=0.0, scale=-0.5)
                if short:
                    return r
                e.drain()
                e.activation(stt[:, so + 16:so + 17], stt[:, so + 12:so + 13], AF.Copy,
                             bias=0.0, scale=-1.0)
                e.drain()
                r = e.activation(stt[:, so + 17:so + 18], stt[:, so + 16:so + 17], AF.Copy,
                                 bias=0.0, scale=stt[:, so + 15:so + 16])
                if out is not None:
                    e.drain()
                    r = e.activation(out, src, AF.Identity, bias=stt[:, so + 17:so + 18],
                                     scale=stt[:, so + 15:so + 16])
                return r
            return f

        x0_sd = {}
        d_x1_d = {}
        x1sp_sd = {}
        usp_sd = {}
        a_ev_d = {}
        r1_d = {}
        r2u_d = {}

        def p3_A1(tc):
            sl3 = (tc % 3) * 1024
            x0_sd[tc] = S.add("sync", lambda e, tc=tc, sl3=sl3: e.dma_start(
                out=x0_t[:, sl3:sl3 + 1024], in_=x_in[tc * 128:(tc + 1) * 128, :]),
                waits=[(sTE, P2_TE), (sDVE, d_x1_d.get(tc - 3, 0))], inc=sSD, amt=16)
            bids = [("pb", 0), ("pb", 1)]
            for cc in range(CCH):
                lw = TrAttn[:, cc * T + tc * 128: cc * T + tc * 128 + 128]
                S.add("tensor", lambda e, lw=lw: e.ldweights(lw),
                      waits=[(sDVE, P2_DVE), (sGD, GD_WO1)] +
                            (bank_w(bids[0]) + bank_w(bids[1]) if cc == 0 else []))
                for cs in range(2):
                    rhs = wsl[cs][:, cc * 512: cc * 512 + 512]
                    te_mm = S.add("tensor", lambda e, b=PB[cs], lw=lw, rhs=rhs, cc=cc:
                                  e.matmul(b[:, 0:512], lw, rhs,
                                           start=(cc == 0), stop=(cc == CCH - 1)), inc=sTE)
            wr[bids[0]] = (sTE, te_mm)
            wr[bids[1]] = (sTE, te_mm)
            x1b = x1_t[:, sl3:sl3 + 1024]

            def wo_ev(e, x1b=x1b):
                e.drain()
                e.copy(x1b[:, 0:512], PB[0][:, 0:512])
                e.drain()
                return e.copy(x1b[:, 512:1024], PB[1][:, 0:512])
            a_ev = S.add("scalar", wo_ev,
                         waits=[(sTE, te_mm), (sSD, x1sp_sd.get(tc - 3, 0))], inc=sACT)
            a_ev_d[tc] = a_ev
            rd[bids[0]] = (sACT, a_ev)
            rd[bids[1]] = (sACT, a_ev)
            return te_mm

        def p3_A2(tc):
            par = tc % 2
            sl3 = (tc % 3) * 1024
            x1b = x1_t[:, sl3:sl3 + 1024]
            st = st3[:, par * 64: par * 64 + 64]
            d_s1 = S.add("vector", stats_fn(x1b, 0, st), waits=[(sACT, a_ev_d[tc])], inc=sDVE)
            r1_d[tc] = S.add("scalar", rsq_fn(0, st, short=True), waits=[(sDVE, d_s1)], inc=sACT)

        def p3_A3(tc):
            par = tc % 2
            sl3 = (tc % 3) * 1024
            x1b = x1_t[:, sl3:sl3 + 1024]
            x0v = x0_t[:, sl3:sl3 + 1024]
            ub = u_t[:, (tc % 4) * 1024:(tc % 4) * 1024 + 1024]
            st = st3[:, par * 64: par * 64 + 64]

            def x1_fn(e, x1b=x1b, x0v=x0v, st=st):
                e.drain()
                e.tensor_scalar(x1b, x1b, st[:, 12:13], st[:, 15:16], ALU.subtract, ALU.mult)
                e.drain()
                return e.tensor_add(x1b, x1b, x0v)
            d_x1 = S.add("vector", x1_fn, waits=[(sACT, r1_d[tc]), (sSD, x0_sd[tc])], inc=sDVE)
            d_x1_d[tc] = d_x1
            x1sp_sd[tc] = S.add("sync", lambda e, x1b=x1b, tc=tc: e.dma_start(
                out=xuscr[tc * 128:(tc + 1) * 128, 1024:2048], in_=x1b),
                waits=[(sDVE, d_x1)], inc=sSD, amt=16)
            d_s2 = S.add("vector", stats_fn(x1b, 18, st), inc=sDVE)
            r2u_d[tc] = S.add("scalar", rsq_fn(18, st, out=ub, src=x1b),
                              waits=[(sDVE, d_s2), (sSD, usp_sd.get(tc - 3, 0))], inc=sACT)

        def p3_A4(tc):
            par = tc % 2
            ub = u_t[:, (tc % 4) * 1024:(tc % 4) * 1024 + 1024]
            st = st3[:, par * 64: par * 64 + 64]
            d_s3 = S.add("vector", stats_fn(ub, 36, st), waits=[(sACT, r2u_d[tc])], inc=sDVE)
            zp = zp_t[:, par * 1024: par * 1024 + 1024]
            a_r3 = S.add("scalar", rsq_fn(36, st, out=zp, src=ub),
                         waits=[(sDVE, d_s3), (sTE, zp_te.get(tc - 2, 0))], inc=sACT)
            zp_act[tc] = a_r3
            usp_sd[tc] = S.add("sync", lambda e, tc=tc, ub=ub: e.dma_start(
                out=xuscr[tc * 128:(tc + 1) * 128, 0:1024], in_=ub),
                waits=[(sACT, r2u_d[tc])], inc=sSD, amt=16)
            xu_spill_sd[tc] = usp_sd[tc]

        def p3_B1(tc):
            par = tc % 2
            zp = zp_t[:, par * 1024: par * 1024 + 1024]
            bid = ("pt", tc % 2)
            pt = PT[tc % 2]
            for cc in range(CCH):
                te_tr = S.add("tensor", lambda e, zp=zp, cc=cc, pt=pt: e.transpose(
                    pt[:, cc * 128:(cc + 1) * 128], zp[:, cc * 128:(cc + 1) * 128],
                    ident_s[:, :]),
                    waits=[(sACT, zp_act[tc])] + (bank_w(bid) if cc == 0 else []), inc=sTE)
            wr[bid] = (sTE, sTE.n)
            for cc in range(CCH):
                dst = Trz[:, cc * T + tc * 128: cc * T + tc * 128 + 128]
                a_tz = S.add("scalar", lambda e, pt=pt, dst=dst, cc=cc: e.activation(
                    dst, pt[:, cc * 128:(cc + 1) * 128], AF.Identity,
                    bias=lgb_s[:, CCH + cc:CCH + cc + 1],
                    scale=lgb_s[:, cc:cc + 1]),
                    waits=[(sTE, sTE.n), (sTE, P2_TE), (sSD, SD_CONST)], inc=sACT)
            rd[bid] = (sACT, sACT.n)
            zp_te[tc] = sTE.n
            trz_act[tc] = sACT.n

        def p3_B23(tc):
            par = tc % 2
            lg = lgS[:, par * 64: par * 64 + 64]
            lg_bid = ("pb", 2)
            for cc in range(CCH):
                lw = Trz[:, cc * T + tc * 128: cc * T + tc * 128 + 128]
                S.add("tensor", lambda e, lw=lw: e.ldweights(lw),
                      waits=[(sACT, trz_act[tc]), (sGD, GD_SMALL)] +
                            (bank_w(lg_bid) if cc == 0 else []))
                te_lg = S.add("tensor", lambda e, lw=lw, cc=cc:
                              e.matmul(PB[2][:, 0:E], lw, rW_s[:, cc * E:(cc + 1) * E],
                                       start=(cc == 0), stop=(cc == CCH - 1)), inc=sTE)
            wr[lg_bid] = (sTE, te_lg)

            def rt1(e, lg=lg):
                e.drain()
                e.tensor_add(lg[:, 0:8], PB[2][:, 0:E], rbc_s[:, :])
                e.drain()
                return e.reduce_max(lg[:, 8:9], lg[:, 0:8], axis=AX, negate=True)
            d_r1 = S.add("vector", rt1, waits=[(sTE, te_lg), (sSD, SD_CONST)], inc=sDVE)
            rd[lg_bid] = (sDVE, d_r1)
            a_re_d[tc] = S.add("scalar", lambda e, lg=lg: e.activation(
                lg[:, 9:17], lg[:, 0:8], AF.Exp, bias=lg[:, 8:9], scale=1.0),
                waits=[(sDVE, d_r1)], inc=sACT)

        def p3_B3(tc):
            par = tc % 2
            lg = lgS[:, par * 64: par * 64 + 64]
            wv = wS[:, tc * E:(tc + 1) * E]
            wb = wbf_t[:, (tc % 16) * 8:(tc % 16) * 8 + 8]

            def rt2(e, wv=wv, wb=wb, lg=lg):
                ex = lg[:, 9:17]
                e.drain()
                e.reduce_max(lg[:, 17:18], ex, axis=AX)
                e.drain()
                e.tensor_scalar(lg[:, 18:26], ex, lg[:, 17:18], None, ALU.is_equal)
                e.drain()
                e.tensor_tensor(lg[:, 26:34], ex, lg[:, 18:26], ALU.mult)
                e.drain()
                e.tensor_sub(lg[:, 34:42], ex, lg[:, 26:34])
                e.drain()
                e.reduce_max(lg[:, 42:43], lg[:, 34:42], axis=AX)
                e.drain()
                e.tensor_scalar(lg[:, 43:51], ex, lg[:, 42:43], None, ALU.is_ge)
                e.drain()
                e.tensor_tensor(lg[:, 51:59], ex, lg[:, 43:51], ALU.mult)
                e.drain()
                e.tensor_add(lg[:, 59:60], lg[:, 17:18], lg[:, 42:43])
                e.drain()
                e.tensor_scalar(lg[:, 60:61], lg[:, 59:60], 1e-8, None, ALU.max)
                e.drain()
                e.reciprocal(lg[:, 61:62], lg[:, 60:61])
                e.drain()
                e.tensor_scalar(wv, lg[:, 51:59], lg[:, 61:62], None, ALU.mult)
                e.drain()
                return e.tensor_copy(wb, wv)
            d_w = S.add("vector", rt2, waits=[(sACT, a_re_d[tc])], inc=sDVE)
            w_done_dve[tc] = d_w

        a_re_d = {}
        for i in range(TC + 8):
            if 0 <= i - 7 < TC:
                p3_B3(i - 7)
            if 0 <= i - 6 < TC:
                p3_B23(i - 6)
            if 0 <= i - 5 < TC:
                p3_B1(i - 5)
            if i < TC:
                wo_last_te = p3_A1(i)
            if 0 <= i - 1 < TC:
                p3_A2(i - 1)
            if 0 <= i - 2 < TC:
                p3_A3(i - 2)
            if 0 <= i - 3 < TC:
                p3_A4(i - 3)
        ffn_w_load(0, 0, 1, 1, te_guard=P2_TE, slot=1,
                   extra=[(sSD, usp_sd[TC - 1]), (sDVE, sDVE.n)])
        P3_TE = sTE.n
        P3_DVE = sDVE.n
        P3_SD = sSD.n

        acc_reload_sd = {}
        acc_b2d = {}
        final_read_dve = {}
        u_rel_guard = []

        bidw = ("pt", 0)
        W8 = TC * 8
        te_wt = S.add("tensor", lambda e: e.transpose(
            PT[0][0:W8, 0:128], wbf_t[:, 0:W8], ident_s[:, :]),
            waits=[(sDVE, w_done_dve[TC - 1]), (sTE, P3_TE)] + bank_w(bidw), inc=sTE)
        wr[bidw] = (sTE, te_wt)
        a_wt = S.add("scalar", lambda e: e.copy(wtscr[0:W8, 0:128], PT[0][0:W8, 0:128]),
                     waits=[(sTE, te_wt)], inc=sACT)
        rd[bidw] = (sACT, a_wt)
        wt_sd = {}

        def emit_acc_chunk(th, i):
            tc = th * THC + i
            wl = [(sSD, P3_SD), (sDVE, P3_DVE)]
            if i < 4:
                wl.append((sTE, P3_TE))
            if th == 1:
                wl.append((sDVE, final_read_dve.get((0, i), 0)))
            acc_reload_sd[(th, i)] = S.add("sync", lambda e, tc=tc, i=i: e.dma_start(
                out=acc_ch(i), in_=xuscr[tc * 128:(tc + 1) * 128, 0:1024]), waits=wl,
                inc=sSD, amt=16)
            wt_sd[tc] = S.add("sync", lambda e, tc=tc: e.dma_start(
                out=wT_s[0:8, tc * 128:(tc + 1) * 128], in_=wtscr[tc * 8:tc * 8 + 8, 0:128]),
                waits=[(sACT, a_wt)], inc=sSD, amt=16)
            b2_bids = [("pb", 4), ("pb", 5)]
            S.add("tensor", lambda e, tc=tc: e.ldweights(wT_s[0:8, tc * 128:(tc + 1) * 128]),
                  waits=[(sSD, wt_sd[tc]), (sGD, GD_SMALL)] +
                        bank_w(b2_bids[0]) + bank_w(b2_bids[1]))
            for hs in range(2):
                te_b2 = S.add("tensor", lambda e, tc=tc, hs=hs:
                              e.matmul(PB[4 + hs][:, 0:512],
                                       wT_s[0:8, tc * 128:(tc + 1) * 128],
                                       b2_s[0:8, hs * 512:(hs + 1) * 512],
                                       start=True, stop=True), inc=sTE)
                wr[b2_bids[hs]] = (sTE, te_b2)

            def b2add(e, i=i):
                ac = acc_ch(i)
                e.drain()
                e.tensor_add(ac[:, 0:512], ac[:, 0:512], PB[4][:, 0:512])
                e.drain()
                return e.tensor_add(ac[:, 512:1024], ac[:, 512:1024], PB[5][:, 0:512])
            d_b2 = S.add("vector", b2add,
                         waits=[(sTE, te_b2), (sSD, acc_reload_sd[(th, i)])], inc=sDVE)
            rd[b2_bids[0]] = (sDVE, d_b2)
            rd[b2_bids[1]] = (sDVE, d_b2)
            acc_b2_dve = d_b2
            acc_reload_sd[(th, i)] = acc_reload_sd[(th, i)]
            acc_b2d[(th, i)] = acc_b2_dve

        gelu_act = {}
        final_store_sd = {}

        def emit_final(th, i):
            tc = th * THC + i
            par = i % 2
            so = par * 32
            x1r = x1r_t[:, par * 1024: par * 1024 + 1024]
            ob = outb_t[:, par * 1024: par * 1024 + 1024]
            if par == 0:
                wl = [(sDVE, final_read_dve.get((th, i - 2), 0)),
                      (sDVE, final_read_dve.get((th, i - 1), 0)),
                      (sDVE, final_read_dve.get((th - 1, THC - 2), 0)),
                      (sDVE, final_read_dve.get((th - 1, THC - 1), 0)),
                      (sSD, xu_spill_sd[TC - 1]), (sTE, zp_te.get(TC - 1, 0))]
                S.add("sync", lambda e, tc=tc: e.dma_start(
                    out=x1r_t[:, 0:2048].rearrange("p (two q) -> p two q", two=2),
                    in_=xuscr[tc * 128:(tc + 2) * 128, 1024:2048].rearrange(
                        "(two p) c -> p two c", p=128)), waits=wl, inc=sSD, amt=16)
            sd_r = sSD.n

            def fstat(e, i=i, so=so):
                ac = acc_ch(i)
                e.drain()
                e.bn_stats(stF[:, so:so + 6], ac[:, 0:512])
                e.bn_stats(stF[:, so + 6:so + 12], ac[:, 512:1024])
                e.drain()
                return e.bn_aggr(stF[:, so + 12:so + 14], stF[:, so:so + 12])
            d_fs = S.add("vector", fstat, inc=sDVE)

            def frsq(e, so=so):
                e.activation(stF[:, so + 14:so + 15], stF[:, so + 13:so + 14], AF.Ln,
                             bias=epsv[:, 0:1], scale=1.0)
                e.drain()
                e.activation(stF[:, so + 15:so + 16], stF[:, so + 14:so + 15], AF.Exp,
                             bias=0.0, scale=-0.5)
                e.drain()
                e.activation(stF[:, so + 16:so + 17], stF[:, so + 12:so + 13], AF.Copy,
                             bias=0.0, scale=-1.0)
                e.drain()
                return e.activation(stF[:, so + 17:so + 18], stF[:, so + 16:so + 17], AF.Copy,
                                    bias=0.0, scale=stF[:, so + 15:so + 16])
            a_fr = S.add("scalar", frsq, waits=[(sDVE, d_fs)], inc=sACT)

            def fout(e, i=i, x1r=x1r, ob=ob, so=so):
                ac = acc_ch(i)
                e.drain()
                e.scalar_tensor_tensor(ob, ac, stF[:, so + 15:so + 16], x1r, ALU.mult, ALU.add)
                e.drain()
                return e.tensor_scalar(ob, ob, stF[:, so + 17:so + 18], None, ALU.add)
            d_fo = S.add("vector", fout,
                         waits=[(sACT, a_fr), (sSD, sd_r),
                                (sSD, final_store_sd.get((th, i - 2), 0))], inc=sDVE)
            final_read_dve[(th, i)] = d_fo
            if par == 1:
                final_store_sd[(th, i - 1)] = final_store_sd[(th, i)] = S.add(
                    "sync", lambda e, tc=tc: e.dma_start(
                        out=y_out[(tc - 1) * 128:(tc + 1) * 128, :].rearrange(
                            "(two p) c -> p two c", p=128),
                        in_=outb_t[:, 0:2048].rearrange("p (two q) -> p two q", two=2)),
                    waits=[(sDVE, final_read_dve[(th, i - 1)]), (sDVE, d_fo)],
                    inc=sSD, amt=16)

        for th in range(2):
            for e_ in range(E):
                nth, ne = (th, e_ + 1) if e_ < E - 1 else (th + 1, 0)
                HL = T // 2
                n_tsl = max(1, HL // 512)
                tw = min(512, HL)
                w1_fc_te = {}
                for fc in range(FCH):
                    hf = fc // 8
                    k1 = (th, e_, 1, hf)
                    gd = gd_half_a[k1] if (fc % 8 < 4 and k1 in gd_half_a) else gd_half[k1]
                    slot = half_slot[k1]
                    bk = [2 * (fc % 2), 2 * (fc % 2) + 1]
                    w1_bids = [("pb", bk[0]), ("pb", bk[1])]
                    for cc in range(CCH):
                        lw = fsl[slot][:, cc * 1024 + (fc % 8) * 128:
                                       cc * 1024 + (fc % 8) * 128 + 128]
                        S.add("tensor", lambda e, lw=lw: e.ldweights(lw),
                              waits=[(sGD, gd), (sACT, trz_act.get(th * THC + THC - 1, 0))] +
                                    (bank_w(w1_bids[0]) + bank_w(w1_bids[1]) if cc == 0 else []))
                        for tsl in range(n_tsl):
                            rhs = Trz[:, cc * T + th * HL + tsl * tw:
                                      cc * T + th * HL + tsl * tw + tw]
                            te_mm = S.add("tensor", lambda e, b=PB[bk[tsl]], lw=lw, rhs=rhs, cc=cc,
                                          tw=tw: e.matmul(b[:, 0:tw], lw, rhs,
                                                          start=(cc == 0), stop=(cc == CCH - 1)),
                                          inc=sTE)
                    wr[w1_bids[0]] = (sTE, te_mm)
                    wr[w1_bids[1]] = (sTE, te_mm)
                    for tsl in range(n_tsl):
                        dst = hidT[:, fc * HL + tsl * tw: fc * HL + tsl * tw + tw]
                        a_g = S.add("scalar", lambda e, b=PB[bk[tsl]], dst=dst, e_=e_, fc=fc, tw=tw:
                                    e.activation(dst, b[:, 0:tw], AF.Gelu,
                                                 bias=b1_s[:, e_ * FCH + fc: e_ * FCH + fc + 1],
                                                 scale=1.0),
                                    waits=[(sTE, te_mm), (sTE, wo_last_te),
                                           (sSD, SD_CONST)], inc=sACT)
                        rd[w1_bids[tsl]] = (sACT, a_g)
                    gelu_act[fc] = a_g
                    w1_fc_te[fc] = sTE.n
                    if e_ == 0 and fc % 2 == 1 and fc // 2 < THC:
                        emit_acc_chunk(th, fc // 2)
                    if fc == 7:
                        ffn_w_load(th, e_, 2, 1)
                ffn_w_load(nth, ne, 1, 0, te_guard=w1_fc_te[11], te_guard2=w1_fc_te[15])
                for t8 in range(THC):
                    bk = [2 * (t8 % 2), 2 * (t8 % 2) + 1]
                    w2_bids = [("pb", bk[0]), ("pb", bk[1])]
                    for fc in range(FCH):
                        hf = fc // 8
                        gd = gd_half[(th, e_, 2, hf)]
                        slot = half_slot[(th, e_, 2, hf)]
                        lw = hidT[:, fc * (T // 2) + t8 * 128: fc * (T // 2) + t8 * 128 + 128]
                        S.add("tensor", lambda e, lw=lw: e.ldweights(lw),
                              waits=[(sGD, gd), (sACT, gelu_act[fc])] +
                                    (bank_w(w2_bids[0]) + bank_w(w2_bids[1]) if fc == 0 else []))
                        for hs in range(2):
                            rhs = fsl[slot][:, (fc % 8) * 1024 + hs * 512:
                                            (fc % 8) * 1024 + hs * 512 + 512]
                            te_mm = S.add("tensor", lambda e, b=PB[bk[hs]], lw=lw, rhs=rhs, fc=fc:
                                          e.matmul(b[:, 0:512], lw, rhs,
                                                   start=(fc == 0), stop=(fc == FCH - 1)),
                                          inc=sTE)
                    wr[w2_bids[0]] = (sTE, te_mm)
                    wr[w2_bids[1]] = (sTE, te_mm)
                    tc = th * THC + t8
                    wcol = wS[:, tc * E + e_: tc * E + e_ + 1]

                    def stt_fn(e, t8=t8, wcol=wcol, bk=bk):
                        ac = acc_ch(t8)
                        e.drain()
                        e.scalar_tensor_tensor(ac[:, 0:512], PB[bk[0]][:, 0:512], wcol,
                                               ac[:, 0:512], ALU.mult, ALU.add)
                        e.drain()
                        return e.scalar_tensor_tensor(ac[:, 512:1024], PB[bk[1]][:, 0:512], wcol,
                                                      ac[:, 512:1024], ALU.mult, ALU.add)
                    d_st = S.add("vector", stt_fn,
                                 waits=[(sTE, te_mm), (sSD, acc_reload_sd[(th, t8)]),
                                        (sDVE, max(acc_b2d[(th, t8)],
                                                   w_done_dve.get(TC - 1, 0)))], inc=sDVE)
                    rd[w2_bids[0]] = (sDVE, d_st)
                    rd[w2_bids[1]] = (sDVE, d_st)
                    if e_ == E - 1 and t8 % 2 == 1:
                        emit_final(th, t8 - 1)
                        emit_final(th, t8)
                ffn_w_load(nth, ne, 1, 1)
                ffn_w_load(nth, ne, 2, 0)

        SD_ALL = sSD.n
        GD_ALL = sGD.n
        S.add("sync", lambda e: e.nop(), waits=[(sSD, SD_ALL), (sGD, GD_ALL)])

        S.emit(block)
    return nc


def _host_consts(T):
    import ml_dtypes
    BF = ml_dtypes.bfloat16
    TC = T // 128
    sl = np.asarray(_slopes(NH), dtype=np.float32)
    amap, ncol = _acol_map(T)
    karr = np.arange(BLK, dtype=np.float32) - np.float32(BLK - 1)
    bias_ref = (sl[:, None] * karr[None, :]).astype(np.float32)
    alibi = np.zeros((128, ncol), np.float32)
    for (h, qb, kb), col in amap.items():
        qw = QW[h]
        qcen = qb * qw + qw // 2
        ks = kb * 128
        alibi[:, col] = (bias_ref[h, ks:ks + 128].astype(np.float64)
                         - np.float64(bias_ref[h, qcen])).astype(np.float32)
    masks = np.zeros((128, 4 * 512), np.float32)
    p = np.arange(128, dtype=np.float32)
    j = np.arange(512, dtype=np.float32)
    for oi in range(4):
        masks[:, oi * 512:(oi + 1) * 512] = (oi * 128 + p[:, None]) <= j[None, :]
    ident = np.eye(128, dtype=np.float32)
    bcon = np.zeros((128, 128), np.float32)
    bcon[32, :] = 1.0
    bcon[64, :] = 1.0
    return {
        "alibi": alibi,
        "masks": masks.astype(BF),
        "ident": ident.astype(BF),
        "bcon": bcon.astype(BF),
    }


def kernel(L0_attn_x, L0_attn_Wq, L0_attn_Wk, L0_attn_Wv, L0_attn_Wo,
           L0_ffn_ln_g, L0_ffn_ln_b, L0_ffn_rW, L0_ffn_rb, L0_ffn_W1,
           L0_ffn_b1, L0_ffn_W2, L0_ffn_b2):
    T = 2048
    x = np.ascontiguousarray(np.asarray(L0_attn_x, dtype=np.float32))
    consts = _host_consts(T)
    lg = np.asarray(L0_ffn_ln_g, np.float32).reshape(CCH, 128).T
    lb = np.asarray(L0_ffn_ln_b, np.float32).reshape(CCH, 128).T
    lgb = np.concatenate([lg, lb], axis=1)
    b1rr = (np.asarray(L0_ffn_b1, np.float32).reshape(E, FCH, 128)
            .transpose(2, 0, 1).reshape(128, E * FCH))
    rbc = np.broadcast_to(np.asarray(L0_ffn_rb, np.float32), (128, E))
    cf32 = np.ascontiguousarray(np.concatenate(
        [lgb, b1rr, rbc, consts["alibi"]], axis=1))
    cb16 = np.ascontiguousarray(np.concatenate(
        [consts["masks"], consts["bcon"]], axis=1))
    common = {
        "Wq": np.ascontiguousarray(L0_attn_Wq, np.float32),
        "Wk": np.ascontiguousarray(L0_attn_Wk, np.float32),
        "Wv": np.ascontiguousarray(L0_attn_Wv, np.float32),
        "Wo": np.ascontiguousarray(L0_attn_Wo, np.float32),
        "W1": np.ascontiguousarray(L0_ffn_W1, np.float32),
        "W2": np.ascontiguousarray(L0_ffn_W2, np.float32),
        "b2": np.ascontiguousarray(L0_ffn_b2, np.float32),
        "rW": np.ascontiguousarray(L0_ffn_rW, np.float32),
        "cf32": cf32, "cb16": cb16, "ident": consts["ident"],
    }
    nc = build_core_program(T)
    in_maps = [dict(common, x=np.ascontiguousarray(x[c])) for c in range(8)]
    res = run_bass_kernel_spmd(nc, in_maps, core_ids=list(range(8)))
    return np.stack([np.asarray(res.results[c]["y"]) for c in range(8)], axis=0)


# version 24 (final)
# speedup vs baseline: 1.0450x; 1.0042x over previous
import math
import contextlib
import numpy as np
import concourse.bass as bass
import concourse.mybir as mybir
from concourse.bass_utils import run_bass_kernel_spmd

F32 = mybir.dt.float32
BF16 = mybir.dt.bfloat16
U8 = mybir.dt.uint8
AF = mybir.ActivationFunctionType
ALU = mybir.AluOpType
AX = mybir.AxisListType.X

C = 1024
NH = 16
HD = 64
E = 8
HF = 2048
EPS = 1e-5
BLK = 2048
CCH = C // 128
FCH = HF // 128
RW = 160

QW = [128, 128, 256, 256] + [512] * 12


def _acol_map(T):
    m = {}
    col = 0
    for h in range(NH):
        qw = QW[h]
        for qb in range(T // qw):
            qe = qb * qw + qw - 1
            for kb in range(qe // 128 + 1):
                m[(h, qb, kb)] = col
                col += 1
    return m, col

AR_A = 0
AR_B = 32768
AR_C = 65536
AR_D = 98304
AR_E = 131072
AR_F = 147456
AR_TOTAL = 163840


def _slopes(n):
    def pow2(m):
        start = 2 ** (-(2 ** (-(math.log2(m) - 3))))
        return [start * start ** i for i in range(m)]
    if math.log2(n).is_integer():
        return pow2(n)
    c = 2 ** math.floor(math.log2(n))
    return pow2(c) + _slopes(2 * c)[0::2][:n - c]


class Sem:
    def __init__(self, nc, es, name):
        self.h = es.enter_context(nc.semaphore(name))
        self.n = 0


class Sched:
    ENGS = ("tensor", "scalar", "vector", "gpsimd", "sync")

    def __init__(self):
        self.ops = {e: [] for e in self.ENGS}
        self.last_wait = {e: {} for e in self.ENGS}

    def add(self, eng, fn, waits=(), inc=None, amt=1):
        if inc is not None and eng in ("gpsimd", "sync") and inc.n > 0:
            waits = list(waits) + [(inc, inc.n)]
        wl = []
        lw = self.last_wait[eng]
        for (s, v) in waits:
            if v <= 0 or lw.get(id(s), 0) >= v:
                continue
            lw[id(s)] = v
            wl.append((s, v))
        self.ops[eng].append((wl, fn, inc, amt))
        if inc is not None:
            inc.n += amt
            return inc.n
        return None

    def emit(self, block):
        for name in self.ENGS:
            oplist = self.ops[name]
            if not oplist:
                continue
            dec = getattr(block, name)

            def section(eng, _ops=oplist):
                for (wl, fn, inc, amt) in _ops:
                    for (s, v) in wl:
                        eng.wait_ge(s.h, v)
                    ins = fn(eng)
                    if inc is not None:
                        ins.then_inc(inc.h, amt)
            dec(section)


def build_core_program(T=2048):
    assert T % 512 == 0
    TC = T // 128
    QC = T // 512
    THC = TC // 2
    NPAIR_D = min(102, TC * 8)
    nc = bass.Bass("TRN2", target_bir_lowering=False)

    x_in = nc.dram_tensor("x", [T, C], F32, kind="ExternalInput").ap()
    Wq = nc.dram_tensor("Wq", [C, C], F32, kind="ExternalInput").ap()
    Wk = nc.dram_tensor("Wk", [C, C], F32, kind="ExternalInput").ap()
    Wv = nc.dram_tensor("Wv", [C, C], F32, kind="ExternalInput").ap()
    Wo = nc.dram_tensor("Wo", [C, C], F32, kind="ExternalInput").ap()
    W1 = nc.dram_tensor("W1", [E, C, HF], F32, kind="ExternalInput").ap()
    W2 = nc.dram_tensor("W2", [E, HF, C], F32, kind="ExternalInput").ap()
    b2v = nc.dram_tensor("b2", [E, C], F32, kind="ExternalInput").ap()
    rWv = nc.dram_tensor("rW", [C, E], F32, kind="ExternalInput").ap()
    amap, acols = _acol_map(T)
    CF32W = 2 * CCH + E * FCH + E + acols
    CB16W = 4 * 512 + 128
    cf32_in = nc.dram_tensor("cf32", [128, CF32W], F32, kind="ExternalInput").ap()
    cb16_in = nc.dram_tensor("cb16", [128, CB16W], BF16, kind="ExternalInput").ap()
    ident_in = nc.dram_tensor("ident", [128, 128], BF16, kind="ExternalInput").ap()
    y_out = nc.dram_tensor("y", [T, C], F32, kind="ExternalOutput").ap()
    xuscr = nc.dram_tensor("xuscr", [T, 2 * C], F32, kind="Internal").ap()

    es = contextlib.ExitStack()
    with es:
        arena = es.enter_context(nc.sbuf_tensor("arena", [128, AR_TOTAL], U8))
        ar_base = nc.lookup_mloc(arena).addr

        def vt(name, shape, dtype, byte_off):
            return nc.alloc_sbuf_tensor_at(name, shape, dtype, offset=ar_base + byte_off)

        Trz1 = vt("Trz1", [128, CCH * T], BF16, AR_A)
        TrAttn = vt("TrAttn", [128, CCH * T], BF16, AR_A)
        hidT = vt("hidT", [128, FCH * (T // 2)], BF16, AR_A)
        qT = vt("qT", [128, CCH * T], BF16, AR_B)
        fsl0 = vt("fsl0", [128, 8192], BF16, AR_B + 16384)
        fsl1 = vt("fsl1", [128, 8192], BF16, AR_B)
        kT = vt("kT", [128, CCH * T], BF16, AR_C)
        fsl2 = vt("fsl2", [128, 8192], BF16, AR_C)
        x0_t = vt("x0t", [128, 3072], F32, AR_C + 16384)
        u_t = vt("ut", [128, 4096], F32, AR_B)
        x1_t = vt("x1t", [128, 3072], F32, AR_F)
        accA = vt("accA", [128, 4096], F32, AR_E)
        accB = vt("accB", [128, 4096], F32, AR_C + 16384)
        vgD = vt("vgD", [128, NPAIR_D * RW], BF16, AR_D)
        Trz = vt("Trz", [128, CCH * T], BF16, AR_D)
        wsl = [vt("wsl0", [128, 4096], BF16, AR_E),
               vt("wsl1", [128, 4096], BF16, AR_E + 8192)]
        xpp = vt("xpp", [128, 4096], F32, AR_F)
        z1pp = vt("z1pp", [128, 3072], BF16, AR_C)
        slabs = vt("slabs", [128, 2048], BF16, AR_F)
        vgF = vt("vgF", [128, max(0, TC * 8 - NPAIR_D) * RW + 16], BF16, AR_F + 4096)
        zp_t = vt("zp", [128, 2048], BF16, AR_F + 12288)
        x1r_t = vt("x1r", [128, 2048], F32, AR_F)
        outb_t = vt("outb", [128, 2048], F32, AR_F + 8192)

        cfa = es.enter_context(nc.sbuf_tensor("cfa", [128, CF32W], F32))
        cfa_base = nc.lookup_mloc(cfa).addr
        lgb_s = nc.alloc_sbuf_tensor_at("lgb_s", [128, 2 * CCH], F32, offset=cfa_base)
        b1_s = nc.alloc_sbuf_tensor_at("b1_s", [128, E * FCH], F32,
                                       offset=cfa_base + 8 * CCH)
        rbc_s = nc.alloc_sbuf_tensor_at("rbc_s", [128, E], F32,
                                        offset=cfa_base + 8 * CCH + 4 * E * FCH)
        alibi_s = nc.alloc_sbuf_tensor_at("alibi_s", [128, acols], F32,
                                          offset=cfa_base + 8 * CCH + 4 * E * FCH + 4 * E)
        cba = es.enter_context(nc.sbuf_tensor("cba", [128, CB16W], BF16))
        cba_base = nc.lookup_mloc(cba).addr
        masks_s = nc.alloc_sbuf_tensor_at("masks_s", [128, 4 * 512], BF16, offset=cba_base)
        bcon_s = nc.alloc_sbuf_tensor_at("bcon_s", [128, 128], BF16,
                                         offset=cba_base + 2 * 4 * 512)
        ident_s = es.enter_context(nc.sbuf_tensor("ident_s", [128, 128], BF16))
        rW_s = es.enter_context(nc.sbuf_tensor("rW_s", [128, CCH * E], BF16))
        b2_s = es.enter_context(nc.sbuf_tensor("b2_s", [8, C], BF16))
        wT_s = es.enter_context(nc.sbuf_tensor("wT_s", [8, T], BF16))
        recb_s = es.enter_context(nc.sbuf_tensor("recb_s", [128, 2048], BF16))
        rtmp_s = es.enter_context(nc.sbuf_tensor("rtmp_s", [128, 512], F32))
        stg_s = es.enter_context(nc.sbuf_tensor("stg_s", [128, 2048], BF16))
        wS = es.enter_context(nc.sbuf_tensor("wS", [128, TC * E], F32))
        stP = es.enter_context(nc.sbuf_tensor("stP", [128, 128], F32))
        st3 = es.enter_context(nc.sbuf_tensor("st3", [128, 128], F32))
        lgS = es.enter_context(nc.sbuf_tensor("lgS", [128, 128], F32))
        stF = es.enter_context(nc.sbuf_tensor("stF", [128, 64], F32))
        epsv = es.enter_context(nc.sbuf_tensor("epsv", [128, 1], F32))
        wbf_t = es.enter_context(nc.sbuf_tensor("wbf_t", [128, 8 * 16], BF16))
        wtscr = es.enter_context(nc.sbuf_tensor("wtscr", [128, 128], BF16))

        PB = [es.enter_context(nc.psum_tensor("pb%d" % i, [128, 512], F32)) for i in range(6)]
        PT = [es.enter_context(nc.psum_tensor("pt%d" % i, [128, 1024], BF16)) for i in range(2)]

        sTE = Sem(nc, es, "sTE")
        sACT = Sem(nc, es, "sACT")
        sDVE = Sem(nc, es, "sDVE")
        sGD = Sem(nc, es, "sGD")
        sSD = Sem(nc, es, "sSD")

        S = Sched()
        block = es.enter_context(nc.Block())

        wr = {}
        rd = {}

        def bank_w(bid):
            return [rd[bid]] if bid in rd else []

        def bank_r(bid):
            return [wr[bid]] if bid in wr else []

        def vgp(pair):
            if pair < NPAIR_D:
                return vgD[:, pair * RW:(pair + 1) * RW]
            p = pair - NPAIR_D
            return vgF[:, p * RW:(p + 1) * RW]

        S.add("sync", lambda e: e.dma_start(out=ident_s[:, :], in_=ident_in[:, :]),
              inc=sSD, amt=16)
        SD_IDENT = sSD.n
        S.add("sync", lambda e: e.dma_start(
            out=xpp[:, 0:2048].rearrange("p (two q) -> p two q", two=2),
            in_=x_in[0:256, :].rearrange("(two p) c -> p two c", p=128)), inc=sSD, amt=16)
        SD_X0 = sSD.n
        S.add("gpsimd", lambda e: e.dma_start(
            out=rW_s[:, :].rearrange("p (cc e) -> p cc e", cc=CCH),
            in_=rWv.rearrange("(cc p) e -> p cc e", p=128)), inc=sGD, amt=16)
        S.add("gpsimd", lambda e: e.dma_start(out=b2_s[:, :], in_=b2v[:, :]), inc=sGD, amt=16)
        GD_SMALL = sGD.n
        S.add("vector", lambda e: e.memset(epsv[:, :], EPS), inc=sDVE)
        S.add("vector", lambda e: e.memset(recb_s[:, :], 0.0), inc=sDVE)
        S.add("vector", lambda e: e.memset(rtmp_s[:, :], 0.0), inc=sDVE)
        DVE_INIT = sDVE.n
        def vg_piece(p0, p1):
            def f(e, p0=p0, p1=p1):
                e.drain()
                e.memset(vgD[:, p0 * RW: p1 * RW], 0.0)
                e.drain()
                for pair in range(p0, p1):
                    e.memset(vgp(pair)[:, 64:65], 1.0)
                e.drain()
                return e.engine_nop()
            return f
        INIT_PIECES = [vg_piece(p, min(p + 13, NPAIR_D)) for p in range(0, NPAIR_D, 13)]

        def w_half_load(Wm, hf):
            def f(e, Wm=Wm, hf=hf, slot=wsl[hf % 2]):
                return e.dma_start(
                    out=slot[:, :].rearrange("p (cc d) -> p cc d", cc=CCH),
                    in_=Wm[:, hf * 512:(hf + 1) * 512].rearrange("(cc p) d -> p cc d", p=128))
            return f

        GD_W = {}
        S.add("gpsimd", w_half_load(Wq, 0), inc=sGD, amt=16); GD_W[(0, 0)] = sGD.n
        S.add("gpsimd", w_half_load(Wq, 1), inc=sGD, amt=16); GD_W[(0, 1)] = sGD.n

        p0_evac = {}

        def qk_part(wi, hf, tp, DST, dcs=(0, 1, 2, 3), defer_evac=False):
            gd = GD_W[(wi, hf)]
            deferred = []
            for dc in dcs:
                bids = [("pb", 2 * (dc % 2)), ("pb", 2 * (dc % 2) + 1)]
                for cc in range(CCH):
                    lw = wsl[hf % 2][:, cc * 512 + dc * 128: cc * 512 + dc * 128 + 128]
                    S.add("tensor", lambda e, lw=lw: e.ldweights(lw),
                          waits=[(sGD, gd), (sDVE, p0_evac[8 * tp + 7])])
                    for half in range(2):
                        ts4 = tp * 2 + half
                        rhs = Trz1[:, cc * T + ts4 * 512: cc * T + ts4 * 512 + 512]
                        te_mm = S.add("tensor", lambda e, b=PB[2 * (dc % 2) + half],
                                      lw=lw, rhs=rhs, cc=cc:
                                      e.matmul(b[:, 0:512], lw, rhs,
                                               start=(cc == 0), stop=(cc == CCH - 1)),
                                      waits=bank_w(bids[half]) if cc == 0 else [], inc=sTE)
                        if cc == CCH - 1:
                            wr[bids[half]] = (sTE, te_mm)

                def ev(dc=dc, bids=bids, hf=hf, tp=tp, wrv=None):
                    wrv = [wr[bids[0]], wr[bids[1]]]
                    for half in range(2):
                        ts4 = tp * 2 + half
                        d = DST[:, (hf * 4 + dc) * T + ts4 * 512:
                                (hf * 4 + dc) * T + ts4 * 512 + 512]
                        d_ev = S.add("vector", lambda e, b=PB[2 * (dc % 2) + half], d=d: (
                            e.drain(), e.tensor_copy(d, b[:, 0:512]))[-1],
                            waits=[wrv[half]], inc=sDVE)
                        rd[bids[half]] = (sDVE, d_ev)
                if defer_evac:
                    deferred.append(ev)
                else:
                    ev()
                    if INIT_PIECES:
                        S.add("vector", INIT_PIECES.pop(0), inc=sDVE)
            return deferred

        p0_norm_act = {}
        p0_tr_te = {}
        p0_sd = {}
        p0_pend = []
        for tc in range(TC):
            xb = xpp[:, (tc % 4) * 1024:(tc % 4) * 1024 + 1024]
            z1 = z1pp[:, (tc % 3) * 1024:(tc % 3) * 1024 + 1024]
            if tc == 0:
                p0_sd[0] = SD_X0
            elif tc % 2 == 0:
                s4 = (tc % 4) * 1024
                p0_sd[tc] = S.add("sync", lambda e, tc=tc, s4=s4: e.dma_start(
                    out=xpp[:, s4:s4 + 2048].rearrange("p (two q) -> p two q", two=2),
                    in_=x_in[tc * 128:(tc + 2) * 128, :].rearrange("(two p) c -> p two c", p=128)),
                    waits=[(sDVE, p0_norm_act.get(tc - 3, 0))], inc=sSD, amt=16)
            sd_x = p0_sd[tc - tc % 2]

            sp = stP[:, (tc % 2) * 64:(tc % 2) * 64 + 64]

            def stats0(e, xb=xb, sp=sp):
                e.drain()
                e.bn_stats(sp[:, 0:6], xb[:, 0:512])
                e.bn_stats(sp[:, 6:12], xb[:, 512:1024])
                e.drain()
                return e.bn_aggr(sp[:, 12:14], sp[:, 0:12])
            dve_st = S.add("vector", stats0,
                           waits=[(sSD, sd_x)], inc=sDVE)

            def rsq0(e, sp=sp):
                e.activation(sp[:, 14:15], sp[:, 13:14], AF.Ln, bias=epsv[:, 0:1], scale=1.0)
                e.drain()
                return e.activation(sp[:, 15:16], sp[:, 14:15], AF.Exp, bias=0.0, scale=-0.5)
            act_rq = S.add("scalar", rsq0, waits=[(sDVE, dve_st)], inc=sACT)

            def nrm0(e, sp=sp, z1=z1, xb=xb):
                e.drain()
                return e.tensor_scalar(z1, xb, sp[:, 12:13], sp[:, 15:16],
                                       ALU.subtract, ALU.mult)
            act_nm = S.add("vector", nrm0,
                           waits=[(sACT, act_rq), (sTE, p0_tr_te.get(tc - 3, 0))], inc=sDVE)
            p0_norm_act[tc] = act_nm

            def p0_tr(tc=tc, z1=z1):
                bid = ("pt", tc % 2)
                pt = PT[tc % 2]
                for cc in range(CCH):
                    te_tr = S.add("tensor", lambda e, z1=z1, cc=cc, pt=pt: e.transpose(
                        pt[:, cc * 128:(cc + 1) * 128], z1[:, cc * 128:(cc + 1) * 128],
                        ident_s[:, :]),
                        waits=[(sDVE, p0_norm_act[tc]), (sSD, SD_IDENT)] +
                              (bank_w(bid) if cc == 0 else []), inc=sTE)
                wr[bid] = (sTE, sTE.n)
                dst = Trz1[:, 0:CCH * T].rearrange(
                    "p (cc t) -> p cc t", cc=CCH)[:, :, tc * 128:(tc + 1) * 128]
                d_ev = S.add("vector", lambda e, pt=pt, dst=dst: (
                    e.drain(), e.tensor_copy(dst, pt[:, 0:1024].rearrange(
                        "p (cc q) -> p cc q", cc=CCH)))[-1],
                    waits=[(sTE, sTE.n)], inc=sDVE)
                rd[bid] = (sDVE, d_ev)
                p0_evac[tc] = d_ev
                p0_tr_te[tc] = sTE.n
            p0_pend.append(p0_tr)
            if len(p0_pend) > 2:
                p0_pend.pop(0)()
            if tc == 9:
                pend_qk_evacs = qk_part(0, 0, 0, qT, dcs=(0, 1), defer_evac=True)
        while p0_pend:
            p0_pend.pop(0)()
        P0_TEN = sTE.n
        P0_ACT = sACT.n
        P0_DVE = sDVE.n
        S.add("sync", lambda e: e.dma_start(out=cfa[:, :], in_=cf32_in[:, :]),
              inc=sSD, amt=16)
        S.add("sync", lambda e: e.dma_start(out=cba[:, :], in_=cb16_in[:, :]),
              inc=sSD, amt=16)
        SD_CONST = sSD.n
        def ones_f(e):
            e.drain()
            e.memset(vgF[:, :], 0.0)
            e.drain()
            for pair in range(NPAIR_D, TC * 8):
                e.memset(vgp(pair)[:, 64:65], 1.0)
            e.drain()
            return e.engine_nop()
        S.add("vector", ones_f, waits=[(sDVE, P0_DVE), (sSD, p0_sd[TC - 2])], inc=sDVE)
        for f in pend_qk_evacs:
            f()
        qk_part(0, 0, 0, qT, dcs=(2, 3))

        qk_part(0, 1, 0, qT)
        qk_part(0, 0, 1, qT)
        S.add("gpsimd", w_half_load(Wk, 0), waits=[(sTE, sTE.n)], inc=sGD, amt=16)
        GD_W[(1, 0)] = sGD.n
        qk_part(0, 1, 1, qT)
        S.add("gpsimd", w_half_load(Wk, 1), waits=[(sTE, sTE.n)], inc=sGD, amt=16)
        GD_W[(1, 1)] = sGD.n
        qk_part(1, 0, 0, kT)
        qk_part(1, 1, 0, kT)
        qk_part(1, 0, 1, kT)
        S.add("gpsimd", w_half_load(Wv, 0), waits=[(sTE, sTE.n)], inc=sGD, amt=16)
        GD_W[(2, 0)] = sGD.n
        qk_part(1, 1, 1, kT)
        S.add("gpsimd", w_half_load(Wv, 1), waits=[(sTE, sTE.n)], inc=sGD, amt=16)
        GD_W[(2, 1)] = sGD.n
        P1_ACT_QK = sACT.n

        for tc in range(TC):
            for ds in range(2):
                gd = GD_W[(2, ds)]
                bid = ("pb", 4 + ds)
                for cc in range(CCH):
                    lw = Trz1[:, cc * T + tc * 128: cc * T + tc * 128 + 128]
                    S.add("tensor", lambda e, lw=lw: e.ldweights(lw), waits=[(sGD, gd)])
                    rhs = wsl[ds % 2][:, cc * 512: cc * 512 + 512]
                    te_mm = S.add("tensor", lambda e, b=PB[4 + ds], lw=lw, rhs=rhs, cc=cc:
                                  e.matmul(b[:, 0:512], lw, rhs,
                                           start=(cc == 0), stop=(cc == CCH - 1)),
                                  waits=bank_w(bid) if cc == 0 else [],
                                  inc=sTE)
                wr[bid] = (sTE, te_mm)

                def vevac(e, b=PB[4 + ds], tc=tc, ds=ds):
                    from concourse.ap import AP
                    e.drain()
                    pair0 = tc * 8 + ds * 4
                    if pair0 + 4 <= NPAIR_D:
                        base = vgD.ap()
                        part = list(base.ap[0])
                        dst = AP(tensor=base.tensor, offset=base.offset + pair0 * RW,
                                 ap=[part, [RW, 4], [96, 2], [1, 64]])
                        e.tensor_copy(dst, b[:, 0:512].rearrange(
                            "p (pr two d) -> p pr two d", pr=4, two=2))
                    else:
                        for j in range(8):
                            h = ds * 8 + j
                            pr = tc * 8 + h // 2
                            off = 0 if h % 2 == 0 else 96
                            e.tensor_copy(vgp(pr)[:, off:off + 64], b[:, j * 64:(j + 1) * 64])
                    e.drain()
                    return e.engine_nop()
                d_ev = S.add("vector", vevac, waits=[wr[bid]], inc=sDVE)
                rd[bid] = (sDVE, d_ev)
        P1_TE = sTE.n
        P1_DVE = sDVE.n
        S.add("gpsimd", w_half_load(Wo, 0), waits=[(sTE, P1_TE)], inc=sGD, amt=16)
        GD_WO0 = sGD.n
        S.add("gpsimd", w_half_load(Wo, 1), waits=[(sTE, P1_TE)], inc=sGD, amt=16)
        GD_WO1 = sGD.n

        slab_free = [0, 0, 0, 0]
        stg_free = [0, 0, 0, 0]
        gi = 0
        pend_evac = [None, None]
        for pi in range(NH // 2):
            qw = QW[2 * pi]
            dc = pi
            for qb in range(T // qw):
                qs = qb * qw
                qe = qs + qw - 1
                K = qe // 128 + 1
                gpar = gi % 2
                pv_bid = {0: ("pb", 2 + 2 * gpar), 1: ("pb", 3 + 2 * gpar)}
                pvb_i = {0: 2 + 2 * gpar, 1: 3 + 2 * gpar}
                exp_done = {}
                mask_done = {}
                if pend_evac[0] is not None:
                    for f in pend_evac[0][0]:
                        f(d_rc=pend_evac[0][1][f])
                    pend_evac[0] = None
                if pend_evac[1] is not None:
                    rfs, efs = pend_evac[1]
                    drcs = {}
                    for rf, ef in zip(rfs, efs):
                        drcs[ef] = rf()
                    pend_evac[0] = (efs, drcs)
                    pend_evac[1] = None

                def emit_score(kb, hp):
                    h = 2 * pi + hp
                    sbi = hp
                    s_bid = ("pb", sbi)
                    r0 = hp * 64
                    tile = None if hp == 0 else (64, 0)
                    lwk = kT[r0:r0 + 64, dc * T + kb * 128: dc * T + kb * 128 + 128]
                    qrhs = qT[r0:r0 + 64, dc * T + qs: dc * T + qs + qw]
                    S.add("tensor", lambda e, lwk=lwk, tile=tile: e.ldweights(lwk, tile_position=tile),
                          waits=[(sACT, P1_ACT_QK)] + bank_w(s_bid))
                    te_s = S.add("tensor", lambda e, sb=PB[sbi], lwk=lwk, qrhs=qrhs, tile=tile,
                                 qw=qw: e.matmul(sb[:, 0:qw], lwk, qrhs, start=True, stop=True,
                                                 tile_position=tile), inc=sTE)
                    wr[s_bid] = (sTE, te_s)
                    slot = (kb % 2) * 2 + hp
                    slab = slabs[:, slot * 512: slot * 512 + qw]
                    ac = amap[(h, qb, kb)]
                    bias = alibi_s[:, ac:ac + 1]
                    a_ex = S.add("scalar", lambda e, slab=slab, sb=PB[sbi], bias=bias, qw=qw:
                                 e.activation(slab, sb[:, 0:qw], AF.Exp, bias=bias, scale=0.125),
                                 waits=[(sTE, te_s), (sTE, slab_free[slot]), (sSD, SD_CONST)],
                                 inc=sACT)
                    rd[s_bid] = (sACT, a_ex)
                    exp_done[(kb, hp)] = a_ex
                    if kb * 128 + 127 > qs:
                        oi = (kb * 128 - qs) // 128
                        d_mk = S.add("vector", lambda e, slab=slab, oi=oi, qw=qw: (
                            e.drain(), e.tensor_mul(slab, slab,
                                                    masks_s[:, oi * 512:oi * 512 + qw]))[-1],
                            waits=[(sACT, a_ex), (sSD, SD_CONST)], inc=sDVE)
                        mask_done[(kb, hp)] = d_mk

                def emit_pv(kb, hp):
                    slot = (kb % 2) * 2 + hp
                    slab = slabs[:, slot * 512: slot * 512 + qw]
                    pv_w = ([(sDVE, mask_done[(kb, hp)])] if (kb, hp) in mask_done
                            else [(sACT, exp_done[(kb, hp)])])
                    pr = kb * 8 + pi
                    stat = vgp(pr)[:, 0:65] if hp == 0 else vgp(pr)[:, 32:160]
                    if kb == 0:
                        pv_w += bank_w(pv_bid[hp]) + [(sDVE, P1_DVE)]
                    mtop = 65 if hp == 0 else 128
                    S.add("tensor", lambda e, stat=stat: e.ldweights(stat), waits=pv_w)
                    te_pv = S.add("tensor", lambda e, pvb=PB[pvb_i[hp]], stat=stat, slab=slab,
                                  kb=kb, K=K, mtop=mtop, qw=qw:
                                  e.matmul(pvb[0:mtop, 0:qw], stat, slab,
                                           start=(kb == 0), stop=(kb == K - 1)),
                                  inc=sTE)
                    slab_free[slot] = te_pv
                    return te_pv

                te_pv_d = {}
                for kb in range(K):
                    emit_score(kb, 0)
                    emit_score(kb, 1)
                    if kb >= 1:
                        te_pv_d[0] = emit_pv(kb - 1, 0)
                        te_pv_d[1] = emit_pv(kb - 1, 1)
                te_pv_d[0] = emit_pv(K - 1, 0)
                te_pv_d[1] = emit_pv(K - 1, 1)
                wr[pv_bid[0]] = (sTE, te_pv_d[0])
                wr[pv_bid[1]] = (sTE, te_pv_d[1])

                rfs = []
                efs = []
                for hp in range(2):
                    h = 2 * pi + hp
                    drow = 64 if hp == 0 else 32
                    rslot = gpar * 2 + hp

                    def emit_recip(pvbi=pvb_i[hp], drow=drow, rslot=rslot, qw=qw):
                        def recip_fn(e, pvb=PB[pvbi], drow=drow, rslot=rslot, qw=qw):
                            e.drain()
                            e.reciprocal(rtmp_s[drow:drow + 1, 0:qw], pvb[drow:drow + 1, 0:qw])
                            e.drain()
                            return e.tensor_copy(
                                recb_s[drow:drow + 1, rslot * 512: rslot * 512 + qw],
                                rtmp_s[drow:drow + 1, 0:qw])
                        return S.add("vector", recip_fn,
                                     waits=[(sTE, te_pv_d[hp]), (sDVE, DVE_INIT)], inc=sDVE)

                    def emit_evac(pvbi=pvb_i[hp], pvbid=pv_bid[hp], te_pv=te_pv_d[hp],
                                  hp=hp, drow=drow, rslot=rslot, qw=qw, dc=dc, qs=qs,
                                  d_rc=0):
                        r0 = hp * 64
                        tileb = (64, 0) if drow == 64 else (32, 0)
                        stg = stg_s[:, rslot * 512: rslot * 512 + qw]
                        a_st = S.add("scalar", lambda e, stg=stg, pvb=PB[pvbi], r0=r0, qw=qw:
                                     e.copy(stg[r0:r0 + 64, :], pvb[r0:r0 + 64, 0:qw]),
                                     waits=[(sTE, te_pv), (sDVE, stg_free[rslot])], inc=sACT)
                        S.add("tensor", lambda e, tileb=tileb, drow=drow: e.ldweights(
                            bcon_s[drow:drow + 32, :], tile_position=tileb),
                            waits=[(sDVE, d_rc), (sSD, SD_CONST), (sACT, a_st)])
                        te_bc = S.add("tensor", lambda e, bcb=PB[pvbi], tileb=tileb, drow=drow,
                                      rslot=rslot, qw=qw:
                                      e.matmul(bcb[:, 0:qw], bcon_s[drow:drow + 32, :],
                                               recb_s[drow:drow + 32,
                                                      rslot * 512: rslot * 512 + qw],
                                               start=True, stop=True, tile_position=tileb),
                                      inc=sTE)
                        dsta = TrAttn[r0:r0 + 64, dc * T + qs: dc * T + qs + qw]
                        d_ml = S.add("vector", lambda e, dsta=dsta, stg=stg, bcb=PB[pvbi],
                                     r0=r0, qw=qw: (
                            e.drain(), e.tensor_mul(dsta, stg[r0:r0 + 64, :],
                                                    bcb[r0:r0 + 64, 0:qw]))[-1],
                            waits=[(sTE, te_bc), (sACT, a_st), (sTE, P1_TE)], inc=sDVE)
                        rd[pvbid] = (sDVE, d_ml)
                        stg_free[rslot] = d_ml
                    rfs.append(emit_recip)
                    efs.append(emit_evac)
                pend_evac[1] = (rfs, efs)
                gi += 1
        if pend_evac[1] is not None:
            rfs, efs = pend_evac[1]
            drcs = {}
            for rf, ef in zip(rfs, efs):
                drcs[ef] = rf()
            if pend_evac[0] is not None:
                for f in pend_evac[0][0]:
                    f(d_rc=pend_evac[0][1][f])
            pend_evac[0] = (efs, drcs)
            pend_evac[1] = None
        if pend_evac[0] is not None:
            for f in pend_evac[0][0]:
                f(d_rc=pend_evac[0][1][f])
        P2_TE = sTE.n
        P2_DVE = sDVE.n
        P2_ACT = sACT.n

        def acc_ch(i):
            if i < 4:
                return accA[:, i * 1024:(i + 1) * 1024]
            return accB[:, (i - 4) * 1024:(i - 4) * 1024 + 1024]

        fsl = [fsl0, fsl1, fsl2]
        gd_half = {}
        half_slot = {}
        _gctr = [0]

        gd_half_a = {}

        def ffn_w_load(th, e_, kind, hf, te_guard=None, extra=(), slot=None,
                       te_guard2=None):
            if th >= 2:
                return
            if slot is None:
                slot = _gctr[0] % 3
            _gctr[0] += 1
            half_slot[(th, e_, kind, hf)] = slot

            def f(e, e_=e_, kind=kind, hf=hf, slot=slot, sub=None):
                dst = fsl[slot][:, :].rearrange("p (a b) -> p a b", a=8)
                if kind == 1:
                    src = W1[e_, :, hf * 1024:(hf + 1) * 1024].rearrange(
                        "(cc p) f -> p cc f", p=128)
                else:
                    src = W2[e_, hf * 1024:(hf + 1) * 1024, :].rearrange(
                        "(fc p) h -> p fc h", p=128)
                if sub is not None:
                    dst = dst[:, :, sub * 512:(sub + 1) * 512]
                    src = src[:, :, sub * 512:(sub + 1) * 512]
                return e.dma_start(out=dst, in_=src)
            if te_guard2 is not None:
                S.add("gpsimd", lambda e: f(e, sub=0),
                      waits=[(sTE, te_guard)] + list(extra), inc=sGD, amt=16)
                gd_half_a[(th, e_, kind, hf)] = sGD.n
                S.add("gpsimd", lambda e: f(e, sub=1),
                      waits=[(sTE, te_guard2)], inc=sGD, amt=16)
            else:
                S.add("gpsimd", f,
                      waits=[(sTE, sTE.n if te_guard is None else te_guard)] + list(extra),
                      inc=sGD, amt=16)
            gd_half[(th, e_, kind, hf)] = sGD.n


        ffn_w_load(0, 0, 1, 0, te_guard=P2_TE, slot=0)
        ffn_w_load(0, 0, 2, 0, te_guard=P2_TE, slot=2)
        wo_last_te = 0
        trz_act = {}
        xu_spill_sd = {}
        zp_te = {}
        w_done_dve = {}
        zp_act = {}

        def stats_fn(buf, so, stt):
            def f(e, buf=buf, so=so, stt=stt):
                e.drain()
                e.bn_stats(stt[:, so:so + 6], buf[:, 0:512])
                e.bn_stats(stt[:, so + 6:so + 12], buf[:, 512:1024])
                e.drain()
                return e.bn_aggr(stt[:, so + 12:so + 14], stt[:, so:so + 12])
            return f

        def rsq_fn(so, stt, out=None, src=None, short=False):
            def f(e, so=so, stt=stt, out=out, src=src, short=short):
                e.activation(stt[:, so + 14:so + 15], stt[:, so + 13:so + 14], AF.Ln,
                             bias=epsv[:, 0:1], scale=1.0)
                e.drain()
                r = e.activation(stt[:, so + 15:so + 16], stt[:, so + 14:so + 15], AF.Exp,
                                 bias=0.0, scale=-0.5)
                if short:
                    return r
                e.drain()
                e.activation(stt[:, so + 16:so + 17], stt[:, so + 12:so + 13], AF.Copy,
                             bias=0.0, scale=-1.0)
                e.drain()
                r = e.activation(stt[:, so + 17:so + 18], stt[:, so + 16:so + 17], AF.Copy,
                                 bias=0.0, scale=stt[:, so + 15:so + 16])
                if out is not None:
                    e.drain()
                    r = e.activation(out, src, AF.Identity, bias=stt[:, so + 17:so + 18],
                                     scale=stt[:, so + 15:so + 16])
                return r
            return f

        x0_sd = {}
        d_x1_d = {}
        x1sp_sd = {}
        usp_sd = {}
        a_ev_d = {}
        r1_d = {}
        r2u_d = {}

        def p3_A1(tc):
            sl3 = (tc % 3) * 1024
            x0_sd[tc] = S.add("sync", lambda e, tc=tc, sl3=sl3: e.dma_start(
                out=x0_t[:, sl3:sl3 + 1024], in_=x_in[tc * 128:(tc + 1) * 128, :]),
                waits=[(sTE, P2_TE), (sDVE, d_x1_d.get(tc - 3, 0))], inc=sSD, amt=16)
            bids = [("pb", 0), ("pb", 1)]
            for cc in range(CCH):
                lw = TrAttn[:, cc * T + tc * 128: cc * T + tc * 128 + 128]
                S.add("tensor", lambda e, lw=lw: e.ldweights(lw),
                      waits=[(sDVE, P2_DVE), (sGD, GD_WO1)] +
                            (bank_w(bids[0]) + bank_w(bids[1]) if cc == 0 else []))
                for cs in range(2):
                    rhs = wsl[cs][:, cc * 512: cc * 512 + 512]
                    te_mm = S.add("tensor", lambda e, b=PB[cs], lw=lw, rhs=rhs, cc=cc:
                                  e.matmul(b[:, 0:512], lw, rhs,
                                           start=(cc == 0), stop=(cc == CCH - 1)), inc=sTE)
            wr[bids[0]] = (sTE, te_mm)
            wr[bids[1]] = (sTE, te_mm)
            x1b = x1_t[:, sl3:sl3 + 1024]

            def wo_ev(e, x1b=x1b):
                e.drain()
                e.copy(x1b[:, 0:512], PB[0][:, 0:512])
                e.drain()
                return e.copy(x1b[:, 512:1024], PB[1][:, 0:512])
            a_ev = S.add("scalar", wo_ev,
                         waits=[(sTE, te_mm), (sSD, x1sp_sd.get(tc - 3, 0))], inc=sACT)
            a_ev_d[tc] = a_ev
            rd[bids[0]] = (sACT, a_ev)
            rd[bids[1]] = (sACT, a_ev)
            return te_mm

        def p3_A2(tc):
            par = tc % 2
            sl3 = (tc % 3) * 1024
            x1b = x1_t[:, sl3:sl3 + 1024]
            st = st3[:, par * 64: par * 64 + 64]
            d_s1 = S.add("vector", stats_fn(x1b, 0, st), waits=[(sACT, a_ev_d[tc])], inc=sDVE)
            r1_d[tc] = S.add("scalar", rsq_fn(0, st, short=True), waits=[(sDVE, d_s1)], inc=sACT)

        def p3_A3(tc):
            par = tc % 2
            sl3 = (tc % 3) * 1024
            x1b = x1_t[:, sl3:sl3 + 1024]
            x0v = x0_t[:, sl3:sl3 + 1024]
            ub = u_t[:, (tc % 4) * 1024:(tc % 4) * 1024 + 1024]
            st = st3[:, par * 64: par * 64 + 64]

            def x1_fn(e, x1b=x1b, x0v=x0v, st=st):
                e.drain()
                e.tensor_scalar(x1b, x1b, st[:, 12:13], st[:, 15:16], ALU.subtract, ALU.mult)
                e.drain()
                return e.tensor_add(x1b, x1b, x0v)
            d_x1 = S.add("vector", x1_fn, waits=[(sACT, r1_d[tc]), (sSD, x0_sd[tc])], inc=sDVE)
            d_x1_d[tc] = d_x1
            x1sp_sd[tc] = S.add("sync", lambda e, x1b=x1b, tc=tc: e.dma_start(
                out=xuscr[tc * 128:(tc + 1) * 128, 1024:2048], in_=x1b),
                waits=[(sDVE, d_x1)], inc=sSD, amt=16)
            d_s2 = S.add("vector", stats_fn(x1b, 18, st), inc=sDVE)
            r2u_d[tc] = S.add("scalar", rsq_fn(18, st, out=ub, src=x1b),
                              waits=[(sDVE, d_s2), (sSD, usp_sd.get(tc - 3, 0))], inc=sACT)

        def p3_A4(tc):
            par = tc % 2
            ub = u_t[:, (tc % 4) * 1024:(tc % 4) * 1024 + 1024]
            st = st3[:, par * 64: par * 64 + 64]
            d_s3 = S.add("vector", stats_fn(ub, 36, st), waits=[(sACT, r2u_d[tc])], inc=sDVE)
            zp = zp_t[:, par * 1024: par * 1024 + 1024]
            a_r3 = S.add("scalar", rsq_fn(36, st, out=zp, src=ub),
                         waits=[(sDVE, d_s3), (sTE, zp_te.get(tc - 2, 0))], inc=sACT)
            zp_act[tc] = a_r3
            usp_sd[tc] = S.add("sync", lambda e, tc=tc, ub=ub: e.dma_start(
                out=xuscr[tc * 128:(tc + 1) * 128, 0:1024], in_=ub),
                waits=[(sACT, r2u_d[tc])], inc=sSD, amt=16)
            xu_spill_sd[tc] = usp_sd[tc]

        def p3_B1(tc):
            par = tc % 2
            zp = zp_t[:, par * 1024: par * 1024 + 1024]
            bid = ("pt", tc % 2)
            pt = PT[tc % 2]
            for cc in range(CCH):
                te_tr = S.add("tensor", lambda e, zp=zp, cc=cc, pt=pt: e.transpose(
                    pt[:, cc * 128:(cc + 1) * 128], zp[:, cc * 128:(cc + 1) * 128],
                    ident_s[:, :]),
                    waits=[(sACT, zp_act[tc])] + (bank_w(bid) if cc == 0 else []), inc=sTE)
            wr[bid] = (sTE, sTE.n)
            for cc in range(CCH):
                dst = Trz[:, cc * T + tc * 128: cc * T + tc * 128 + 128]
                a_tz = S.add("scalar", lambda e, pt=pt, dst=dst, cc=cc: e.activation(
                    dst, pt[:, cc * 128:(cc + 1) * 128], AF.Identity,
                    bias=lgb_s[:, CCH + cc:CCH + cc + 1],
                    scale=lgb_s[:, cc:cc + 1]),
                    waits=[(sTE, sTE.n), (sTE, P2_TE), (sSD, SD_CONST)], inc=sACT)
            rd[bid] = (sACT, sACT.n)
            zp_te[tc] = sTE.n
            trz_act[tc] = sACT.n

        def p3_B23(tc):
            par = tc % 2
            lg = lgS[:, par * 64: par * 64 + 64]
            lg_bid = ("pb", 2)
            for cc in range(CCH):
                lw = Trz[:, cc * T + tc * 128: cc * T + tc * 128 + 128]
                S.add("tensor", lambda e, lw=lw: e.ldweights(lw),
                      waits=[(sACT, trz_act[tc]), (sGD, GD_SMALL)] +
                            (bank_w(lg_bid) if cc == 0 else []))
                te_lg = S.add("tensor", lambda e, lw=lw, cc=cc:
                              e.matmul(PB[2][:, 0:E], lw, rW_s[:, cc * E:(cc + 1) * E],
                                       start=(cc == 0), stop=(cc == CCH - 1)), inc=sTE)
            wr[lg_bid] = (sTE, te_lg)

            def rt1(e, lg=lg):
                e.drain()
                e.tensor_add(lg[:, 0:8], PB[2][:, 0:E], rbc_s[:, :])
                e.drain()
                return e.reduce_max(lg[:, 8:9], lg[:, 0:8], axis=AX, negate=True)
            d_r1 = S.add("vector", rt1, waits=[(sTE, te_lg), (sSD, SD_CONST)], inc=sDVE)
            rd[lg_bid] = (sDVE, d_r1)
            a_re_d[tc] = S.add("scalar", lambda e, lg=lg: e.activation(
                lg[:, 9:17], lg[:, 0:8], AF.Exp, bias=lg[:, 8:9], scale=1.0),
                waits=[(sDVE, d_r1)], inc=sACT)

        def p3_B3(tc):
            par = tc % 2
            lg = lgS[:, par * 64: par * 64 + 64]
            wv = wS[:, tc * E:(tc + 1) * E]
            wb = wbf_t[:, (tc % 16) * 8:(tc % 16) * 8 + 8]

            def rt2(e, wv=wv, wb=wb, lg=lg):
                ex = lg[:, 9:17]
                e.drain()
                e.reduce_max(lg[:, 17:18], ex, axis=AX)
                e.drain()
                e.tensor_scalar(lg[:, 18:26], ex, lg[:, 17:18], None, ALU.is_equal)
                e.drain()
                e.tensor_tensor(lg[:, 26:34], ex, lg[:, 18:26], ALU.mult)
                e.drain()
                e.tensor_sub(lg[:, 34:42], ex, lg[:, 26:34])
                e.drain()
                e.reduce_max(lg[:, 42:43], lg[:, 34:42], axis=AX)
                e.drain()
                e.tensor_scalar(lg[:, 43:51], ex, lg[:, 42:43], None, ALU.is_ge)
                e.drain()
                e.tensor_tensor(lg[:, 51:59], ex, lg[:, 43:51], ALU.mult)
                e.drain()
                e.tensor_add(lg[:, 59:60], lg[:, 17:18], lg[:, 42:43])
                e.drain()
                e.tensor_scalar(lg[:, 60:61], lg[:, 59:60], 1e-8, None, ALU.max)
                e.drain()
                e.reciprocal(lg[:, 61:62], lg[:, 60:61])
                e.drain()
                e.tensor_scalar(wv, lg[:, 51:59], lg[:, 61:62], None, ALU.mult)
                e.drain()
                return e.tensor_copy(wb, wv)
            d_w = S.add("vector", rt2, waits=[(sACT, a_re_d[tc])], inc=sDVE)
            w_done_dve[tc] = d_w

        a_re_d = {}
        for i in range(TC + 8):
            if 0 <= i - 7 < TC:
                p3_B3(i - 7)
            if 0 <= i - 6 < TC:
                p3_B23(i - 6)
            if 0 <= i - 5 < TC:
                p3_B1(i - 5)
            if i < TC:
                wo_last_te = p3_A1(i)
            if 0 <= i - 1 < TC:
                p3_A2(i - 1)
            if 0 <= i - 2 < TC:
                p3_A3(i - 2)
            if 0 <= i - 3 < TC:
                p3_A4(i - 3)
        ffn_w_load(0, 0, 1, 1, te_guard=P2_TE, slot=1,
                   extra=[(sSD, usp_sd[TC - 1]), (sDVE, sDVE.n)])
        P3_TE = sTE.n
        P3_DVE = sDVE.n
        P3_SD = sSD.n

        acc_reload_sd = {}
        acc_b2d = {}
        final_read_dve = {}
        u_rel_guard = []

        bidw = ("pt", 0)
        W8 = TC * 8
        te_wt = S.add("tensor", lambda e: e.transpose(
            PT[0][0:W8, 0:128], wbf_t[:, 0:W8], ident_s[:, :]),
            waits=[(sDVE, w_done_dve[TC - 1]), (sTE, P3_TE)] + bank_w(bidw), inc=sTE)
        wr[bidw] = (sTE, te_wt)
        a_wt = S.add("scalar", lambda e: e.copy(wtscr[0:W8, 0:128], PT[0][0:W8, 0:128]),
                     waits=[(sTE, te_wt)], inc=sACT)
        rd[bidw] = (sACT, a_wt)
        wt_sd = {}

        def emit_acc_chunk(th, i):
            tc = th * THC + i
            wl = [(sSD, P3_SD), (sDVE, P3_DVE)]
            if i < 4:
                wl.append((sTE, P3_TE))
            if th == 1:
                wl.append((sDVE, final_read_dve.get((0, i), 0)))
            acc_reload_sd[(th, i)] = S.add("sync", lambda e, tc=tc, i=i: e.dma_start(
                out=acc_ch(i), in_=xuscr[tc * 128:(tc + 1) * 128, 0:1024]), waits=wl,
                inc=sSD, amt=16)
            wt_sd[tc] = S.add("sync", lambda e, tc=tc: e.dma_start(
                out=wT_s[0:8, tc * 128:(tc + 1) * 128], in_=wtscr[tc * 8:tc * 8 + 8, 0:128]),
                waits=[(sACT, a_wt)], inc=sSD, amt=16)
            b2_bids = [("pb", 4), ("pb", 5)]
            S.add("tensor", lambda e, tc=tc: e.ldweights(wT_s[0:8, tc * 128:(tc + 1) * 128]),
                  waits=[(sSD, wt_sd[tc]), (sGD, GD_SMALL)] +
                        bank_w(b2_bids[0]) + bank_w(b2_bids[1]))
            for hs in range(2):
                te_b2 = S.add("tensor", lambda e, tc=tc, hs=hs:
                              e.matmul(PB[4 + hs][:, 0:512],
                                       wT_s[0:8, tc * 128:(tc + 1) * 128],
                                       b2_s[0:8, hs * 512:(hs + 1) * 512],
                                       start=True, stop=True), inc=sTE)
                wr[b2_bids[hs]] = (sTE, te_b2)

            def b2add(e, i=i):
                ac = acc_ch(i)
                e.drain()
                e.tensor_add(ac[:, 0:512], ac[:, 0:512], PB[4][:, 0:512])
                e.drain()
                return e.tensor_add(ac[:, 512:1024], ac[:, 512:1024], PB[5][:, 0:512])
            d_b2 = S.add("vector", b2add,
                         waits=[(sTE, te_b2), (sSD, acc_reload_sd[(th, i)])], inc=sDVE)
            rd[b2_bids[0]] = (sDVE, d_b2)
            rd[b2_bids[1]] = (sDVE, d_b2)
            acc_b2_dve = d_b2
            acc_reload_sd[(th, i)] = acc_reload_sd[(th, i)]
            acc_b2d[(th, i)] = acc_b2_dve

        gelu_act = {}
        final_store_sd = {}

        def emit_final(th, i):
            tc = th * THC + i
            par = i % 2
            so = par * 32
            x1r = x1r_t[:, par * 1024: par * 1024 + 1024]
            ob = outb_t[:, par * 1024: par * 1024 + 1024]
            if par == 0:
                wl = [(sDVE, final_read_dve.get((th, i - 2), 0)),
                      (sDVE, final_read_dve.get((th, i - 1), 0)),
                      (sDVE, final_read_dve.get((th - 1, THC - 2), 0)),
                      (sDVE, final_read_dve.get((th - 1, THC - 1), 0)),
                      (sSD, xu_spill_sd[TC - 1]), (sTE, zp_te.get(TC - 1, 0))]
                S.add("sync", lambda e, tc=tc: e.dma_start(
                    out=x1r_t[:, 0:2048].rearrange("p (two q) -> p two q", two=2),
                    in_=xuscr[tc * 128:(tc + 2) * 128, 1024:2048].rearrange(
                        "(two p) c -> p two c", p=128)), waits=wl, inc=sSD, amt=16)
            sd_r = sSD.n

            def fstat(e, i=i, so=so):
                ac = acc_ch(i)
                e.drain()
                e.bn_stats(stF[:, so:so + 6], ac[:, 0:512])
                e.bn_stats(stF[:, so + 6:so + 12], ac[:, 512:1024])
                e.drain()
                return e.bn_aggr(stF[:, so + 12:so + 14], stF[:, so:so + 12])
            d_fs = S.add("vector", fstat, inc=sDVE)

            def frsq(e, so=so):
                e.activation(stF[:, so + 14:so + 15], stF[:, so + 13:so + 14], AF.Ln,
                             bias=epsv[:, 0:1], scale=1.0)
                e.drain()
                e.activation(stF[:, so + 15:so + 16], stF[:, so + 14:so + 15], AF.Exp,
                             bias=0.0, scale=-0.5)
                e.drain()
                e.activation(stF[:, so + 16:so + 17], stF[:, so + 12:so + 13], AF.Copy,
                             bias=0.0, scale=-1.0)
                e.drain()
                return e.activation(stF[:, so + 17:so + 18], stF[:, so + 16:so + 17], AF.Copy,
                                    bias=0.0, scale=stF[:, so + 15:so + 16])
            a_fr = S.add("scalar", frsq, waits=[(sDVE, d_fs)], inc=sACT)

            def fout(e, i=i, x1r=x1r, ob=ob, so=so):
                ac = acc_ch(i)
                e.drain()
                e.scalar_tensor_tensor(ob, ac, stF[:, so + 15:so + 16], x1r, ALU.mult, ALU.add)
                e.drain()
                return e.tensor_scalar(ob, ob, stF[:, so + 17:so + 18], None, ALU.add)
            d_fo = S.add("vector", fout,
                         waits=[(sACT, a_fr), (sSD, sd_r),
                                (sSD, final_store_sd.get((th, i - 2), 0))], inc=sDVE)
            final_read_dve[(th, i)] = d_fo
            if par == 1:
                final_store_sd[(th, i - 1)] = final_store_sd[(th, i)] = S.add(
                    "sync", lambda e, tc=tc: e.dma_start(
                        out=y_out[(tc - 1) * 128:(tc + 1) * 128, :].rearrange(
                            "(two p) c -> p two c", p=128),
                        in_=outb_t[:, 0:2048].rearrange("p (two q) -> p two q", two=2)),
                    waits=[(sDVE, final_read_dve[(th, i - 1)]), (sDVE, d_fo)],
                    inc=sSD, amt=16)

        for th in range(2):
            for e_ in range(E):
                nth, ne = (th, e_ + 1) if e_ < E - 1 else (th + 1, 0)
                HL = T // 2
                n_tsl = max(1, HL // 512)
                tw = min(512, HL)
                w1_fc_te = {}
                for fc in range(FCH):
                    hf = fc // 8
                    k1 = (th, e_, 1, hf)
                    gd = gd_half_a[k1] if (fc % 8 < 4 and k1 in gd_half_a) else gd_half[k1]
                    slot = half_slot[k1]
                    bk = [2 * (fc % 2), 2 * (fc % 2) + 1]
                    w1_bids = [("pb", bk[0]), ("pb", bk[1])]
                    for cc in range(CCH):
                        lw = fsl[slot][:, cc * 1024 + (fc % 8) * 128:
                                       cc * 1024 + (fc % 8) * 128 + 128]
                        S.add("tensor", lambda e, lw=lw: e.ldweights(lw),
                              waits=[(sGD, gd), (sACT, trz_act.get(th * THC + THC - 1, 0))] +
                                    (bank_w(w1_bids[0]) + bank_w(w1_bids[1]) if cc == 0 else []))
                        for tsl in range(n_tsl):
                            rhs = Trz[:, cc * T + th * HL + tsl * tw:
                                      cc * T + th * HL + tsl * tw + tw]
                            te_mm = S.add("tensor", lambda e, b=PB[bk[tsl]], lw=lw, rhs=rhs, cc=cc,
                                          tw=tw: e.matmul(b[:, 0:tw], lw, rhs,
                                                          start=(cc == 0), stop=(cc == CCH - 1)),
                                          inc=sTE)
                    wr[w1_bids[0]] = (sTE, te_mm)
                    wr[w1_bids[1]] = (sTE, te_mm)
                    for tsl in range(n_tsl):
                        dst = hidT[:, fc * HL + tsl * tw: fc * HL + tsl * tw + tw]
                        a_g = S.add("scalar", lambda e, b=PB[bk[tsl]], dst=dst, e_=e_, fc=fc, tw=tw:
                                    e.activation(dst, b[:, 0:tw], AF.Gelu,
                                                 bias=b1_s[:, e_ * FCH + fc: e_ * FCH + fc + 1],
                                                 scale=1.0),
                                    waits=[(sTE, te_mm), (sTE, wo_last_te),
                                           (sSD, SD_CONST)], inc=sACT)
                        rd[w1_bids[tsl]] = (sACT, a_g)
                    gelu_act[fc] = a_g
                    w1_fc_te[fc] = sTE.n
                    if e_ == 0 and fc % 2 == 1 and fc // 2 < THC:
                        emit_acc_chunk(th, fc // 2)
                    if fc == 7:
                        ffn_w_load(th, e_, 2, 1)
                ffn_w_load(nth, ne, 1, 0, te_guard=w1_fc_te[11], te_guard2=w1_fc_te[15])
                for t8 in range(THC):
                    bk = [2 * (t8 % 2), 2 * (t8 % 2) + 1]
                    w2_bids = [("pb", bk[0]), ("pb", bk[1])]
                    for fc in range(FCH):
                        hf = fc // 8
                        gd = gd_half[(th, e_, 2, hf)]
                        slot = half_slot[(th, e_, 2, hf)]
                        lw = hidT[:, fc * (T // 2) + t8 * 128: fc * (T // 2) + t8 * 128 + 128]
                        S.add("tensor", lambda e, lw=lw: e.ldweights(lw),
                              waits=[(sGD, gd), (sACT, gelu_act[fc])] +
                                    (bank_w(w2_bids[0]) + bank_w(w2_bids[1]) if fc == 0 else []))
                        for hs in range(2):
                            rhs = fsl[slot][:, (fc % 8) * 1024 + hs * 512:
                                            (fc % 8) * 1024 + hs * 512 + 512]
                            te_mm = S.add("tensor", lambda e, b=PB[bk[hs]], lw=lw, rhs=rhs, fc=fc:
                                          e.matmul(b[:, 0:512], lw, rhs,
                                                   start=(fc == 0), stop=(fc == FCH - 1)),
                                          inc=sTE)
                    wr[w2_bids[0]] = (sTE, te_mm)
                    wr[w2_bids[1]] = (sTE, te_mm)
                    tc = th * THC + t8
                    wcol = wS[:, tc * E + e_: tc * E + e_ + 1]

                    def stt_fn(e, t8=t8, wcol=wcol, bk=bk):
                        ac = acc_ch(t8)
                        e.drain()
                        e.scalar_tensor_tensor(ac[:, 0:512], PB[bk[0]][:, 0:512], wcol,
                                               ac[:, 0:512], ALU.mult, ALU.add)
                        e.drain()
                        return e.scalar_tensor_tensor(ac[:, 512:1024], PB[bk[1]][:, 0:512], wcol,
                                                      ac[:, 512:1024], ALU.mult, ALU.add)
                    d_st = S.add("vector", stt_fn,
                                 waits=[(sTE, te_mm), (sSD, acc_reload_sd[(th, t8)]),
                                        (sDVE, max(acc_b2d[(th, t8)],
                                                   w_done_dve.get(TC - 1, 0)))], inc=sDVE)
                    rd[w2_bids[0]] = (sDVE, d_st)
                    rd[w2_bids[1]] = (sDVE, d_st)
                    if e_ == E - 1 and t8 % 2 == 1:
                        emit_final(th, t8 - 1)
                        emit_final(th, t8)
                ffn_w_load(nth, ne, 1, 1)
                ffn_w_load(nth, ne, 2, 0)

        SD_ALL = sSD.n
        GD_ALL = sGD.n
        S.add("sync", lambda e: e.nop(), waits=[(sSD, SD_ALL), (sGD, GD_ALL)])

        S.emit(block)
    return nc


def _host_consts(T):
    import ml_dtypes
    BF = ml_dtypes.bfloat16
    TC = T // 128
    sl = np.asarray(_slopes(NH), dtype=np.float32)
    amap, ncol = _acol_map(T)
    karr = np.arange(BLK, dtype=np.float32) - np.float32(BLK - 1)
    bias_ref = (sl[:, None] * karr[None, :]).astype(np.float32)
    alibi = np.zeros((128, ncol), np.float32)
    for (h, qb, kb), col in amap.items():
        qw = QW[h]
        qcen = qb * qw + qw // 2
        ks = kb * 128
        alibi[:, col] = (bias_ref[h, ks:ks + 128].astype(np.float64)
                         - np.float64(bias_ref[h, qcen])).astype(np.float32)
    masks = np.zeros((128, 4 * 512), np.float32)
    p = np.arange(128, dtype=np.float32)
    j = np.arange(512, dtype=np.float32)
    for oi in range(4):
        masks[:, oi * 512:(oi + 1) * 512] = (oi * 128 + p[:, None]) <= j[None, :]
    ident = np.eye(128, dtype=np.float32)
    bcon = np.zeros((128, 128), np.float32)
    bcon[32, :] = 1.0
    bcon[64, :] = 1.0
    return {
        "alibi": alibi,
        "masks": masks.astype(BF),
        "ident": ident.astype(BF),
        "bcon": bcon.astype(BF),
    }


def kernel(L0_attn_x, L0_attn_Wq, L0_attn_Wk, L0_attn_Wv, L0_attn_Wo,
           L0_ffn_ln_g, L0_ffn_ln_b, L0_ffn_rW, L0_ffn_rb, L0_ffn_W1,
           L0_ffn_b1, L0_ffn_W2, L0_ffn_b2):
    T = 2048
    x = np.ascontiguousarray(np.asarray(L0_attn_x, dtype=np.float32))
    consts = _host_consts(T)
    lg = np.asarray(L0_ffn_ln_g, np.float32).reshape(CCH, 128).T
    lb = np.asarray(L0_ffn_ln_b, np.float32).reshape(CCH, 128).T
    lgb = np.concatenate([lg, lb], axis=1)
    b1rr = (np.asarray(L0_ffn_b1, np.float32).reshape(E, FCH, 128)
            .transpose(2, 0, 1).reshape(128, E * FCH))
    rbc = np.broadcast_to(np.asarray(L0_ffn_rb, np.float32), (128, E))
    cf32 = np.ascontiguousarray(np.concatenate(
        [lgb, b1rr, rbc, consts["alibi"]], axis=1))
    cb16 = np.ascontiguousarray(np.concatenate(
        [consts["masks"], consts["bcon"]], axis=1))
    common = {
        "Wq": np.ascontiguousarray(L0_attn_Wq, np.float32),
        "Wk": np.ascontiguousarray(L0_attn_Wk, np.float32),
        "Wv": np.ascontiguousarray(L0_attn_Wv, np.float32),
        "Wo": np.ascontiguousarray(L0_attn_Wo, np.float32),
        "W1": np.ascontiguousarray(L0_ffn_W1, np.float32),
        "W2": np.ascontiguousarray(L0_ffn_W2, np.float32),
        "b2": np.ascontiguousarray(L0_ffn_b2, np.float32),
        "rW": np.ascontiguousarray(L0_ffn_rW, np.float32),
        "cf32": cf32, "cb16": cb16, "ident": consts["ident"],
    }
    nc = build_core_program(T)
    in_maps = [dict(common, x=np.ascontiguousarray(x[c])) for c in range(8)]
    res = run_bass_kernel_spmd(nc, in_maps, core_ids=list(range(8)))
    return np.stack([np.asarray(res.results[c]["y"]) for c in range(8)], axis=0)


# version 26
# speedup vs baseline: 1.0460x; 1.0010x over previous
import math
import contextlib
import numpy as np
import concourse.bass as bass
import concourse.mybir as mybir
from concourse.bass_utils import run_bass_kernel_spmd

F32 = mybir.dt.float32
BF16 = mybir.dt.bfloat16
U8 = mybir.dt.uint8
AF = mybir.ActivationFunctionType
ALU = mybir.AluOpType
AX = mybir.AxisListType.X

C = 1024
NH = 16
HD = 64
E = 8
HF = 2048
EPS = 1e-5
BLK = 2048
CCH = C // 128
FCH = HF // 128
RW = 160

QW = [128, 128, 256, 256] + [512] * 12


def _acol_map(T):
    m = {}
    col = 0
    for h in range(NH):
        qw = QW[h]
        for qb in range(T // qw):
            qe = qb * qw + qw - 1
            for kb in range(qe // 128 + 1):
                m[(h, qb, kb)] = col
                col += 1
    return m, col

AR_A = 0
AR_B = 32768
AR_C = 65536
AR_D = 98304
AR_E = 131072
AR_F = 147456
AR_TOTAL = 163840


def _slopes(n):
    def pow2(m):
        start = 2 ** (-(2 ** (-(math.log2(m) - 3))))
        return [start * start ** i for i in range(m)]
    if math.log2(n).is_integer():
        return pow2(n)
    c = 2 ** math.floor(math.log2(n))
    return pow2(c) + _slopes(2 * c)[0::2][:n - c]


class Sem:
    def __init__(self, nc, es, name):
        self.h = es.enter_context(nc.semaphore(name))
        self.n = 0


class Sched:
    ENGS = ("tensor", "scalar", "vector", "gpsimd", "sync")

    def __init__(self):
        self.ops = {e: [] for e in self.ENGS}
        self.last_wait = {e: {} for e in self.ENGS}

    def add(self, eng, fn, waits=(), inc=None, amt=1):
        if inc is not None and eng in ("gpsimd", "sync") and inc.n > 0:
            waits = list(waits) + [(inc, inc.n)]
        wl = []
        lw = self.last_wait[eng]
        for (s, v) in waits:
            if v <= 0 or lw.get(id(s), 0) >= v:
                continue
            lw[id(s)] = v
            wl.append((s, v))
        self.ops[eng].append((wl, fn, inc, amt))
        if inc is not None:
            inc.n += amt
            return inc.n
        return None

    def emit(self, block):
        for name in self.ENGS:
            oplist = self.ops[name]
            if not oplist:
                continue
            dec = getattr(block, name)

            def section(eng, _ops=oplist):
                for (wl, fn, inc, amt) in _ops:
                    for (s, v) in wl:
                        eng.wait_ge(s.h, v)
                    ins = fn(eng)
                    if inc is not None:
                        ins.then_inc(inc.h, amt)
            dec(section)


def build_core_program(T=2048):
    assert T % 512 == 0
    TC = T // 128
    QC = T // 512
    THC = TC // 2
    NPAIR_D = min(102, TC * 8)
    nc = bass.Bass("TRN2", target_bir_lowering=False)

    x_in = nc.dram_tensor("x", [T, C], F32, kind="ExternalInput").ap()
    Wq = nc.dram_tensor("Wq", [C, C], F32, kind="ExternalInput").ap()
    Wk = nc.dram_tensor("Wk", [C, C], F32, kind="ExternalInput").ap()
    Wv = nc.dram_tensor("Wv", [C, C], F32, kind="ExternalInput").ap()
    Wo = nc.dram_tensor("Wo", [C, C], F32, kind="ExternalInput").ap()
    W1 = nc.dram_tensor("W1", [E, C, HF], F32, kind="ExternalInput").ap()
    W2 = nc.dram_tensor("W2", [E, HF, C], F32, kind="ExternalInput").ap()
    b2v = nc.dram_tensor("b2", [E, C], F32, kind="ExternalInput").ap()
    rWv = nc.dram_tensor("rW", [C, E], F32, kind="ExternalInput").ap()
    amap, acols = _acol_map(T)
    CF32W = 2 * CCH + E * FCH + E + acols
    CB16W = 4 * 512 + 128
    cf32_in = nc.dram_tensor("cf32", [128, CF32W], F32, kind="ExternalInput").ap()
    cb16_in = nc.dram_tensor("cb16", [128, CB16W], BF16, kind="ExternalInput").ap()
    ident_in = nc.dram_tensor("ident", [128, 128], BF16, kind="ExternalInput").ap()
    y_out = nc.dram_tensor("y", [T, C], F32, kind="ExternalOutput").ap()
    xuscr = nc.dram_tensor("xuscr", [T, 2 * C], F32, kind="Internal").ap()

    es = contextlib.ExitStack()
    with es:
        arena = es.enter_context(nc.sbuf_tensor("arena", [128, AR_TOTAL], U8))
        ar_base = nc.lookup_mloc(arena).addr

        def vt(name, shape, dtype, byte_off):
            return nc.alloc_sbuf_tensor_at(name, shape, dtype, offset=ar_base + byte_off)

        Trz1 = vt("Trz1", [128, CCH * T], BF16, AR_A)
        TrAttn = vt("TrAttn", [128, CCH * T], BF16, AR_A)
        hidT = vt("hidT", [128, FCH * (T // 2)], BF16, AR_A)
        qT = vt("qT", [128, CCH * T], BF16, AR_B)
        fsl0 = vt("fsl0", [128, 8192], BF16, AR_B + 16384)
        fsl1 = vt("fsl1", [128, 8192], BF16, AR_B)
        kT = vt("kT", [128, CCH * T], BF16, AR_C)
        fsl2 = vt("fsl2", [128, 8192], BF16, AR_C)
        x0_t = vt("x0t", [128, 3072], F32, AR_C + 16384)
        u_t = vt("ut", [128, 4096], F32, AR_B)
        x1_t = vt("x1t", [128, 3072], F32, AR_F)
        accA = vt("accA", [128, 4096], F32, AR_E)
        accB = vt("accB", [128, 4096], F32, AR_C + 16384)
        vgD = vt("vgD", [128, NPAIR_D * RW], BF16, AR_D)
        Trz = vt("Trz", [128, CCH * T], BF16, AR_D)
        wsl = [vt("wsl0", [128, 4096], BF16, AR_E),
               vt("wsl1", [128, 4096], BF16, AR_E + 8192)]
        xpp = vt("xpp", [128, 4096], F32, AR_F)
        z1pp = vt("z1pp", [128, 3072], BF16, AR_C)
        slabs = vt("slabs", [128, 2048], BF16, AR_F)
        vgF = vt("vgF", [128, max(0, TC * 8 - NPAIR_D) * RW + 16], BF16, AR_F + 4096)
        zp_t = vt("zp", [128, 2048], BF16, AR_F + 12288)
        x1r_t = vt("x1r", [128, 2048], F32, AR_F)
        outb_t = vt("outb", [128, 2048], F32, AR_F + 8192)

        cfa = es.enter_context(nc.sbuf_tensor("cfa", [128, CF32W], F32))
        cfa_base = nc.lookup_mloc(cfa).addr
        lgb_s = nc.alloc_sbuf_tensor_at("lgb_s", [128, 2 * CCH], F32, offset=cfa_base)
        b1_s = nc.alloc_sbuf_tensor_at("b1_s", [128, E * FCH], F32,
                                       offset=cfa_base + 8 * CCH)
        rbc_s = nc.alloc_sbuf_tensor_at("rbc_s", [128, E], F32,
                                        offset=cfa_base + 8 * CCH + 4 * E * FCH)
        alibi_s = nc.alloc_sbuf_tensor_at("alibi_s", [128, acols], F32,
                                          offset=cfa_base + 8 * CCH + 4 * E * FCH + 4 * E)
        cba = es.enter_context(nc.sbuf_tensor("cba", [128, CB16W], BF16))
        cba_base = nc.lookup_mloc(cba).addr
        masks_s = nc.alloc_sbuf_tensor_at("masks_s", [128, 4 * 512], BF16, offset=cba_base)
        bcon_s = nc.alloc_sbuf_tensor_at("bcon_s", [128, 128], BF16,
                                         offset=cba_base + 2 * 4 * 512)
        ident_s = es.enter_context(nc.sbuf_tensor("ident_s", [128, 128], BF16))
        rW_s = es.enter_context(nc.sbuf_tensor("rW_s", [128, CCH * E], BF16))
        b2_s = es.enter_context(nc.sbuf_tensor("b2_s", [8, C], BF16))
        wT_s = es.enter_context(nc.sbuf_tensor("wT_s", [8, T], BF16))
        recb_s = es.enter_context(nc.sbuf_tensor("recb_s", [128, 2048], BF16))
        rtmp_s = es.enter_context(nc.sbuf_tensor("rtmp_s", [128, 512], F32))
        stg_s = es.enter_context(nc.sbuf_tensor("stg_s", [128, 2048], BF16))
        wS = es.enter_context(nc.sbuf_tensor("wS", [128, TC * E], F32))
        stP = es.enter_context(nc.sbuf_tensor("stP", [128, 128], F32))
        st3 = es.enter_context(nc.sbuf_tensor("st3", [128, 128], F32))
        lgS = es.enter_context(nc.sbuf_tensor("lgS", [128, 128], F32))
        stF = es.enter_context(nc.sbuf_tensor("stF", [128, 64], F32))
        epsv = es.enter_context(nc.sbuf_tensor("epsv", [128, 1], F32))
        wbf_t = es.enter_context(nc.sbuf_tensor("wbf_t", [128, 8 * 16], BF16))
        wtscr = es.enter_context(nc.sbuf_tensor("wtscr", [128, 128], BF16))

        PB = [es.enter_context(nc.psum_tensor("pb%d" % i, [128, 512], F32)) for i in range(6)]
        PT = [es.enter_context(nc.psum_tensor("pt%d" % i, [128, 1024], BF16)) for i in range(2)]

        sTE = Sem(nc, es, "sTE")
        sACT = Sem(nc, es, "sACT")
        sDVE = Sem(nc, es, "sDVE")
        sGD = Sem(nc, es, "sGD")
        sSD = Sem(nc, es, "sSD")

        S = Sched()
        block = es.enter_context(nc.Block())

        wr = {}
        rd = {}

        def bank_w(bid):
            return [rd[bid]] if bid in rd else []

        def bank_r(bid):
            return [wr[bid]] if bid in wr else []

        def vgp(pair):
            if pair < NPAIR_D:
                return vgD[:, pair * RW:(pair + 1) * RW]
            p = pair - NPAIR_D
            return vgF[:, p * RW:(p + 1) * RW]

        S.add("sync", lambda e: e.dma_start(out=ident_s[:, :], in_=ident_in[:, :]),
              inc=sSD, amt=16)
        SD_IDENT = sSD.n
        S.add("sync", lambda e: e.dma_start(
            out=xpp[:, 0:2048].rearrange("p (two q) -> p two q", two=2),
            in_=x_in[0:256, :].rearrange("(two p) c -> p two c", p=128)), inc=sSD, amt=16)
        SD_X0 = sSD.n
        S.add("gpsimd", lambda e: e.dma_start(
            out=rW_s[:, :].rearrange("p (cc e) -> p cc e", cc=CCH),
            in_=rWv.rearrange("(cc p) e -> p cc e", p=128)), inc=sGD, amt=16)
        S.add("gpsimd", lambda e: e.dma_start(out=b2_s[:, :], in_=b2v[:, :]), inc=sGD, amt=16)
        GD_SMALL = sGD.n
        S.add("vector", lambda e: e.memset(epsv[:, :], EPS), inc=sDVE)
        S.add("vector", lambda e: e.memset(recb_s[:, :], 0.0), inc=sDVE)
        S.add("vector", lambda e: e.memset(rtmp_s[:, :], 0.0), inc=sDVE)
        DVE_INIT = sDVE.n
        def vg_piece(p0, p1):
            def f(e, p0=p0, p1=p1):
                e.drain()
                e.memset(vgD[:, p0 * RW: p1 * RW], 0.0)
                e.drain()
                for pair in range(p0, p1):
                    e.memset(vgp(pair)[:, 64:65], 1.0)
                e.drain()
                return e.engine_nop()
            return f
        INIT_PIECES = [vg_piece(p, min(p + 13, NPAIR_D)) for p in range(0, NPAIR_D, 13)]

        def w_half_load(Wm, hf):
            def f(e, Wm=Wm, hf=hf, slot=wsl[hf % 2]):
                return e.dma_start(
                    out=slot[:, :].rearrange("p (cc d) -> p cc d", cc=CCH),
                    in_=Wm[:, hf * 512:(hf + 1) * 512].rearrange("(cc p) d -> p cc d", p=128))
            return f

        GD_W = {}
        S.add("gpsimd", w_half_load(Wq, 0), inc=sGD, amt=16); GD_W[(0, 0)] = sGD.n
        S.add("gpsimd", w_half_load(Wq, 1), inc=sGD, amt=16); GD_W[(0, 1)] = sGD.n

        p0_evac = {}

        def qk_part(wi, hf, tp, DST, dcs=(0, 1, 2, 3), defer_evac=False):
            gd = GD_W[(wi, hf)]
            deferred = []
            for dc in dcs:
                bids = [("pb", 2 * (dc % 2)), ("pb", 2 * (dc % 2) + 1)]
                for cc in range(CCH):
                    lw = wsl[hf % 2][:, cc * 512 + dc * 128: cc * 512 + dc * 128 + 128]
                    S.add("tensor", lambda e, lw=lw: e.ldweights(lw),
                          waits=[(sGD, gd), (sDVE, p0_evac[8 * tp + 7])])
                    for half in range(2):
                        ts4 = tp * 2 + half
                        rhs = Trz1[:, cc * T + ts4 * 512: cc * T + ts4 * 512 + 512]
                        te_mm = S.add("tensor", lambda e, b=PB[2 * (dc % 2) + half],
                                      lw=lw, rhs=rhs, cc=cc:
                                      e.matmul(b[:, 0:512], lw, rhs,
                                               start=(cc == 0), stop=(cc == CCH - 1)),
                                      waits=bank_w(bids[half]) if cc == 0 else [], inc=sTE)
                        if cc == CCH - 1:
                            wr[bids[half]] = (sTE, te_mm)

                def ev(dc=dc, bids=bids, hf=hf, tp=tp, wrv=None):
                    wrv = [wr[bids[0]], wr[bids[1]]]
                    for half in range(2):
                        ts4 = tp * 2 + half
                        d = DST[:, (hf * 4 + dc) * T + ts4 * 512:
                                (hf * 4 + dc) * T + ts4 * 512 + 512]
                        d_ev = S.add("vector", lambda e, b=PB[2 * (dc % 2) + half], d=d: (
                            e.drain(), e.tensor_copy(d, b[:, 0:512]))[-1],
                            waits=[wrv[half]], inc=sDVE)
                        rd[bids[half]] = (sDVE, d_ev)
                if defer_evac:
                    deferred.append(ev)
                else:
                    ev()
                    if INIT_PIECES:
                        S.add("vector", INIT_PIECES.pop(0), inc=sDVE)
            return deferred

        p0_norm_act = {}
        p0_tr_te = {}
        p0_sd = {}
        p0_pend = []
        for tc in range(TC):
            xb = xpp[:, (tc % 4) * 1024:(tc % 4) * 1024 + 1024]
            z1 = z1pp[:, (tc % 3) * 1024:(tc % 3) * 1024 + 1024]
            if tc == 0:
                p0_sd[0] = SD_X0
            elif tc % 2 == 0:
                s4 = (tc % 4) * 1024
                p0_sd[tc] = S.add("sync", lambda e, tc=tc, s4=s4: e.dma_start(
                    out=xpp[:, s4:s4 + 2048].rearrange("p (two q) -> p two q", two=2),
                    in_=x_in[tc * 128:(tc + 2) * 128, :].rearrange("(two p) c -> p two c", p=128)),
                    waits=[(sDVE, p0_norm_act.get(tc - 3, 0))], inc=sSD, amt=16)
            sd_x = p0_sd[tc - tc % 2]

            sp = stP[:, (tc % 2) * 64:(tc % 2) * 64 + 64]

            def stats0(e, xb=xb, sp=sp):
                e.drain()
                e.bn_stats(sp[:, 0:6], xb[:, 0:512])
                e.bn_stats(sp[:, 6:12], xb[:, 512:1024])
                e.drain()
                return e.bn_aggr(sp[:, 12:14], sp[:, 0:12])
            dve_st = S.add("vector", stats0,
                           waits=[(sSD, sd_x)], inc=sDVE)

            def rsq0(e, sp=sp):
                e.activation(sp[:, 14:15], sp[:, 13:14], AF.Ln, bias=epsv[:, 0:1], scale=1.0)
                e.drain()
                return e.activation(sp[:, 15:16], sp[:, 14:15], AF.Exp, bias=0.0, scale=-0.5)
            act_rq = S.add("scalar", rsq0, waits=[(sDVE, dve_st)], inc=sACT)

            def nrm0(e, sp=sp, z1=z1, xb=xb):
                e.drain()
                return e.tensor_scalar(z1, xb, sp[:, 12:13], sp[:, 15:16],
                                       ALU.subtract, ALU.mult)
            act_nm = S.add("vector", nrm0,
                           waits=[(sACT, act_rq), (sTE, p0_tr_te.get(tc - 3, 0))], inc=sDVE)
            p0_norm_act[tc] = act_nm

            def p0_tr(tc=tc, z1=z1):
                bid = ("pt", tc % 2)
                pt = PT[tc % 2]
                for cc in range(CCH):
                    te_tr = S.add("tensor", lambda e, z1=z1, cc=cc, pt=pt: e.transpose(
                        pt[:, cc * 128:(cc + 1) * 128], z1[:, cc * 128:(cc + 1) * 128],
                        ident_s[:, :]),
                        waits=[(sDVE, p0_norm_act[tc]), (sSD, SD_IDENT)] +
                              (bank_w(bid) if cc == 0 else []), inc=sTE)
                wr[bid] = (sTE, sTE.n)
                dst = Trz1[:, 0:CCH * T].rearrange(
                    "p (cc t) -> p cc t", cc=CCH)[:, :, tc * 128:(tc + 1) * 128]
                d_ev = S.add("vector", lambda e, pt=pt, dst=dst: (
                    e.drain(), e.tensor_copy(dst, pt[:, 0:1024].rearrange(
                        "p (cc q) -> p cc q", cc=CCH)))[-1],
                    waits=[(sTE, sTE.n)], inc=sDVE)
                rd[bid] = (sDVE, d_ev)
                p0_evac[tc] = d_ev
                p0_tr_te[tc] = sTE.n
            p0_pend.append(p0_tr)
            if len(p0_pend) > 2:
                p0_pend.pop(0)()
            if tc == 9:
                pend_qk_evacs = qk_part(0, 0, 0, qT, dcs=(0, 1), defer_evac=True)
        while p0_pend:
            p0_pend.pop(0)()
        P0_TEN = sTE.n
        P0_ACT = sACT.n
        P0_DVE = sDVE.n
        S.add("sync", lambda e: e.dma_start(out=cfa[:, :], in_=cf32_in[:, :]),
              inc=sSD, amt=16)
        S.add("sync", lambda e: e.dma_start(out=cba[:, :], in_=cb16_in[:, :]),
              inc=sSD, amt=16)
        SD_CONST = sSD.n
        def ones_f(e):
            e.drain()
            e.memset(vgF[:, :], 0.0)
            e.drain()
            for pair in range(NPAIR_D, TC * 8):
                e.memset(vgp(pair)[:, 64:65], 1.0)
            e.drain()
            return e.engine_nop()
        S.add("vector", ones_f, waits=[(sDVE, P0_DVE), (sSD, p0_sd[TC - 2])], inc=sDVE)
        for f in pend_qk_evacs:
            f()
        qk_part(0, 0, 0, qT, dcs=(2, 3))

        qk_part(0, 1, 0, qT)
        qk_part(0, 0, 1, qT)
        S.add("gpsimd", w_half_load(Wk, 0), waits=[(sTE, sTE.n)], inc=sGD, amt=16)
        GD_W[(1, 0)] = sGD.n
        qk_part(0, 1, 1, qT)
        S.add("gpsimd", w_half_load(Wk, 1), waits=[(sTE, sTE.n)], inc=sGD, amt=16)
        GD_W[(1, 1)] = sGD.n
        qk_part(1, 0, 0, kT)
        qk_part(1, 1, 0, kT)
        qk_part(1, 0, 1, kT)
        S.add("gpsimd", w_half_load(Wv, 0), waits=[(sTE, sTE.n)], inc=sGD, amt=16)
        GD_W[(2, 0)] = sGD.n
        qk_part(1, 1, 1, kT)
        S.add("gpsimd", w_half_load(Wv, 1), waits=[(sTE, sTE.n)], inc=sGD, amt=16)
        GD_W[(2, 1)] = sGD.n
        P1_ACT_QK = sACT.n

        for tc in range(TC):
            for ds in range(2):
                gd = GD_W[(2, ds)]
                bid = ("pb", 4 + ds)
                for cc in range(CCH):
                    lw = Trz1[:, cc * T + tc * 128: cc * T + tc * 128 + 128]
                    S.add("tensor", lambda e, lw=lw: e.ldweights(lw), waits=[(sGD, gd)])
                    rhs = wsl[ds % 2][:, cc * 512: cc * 512 + 512]
                    te_mm = S.add("tensor", lambda e, b=PB[4 + ds], lw=lw, rhs=rhs, cc=cc:
                                  e.matmul(b[:, 0:512], lw, rhs,
                                           start=(cc == 0), stop=(cc == CCH - 1)),
                                  waits=bank_w(bid) if cc == 0 else [],
                                  inc=sTE)
                wr[bid] = (sTE, te_mm)

                def vevac(e, b=PB[4 + ds], tc=tc, ds=ds):
                    from concourse.ap import AP
                    e.drain()
                    pair0 = tc * 8 + ds * 4
                    if pair0 + 4 <= NPAIR_D:
                        base = vgD.ap()
                        part = list(base.ap[0])
                        dst = AP(tensor=base.tensor, offset=base.offset + pair0 * RW,
                                 ap=[part, [RW, 4], [96, 2], [1, 64]])
                        e.tensor_copy(dst, b[:, 0:512].rearrange(
                            "p (pr two d) -> p pr two d", pr=4, two=2))
                    else:
                        for j in range(8):
                            h = ds * 8 + j
                            pr = tc * 8 + h // 2
                            off = 0 if h % 2 == 0 else 96
                            e.tensor_copy(vgp(pr)[:, off:off + 64], b[:, j * 64:(j + 1) * 64])
                    e.drain()
                    return e.engine_nop()
                d_ev = S.add("vector", vevac, waits=[wr[bid]], inc=sDVE)
                rd[bid] = (sDVE, d_ev)
        P1_TE = sTE.n
        P1_DVE = sDVE.n
        S.add("gpsimd", w_half_load(Wo, 0), waits=[(sTE, P1_TE)], inc=sGD, amt=16)
        GD_WO0 = sGD.n
        S.add("gpsimd", w_half_load(Wo, 1), waits=[(sTE, P1_TE)], inc=sGD, amt=16)
        GD_WO1 = sGD.n

        slab_free = [0, 0, 0, 0]
        stg_free = [0, 0, 0, 0]
        gi = 0
        pend_evac = [None, None]
        for pi in range(NH // 2):
            qw = QW[2 * pi]
            dc = pi
            for qb in range(T // qw):
                qs = qb * qw
                qe = qs + qw - 1
                K = qe // 128 + 1
                gpar = gi % 2
                pv_bid = {0: ("pb", 2 + 2 * gpar), 1: ("pb", 3 + 2 * gpar)}
                pvb_i = {0: 2 + 2 * gpar, 1: 3 + 2 * gpar}
                exp_done = {}
                mask_done = {}
                if pend_evac[0] is not None:
                    for f in pend_evac[0][0]:
                        f(d_rc=pend_evac[0][1][f])
                    pend_evac[0] = None
                if pend_evac[1] is not None:
                    rfs, efs = pend_evac[1]
                    drcs = {}
                    for rf, ef in zip(rfs, efs):
                        drcs[ef] = rf()
                    pend_evac[0] = (efs, drcs)
                    pend_evac[1] = None

                def emit_score(kb, hp):
                    h = 2 * pi + hp
                    sbi = hp
                    s_bid = ("pb", sbi)
                    r0 = hp * 64
                    tile = None if hp == 0 else (64, 0)
                    lwk = kT[r0:r0 + 64, dc * T + kb * 128: dc * T + kb * 128 + 128]
                    qrhs = qT[r0:r0 + 64, dc * T + qs: dc * T + qs + qw]
                    S.add("tensor", lambda e, lwk=lwk, tile=tile: e.ldweights(lwk, tile_position=tile),
                          waits=[(sACT, P1_ACT_QK)] + bank_w(s_bid))
                    te_s = S.add("tensor", lambda e, sb=PB[sbi], lwk=lwk, qrhs=qrhs, tile=tile,
                                 qw=qw: e.matmul(sb[:, 0:qw], lwk, qrhs, start=True, stop=True,
                                                 tile_position=tile), inc=sTE)
                    wr[s_bid] = (sTE, te_s)
                    slot = (kb % 2) * 2 + hp
                    slab = slabs[:, slot * 512: slot * 512 + qw]
                    ac = amap[(h, qb, kb)]
                    bias = alibi_s[:, ac:ac + 1]
                    a_ex = S.add("scalar", lambda e, slab=slab, sb=PB[sbi], bias=bias, qw=qw:
                                 e.activation(slab, sb[:, 0:qw], AF.Exp, bias=bias, scale=0.125),
                                 waits=[(sTE, te_s), (sTE, slab_free[slot]), (sSD, SD_CONST)],
                                 inc=sACT)
                    rd[s_bid] = (sACT, a_ex)
                    exp_done[(kb, hp)] = a_ex
                    if kb * 128 + 127 > qs:
                        oi = (kb * 128 - qs) // 128
                        d_mk = S.add("vector", lambda e, slab=slab, oi=oi, qw=qw: (
                            e.drain(), e.tensor_mul(slab, slab,
                                                    masks_s[:, oi * 512:oi * 512 + qw]))[-1],
                            waits=[(sACT, a_ex), (sSD, SD_CONST)], inc=sDVE)
                        mask_done[(kb, hp)] = d_mk

                def emit_pv(kb, hp):
                    slot = (kb % 2) * 2 + hp
                    slab = slabs[:, slot * 512: slot * 512 + qw]
                    pv_w = ([(sDVE, mask_done[(kb, hp)])] if (kb, hp) in mask_done
                            else [(sACT, exp_done[(kb, hp)])])
                    pr = kb * 8 + pi
                    stat = vgp(pr)[:, 0:65] if hp == 0 else vgp(pr)[:, 32:160]
                    if kb == 0:
                        pv_w += bank_w(pv_bid[hp]) + [(sDVE, P1_DVE)]
                    mtop = 65 if hp == 0 else 128
                    S.add("tensor", lambda e, stat=stat: e.ldweights(stat), waits=pv_w)
                    te_pv = S.add("tensor", lambda e, pvb=PB[pvb_i[hp]], stat=stat, slab=slab,
                                  kb=kb, K=K, mtop=mtop, qw=qw:
                                  e.matmul(pvb[0:mtop, 0:qw], stat, slab,
                                           start=(kb == 0), stop=(kb == K - 1)),
                                  inc=sTE)
                    slab_free[slot] = te_pv
                    return te_pv

                te_pv_d = {}
                for kb in range(K):
                    emit_score(kb, 0)
                    emit_score(kb, 1)
                    if kb >= 1:
                        te_pv_d[0] = emit_pv(kb - 1, 0)
                        te_pv_d[1] = emit_pv(kb - 1, 1)
                te_pv_d[0] = emit_pv(K - 1, 0)
                te_pv_d[1] = emit_pv(K - 1, 1)
                wr[pv_bid[0]] = (sTE, te_pv_d[0])
                wr[pv_bid[1]] = (sTE, te_pv_d[1])

                rfs = []
                efs = []
                for hp in range(2):
                    h = 2 * pi + hp
                    drow = 64 if hp == 0 else 32
                    rslot = gpar * 2 + hp

                    def emit_recip(pvbi=pvb_i[hp], drow=drow, rslot=rslot, qw=qw):
                        def recip_fn(e, pvb=PB[pvbi], drow=drow, rslot=rslot, qw=qw):
                            e.drain()
                            with nc.allow_low_precision(
                                    reason="bf16 store of softmax reciprocal row; "
                                           "same rounding as fp32-then-cast"):
                                return e.reciprocal(
                                    recb_s[drow:drow + 1, rslot * 512: rslot * 512 + qw],
                                    pvb[drow:drow + 1, 0:qw])
                        return S.add("vector", recip_fn,
                                     waits=[(sTE, te_pv_d[hp]), (sDVE, DVE_INIT)], inc=sDVE)

                    def emit_evac(pvbi=pvb_i[hp], pvbid=pv_bid[hp], te_pv=te_pv_d[hp],
                                  hp=hp, drow=drow, rslot=rslot, qw=qw, dc=dc, qs=qs,
                                  d_rc=0):
                        r0 = hp * 64
                        tileb = (64, 0) if drow == 64 else (32, 0)
                        stg = stg_s[:, rslot * 512: rslot * 512 + qw]
                        a_st = S.add("scalar", lambda e, stg=stg, pvb=PB[pvbi], r0=r0, qw=qw:
                                     e.copy(stg[r0:r0 + 64, :], pvb[r0:r0 + 64, 0:qw]),
                                     waits=[(sTE, te_pv), (sDVE, stg_free[rslot])], inc=sACT)
                        S.add("tensor", lambda e, tileb=tileb, drow=drow: e.ldweights(
                            bcon_s[drow:drow + 32, :], tile_position=tileb),
                            waits=[(sDVE, d_rc), (sSD, SD_CONST), (sACT, a_st)])
                        te_bc = S.add("tensor", lambda e, bcb=PB[pvbi], tileb=tileb, drow=drow,
                                      rslot=rslot, qw=qw:
                                      e.matmul(bcb[:, 0:qw], bcon_s[drow:drow + 32, :],
                                               recb_s[drow:drow + 32,
                                                      rslot * 512: rslot * 512 + qw],
                                               start=True, stop=True, tile_position=tileb),
                                      inc=sTE)
                        dsta = TrAttn[r0:r0 + 64, dc * T + qs: dc * T + qs + qw]
                        d_ml = S.add("vector", lambda e, dsta=dsta, stg=stg, bcb=PB[pvbi],
                                     r0=r0, qw=qw: (
                            e.drain(), e.tensor_mul(dsta, stg[r0:r0 + 64, :],
                                                    bcb[r0:r0 + 64, 0:qw]))[-1],
                            waits=[(sTE, te_bc), (sACT, a_st), (sTE, P1_TE)], inc=sDVE)
                        rd[pvbid] = (sDVE, d_ml)
                        stg_free[rslot] = d_ml
                    rfs.append(emit_recip)
                    efs.append(emit_evac)
                pend_evac[1] = (rfs, efs)
                gi += 1
        if pend_evac[1] is not None:
            rfs, efs = pend_evac[1]
            drcs = {}
            for rf, ef in zip(rfs, efs):
                drcs[ef] = rf()
            if pend_evac[0] is not None:
                for f in pend_evac[0][0]:
                    f(d_rc=pend_evac[0][1][f])
            pend_evac[0] = (efs, drcs)
            pend_evac[1] = None
        if pend_evac[0] is not None:
            for f in pend_evac[0][0]:
                f(d_rc=pend_evac[0][1][f])
        P2_TE = sTE.n
        P2_DVE = sDVE.n
        P2_ACT = sACT.n

        def acc_ch(i):
            if i < 4:
                return accA[:, i * 1024:(i + 1) * 1024]
            return accB[:, (i - 4) * 1024:(i - 4) * 1024 + 1024]

        fsl = [fsl0, fsl1, fsl2]
        gd_half = {}
        half_slot = {}
        _gctr = [0]

        gd_half_a = {}

        def ffn_w_load(th, e_, kind, hf, te_guard=None, extra=(), slot=None,
                       te_guard2=None):
            if th >= 2:
                return
            if slot is None:
                slot = _gctr[0] % 3
            _gctr[0] += 1
            half_slot[(th, e_, kind, hf)] = slot

            def f(e, e_=e_, kind=kind, hf=hf, slot=slot, sub=None):
                dst = fsl[slot][:, :].rearrange("p (a b) -> p a b", a=8)
                if kind == 1:
                    src = W1[e_, :, hf * 1024:(hf + 1) * 1024].rearrange(
                        "(cc p) f -> p cc f", p=128)
                else:
                    src = W2[e_, hf * 1024:(hf + 1) * 1024, :].rearrange(
                        "(fc p) h -> p fc h", p=128)
                if sub is not None:
                    dst = dst[:, :, sub * 512:(sub + 1) * 512]
                    src = src[:, :, sub * 512:(sub + 1) * 512]
                return e.dma_start(out=dst, in_=src)
            if te_guard2 is not None:
                S.add("gpsimd", lambda e: f(e, sub=0),
                      waits=[(sTE, te_guard)] + list(extra), inc=sGD, amt=16)
                gd_half_a[(th, e_, kind, hf)] = sGD.n
                S.add("gpsimd", lambda e: f(e, sub=1),
                      waits=[(sTE, te_guard2)], inc=sGD, amt=16)
            else:
                S.add("gpsimd", f,
                      waits=[(sTE, sTE.n if te_guard is None else te_guard)] + list(extra),
                      inc=sGD, amt=16)
            gd_half[(th, e_, kind, hf)] = sGD.n


        ffn_w_load(0, 0, 1, 0, te_guard=P2_TE, slot=0)
        ffn_w_load(0, 0, 2, 0, te_guard=P2_TE, slot=2)
        wo_last_te = 0
        trz_act = {}
        xu_spill_sd = {}
        zp_te = {}
        w_done_dve = {}
        zp_act = {}

        def stats_fn(buf, so, stt):
            def f(e, buf=buf, so=so, stt=stt):
                e.drain()
                e.bn_stats(stt[:, so:so + 6], buf[:, 0:512])
                e.bn_stats(stt[:, so + 6:so + 12], buf[:, 512:1024])
                e.drain()
                return e.bn_aggr(stt[:, so + 12:so + 14], stt[:, so:so + 12])
            return f

        def rsq_fn(so, stt, out=None, src=None, short=False):
            def f(e, so=so, stt=stt, out=out, src=src, short=short):
                e.activation(stt[:, so + 14:so + 15], stt[:, so + 13:so + 14], AF.Ln,
                             bias=epsv[:, 0:1], scale=1.0)
                e.drain()
                r = e.activation(stt[:, so + 15:so + 16], stt[:, so + 14:so + 15], AF.Exp,
                                 bias=0.0, scale=-0.5)
                if short:
                    return r
                e.drain()
                e.activation(stt[:, so + 16:so + 17], stt[:, so + 12:so + 13], AF.Copy,
                             bias=0.0, scale=-1.0)
                e.drain()
                r = e.activation(stt[:, so + 17:so + 18], stt[:, so + 16:so + 17], AF.Copy,
                                 bias=0.0, scale=stt[:, so + 15:so + 16])
                if out is not None:
                    e.drain()
                    r = e.activation(out, src, AF.Identity, bias=stt[:, so + 17:so + 18],
                                     scale=stt[:, so + 15:so + 16])
                return r
            return f

        x0_sd = {}
        d_x1_d = {}
        x1sp_sd = {}
        usp_sd = {}
        a_ev_d = {}
        r1_d = {}
        r2u_d = {}

        def p3_A1(tc):
            sl3 = (tc % 3) * 1024
            x0_sd[tc] = S.add("sync", lambda e, tc=tc, sl3=sl3: e.dma_start(
                out=x0_t[:, sl3:sl3 + 1024], in_=x_in[tc * 128:(tc + 1) * 128, :]),
                waits=[(sTE, P2_TE), (sDVE, d_x1_d.get(tc - 3, 0))], inc=sSD, amt=16)
            bids = [("pb", 0), ("pb", 1)]
            for cc in range(CCH):
                lw = TrAttn[:, cc * T + tc * 128: cc * T + tc * 128 + 128]
                S.add("tensor", lambda e, lw=lw: e.ldweights(lw),
                      waits=[(sDVE, P2_DVE), (sGD, GD_WO1)] +
                            (bank_w(bids[0]) + bank_w(bids[1]) if cc == 0 else []))
                for cs in range(2):
                    rhs = wsl[cs][:, cc * 512: cc * 512 + 512]
                    te_mm = S.add("tensor", lambda e, b=PB[cs], lw=lw, rhs=rhs, cc=cc:
                                  e.matmul(b[:, 0:512], lw, rhs,
                                           start=(cc == 0), stop=(cc == CCH - 1)), inc=sTE)
            wr[bids[0]] = (sTE, te_mm)
            wr[bids[1]] = (sTE, te_mm)
            x1b = x1_t[:, sl3:sl3 + 1024]

            def wo_ev(e, x1b=x1b):
                e.drain()
                e.copy(x1b[:, 0:512], PB[0][:, 0:512])
                e.drain()
                return e.copy(x1b[:, 512:1024], PB[1][:, 0:512])
            a_ev = S.add("scalar", wo_ev,
                         waits=[(sTE, te_mm), (sSD, x1sp_sd.get(tc - 3, 0))], inc=sACT)
            a_ev_d[tc] = a_ev
            rd[bids[0]] = (sACT, a_ev)
            rd[bids[1]] = (sACT, a_ev)
            return te_mm

        def p3_A2(tc):
            par = tc % 2
            sl3 = (tc % 3) * 1024
            x1b = x1_t[:, sl3:sl3 + 1024]
            st = st3[:, par * 64: par * 64 + 64]
            d_s1 = S.add("vector", stats_fn(x1b, 0, st), waits=[(sACT, a_ev_d[tc])], inc=sDVE)
            r1_d[tc] = S.add("scalar", rsq_fn(0, st, short=True), waits=[(sDVE, d_s1)], inc=sACT)

        def p3_A3(tc):
            par = tc % 2
            sl3 = (tc % 3) * 1024
            x1b = x1_t[:, sl3:sl3 + 1024]
            x0v = x0_t[:, sl3:sl3 + 1024]
            ub = u_t[:, (tc % 4) * 1024:(tc % 4) * 1024 + 1024]
            st = st3[:, par * 64: par * 64 + 64]

            def x1_fn(e, x1b=x1b, x0v=x0v, st=st):
                e.drain()
                e.tensor_scalar(x1b, x1b, st[:, 12:13], st[:, 15:16], ALU.subtract, ALU.mult)
                e.drain()
                return e.tensor_add(x1b, x1b, x0v)
            d_x1 = S.add("vector", x1_fn, waits=[(sACT, r1_d[tc]), (sSD, x0_sd[tc])], inc=sDVE)
            d_x1_d[tc] = d_x1
            x1sp_sd[tc] = S.add("sync", lambda e, x1b=x1b, tc=tc: e.dma_start(
                out=xuscr[tc * 128:(tc + 1) * 128, 1024:2048], in_=x1b),
                waits=[(sDVE, d_x1)], inc=sSD, amt=16)
            d_s2 = S.add("vector", stats_fn(x1b, 18, st), inc=sDVE)
            r2u_d[tc] = S.add("scalar", rsq_fn(18, st, out=ub, src=x1b),
                              waits=[(sDVE, d_s2), (sSD, usp_sd.get(tc - 3, 0))], inc=sACT)

        def p3_A4(tc):
            par = tc % 2
            ub = u_t[:, (tc % 4) * 1024:(tc % 4) * 1024 + 1024]
            st = st3[:, par * 64: par * 64 + 64]
            d_s3 = S.add("vector", stats_fn(ub, 36, st), waits=[(sACT, r2u_d[tc])], inc=sDVE)
            zp = zp_t[:, par * 1024: par * 1024 + 1024]
            a_r3 = S.add("scalar", rsq_fn(36, st, out=zp, src=ub),
                         waits=[(sDVE, d_s3), (sTE, zp_te.get(tc - 2, 0))], inc=sACT)
            zp_act[tc] = a_r3
            usp_sd[tc] = S.add("sync", lambda e, tc=tc, ub=ub: e.dma_start(
                out=xuscr[tc * 128:(tc + 1) * 128, 0:1024], in_=ub),
                waits=[(sACT, r2u_d[tc])], inc=sSD, amt=16)
            xu_spill_sd[tc] = usp_sd[tc]

        def p3_B1(tc):
            par = tc % 2
            zp = zp_t[:, par * 1024: par * 1024 + 1024]
            bid = ("pt", tc % 2)
            pt = PT[tc % 2]
            for cc in range(CCH):
                te_tr = S.add("tensor", lambda e, zp=zp, cc=cc, pt=pt: e.transpose(
                    pt[:, cc * 128:(cc + 1) * 128], zp[:, cc * 128:(cc + 1) * 128],
                    ident_s[:, :]),
                    waits=[(sACT, zp_act[tc])] + (bank_w(bid) if cc == 0 else []), inc=sTE)
            wr[bid] = (sTE, sTE.n)
            for cc in range(CCH):
                dst = Trz[:, cc * T + tc * 128: cc * T + tc * 128 + 128]
                a_tz = S.add("scalar", lambda e, pt=pt, dst=dst, cc=cc: e.activation(
                    dst, pt[:, cc * 128:(cc + 1) * 128], AF.Identity,
                    bias=lgb_s[:, CCH + cc:CCH + cc + 1],
                    scale=lgb_s[:, cc:cc + 1]),
                    waits=[(sTE, sTE.n), (sTE, P2_TE), (sSD, SD_CONST)], inc=sACT)
            rd[bid] = (sACT, sACT.n)
            zp_te[tc] = sTE.n
            trz_act[tc] = sACT.n

        def p3_B23(tc):
            par = tc % 2
            lg = lgS[:, par * 64: par * 64 + 64]
            lg_bid = ("pb", 2)
            for cc in range(CCH):
                lw = Trz[:, cc * T + tc * 128: cc * T + tc * 128 + 128]
                S.add("tensor", lambda e, lw=lw: e.ldweights(lw),
                      waits=[(sACT, trz_act[tc]), (sGD, GD_SMALL)] +
                            (bank_w(lg_bid) if cc == 0 else []))
                te_lg = S.add("tensor", lambda e, lw=lw, cc=cc:
                              e.matmul(PB[2][:, 0:E], lw, rW_s[:, cc * E:(cc + 1) * E],
                                       start=(cc == 0), stop=(cc == CCH - 1)), inc=sTE)
            wr[lg_bid] = (sTE, te_lg)

            def rt1(e, lg=lg):
                e.drain()
                e.tensor_add(lg[:, 0:8], PB[2][:, 0:E], rbc_s[:, :])
                e.drain()
                return e.reduce_max(lg[:, 8:9], lg[:, 0:8], axis=AX, negate=True)
            d_r1 = S.add("vector", rt1, waits=[(sTE, te_lg), (sSD, SD_CONST)], inc=sDVE)
            rd[lg_bid] = (sDVE, d_r1)
            a_re_d[tc] = S.add("scalar", lambda e, lg=lg: e.activation(
                lg[:, 9:17], lg[:, 0:8], AF.Exp, bias=lg[:, 8:9], scale=1.0),
                waits=[(sDVE, d_r1)], inc=sACT)

        def p3_B3(tc):
            par = tc % 2
            lg = lgS[:, par * 64: par * 64 + 64]
            wv = wS[:, tc * E:(tc + 1) * E]
            wb = wbf_t[:, (tc % 16) * 8:(tc % 16) * 8 + 8]

            def rt2(e, wv=wv, wb=wb, lg=lg):
                ex = lg[:, 9:17]
                e.drain()
                e.reduce_max(lg[:, 17:18], ex, axis=AX)
                e.drain()
                e.tensor_scalar(lg[:, 18:26], ex, lg[:, 17:18], None, ALU.is_equal)
                e.drain()
                e.tensor_tensor(lg[:, 26:34], ex, lg[:, 18:26], ALU.mult)
                e.drain()
                e.tensor_sub(lg[:, 34:42], ex, lg[:, 26:34])
                e.drain()
                e.reduce_max(lg[:, 42:43], lg[:, 34:42], axis=AX)
                e.drain()
                e.tensor_scalar(lg[:, 43:51], ex, lg[:, 42:43], None, ALU.is_ge)
                e.drain()
                e.tensor_tensor(lg[:, 51:59], ex, lg[:, 43:51], ALU.mult)
                e.drain()
                e.tensor_add(lg[:, 59:60], lg[:, 17:18], lg[:, 42:43])
                e.drain()
                e.tensor_scalar(lg[:, 60:61], lg[:, 59:60], 1e-8, None, ALU.max)
                e.drain()
                e.reciprocal(lg[:, 61:62], lg[:, 60:61])
                e.drain()
                e.tensor_scalar(wv, lg[:, 51:59], lg[:, 61:62], None, ALU.mult)
                e.drain()
                return e.tensor_copy(wb, wv)
            d_w = S.add("vector", rt2, waits=[(sACT, a_re_d[tc])], inc=sDVE)
            w_done_dve[tc] = d_w

        a_re_d = {}
        for i in range(TC + 8):
            if 0 <= i - 7 < TC:
                p3_B3(i - 7)
            if 0 <= i - 6 < TC:
                p3_B23(i - 6)
            if 0 <= i - 5 < TC:
                p3_B1(i - 5)
            if i < TC:
                wo_last_te = p3_A1(i)
            if 0 <= i - 1 < TC:
                p3_A2(i - 1)
            if 0 <= i - 2 < TC:
                p3_A3(i - 2)
            if 0 <= i - 3 < TC:
                p3_A4(i - 3)
        ffn_w_load(0, 0, 1, 1, te_guard=P2_TE, slot=1,
                   extra=[(sSD, usp_sd[TC - 1]), (sDVE, sDVE.n)])
        P3_TE = sTE.n
        P3_DVE = sDVE.n
        P3_SD = sSD.n

        acc_reload_sd = {}
        acc_b2d = {}
        final_read_dve = {}
        u_rel_guard = []

        bidw = ("pt", 0)
        W8 = TC * 8
        te_wt = S.add("tensor", lambda e: e.transpose(
            PT[0][0:W8, 0:128], wbf_t[:, 0:W8], ident_s[:, :]),
            waits=[(sDVE, w_done_dve[TC - 1]), (sTE, P3_TE)] + bank_w(bidw), inc=sTE)
        wr[bidw] = (sTE, te_wt)
        a_wt = S.add("scalar", lambda e: e.copy(wtscr[0:W8, 0:128], PT[0][0:W8, 0:128]),
                     waits=[(sTE, te_wt)], inc=sACT)
        rd[bidw] = (sACT, a_wt)
        wt_sd = {}

        def emit_acc_chunk(th, i):
            tc = th * THC + i
            wl = [(sSD, P3_SD), (sDVE, P3_DVE)]
            if i < 4:
                wl.append((sTE, P3_TE))
            if th == 1:
                wl.append((sDVE, final_read_dve.get((0, i), 0)))
            acc_reload_sd[(th, i)] = S.add("sync", lambda e, tc=tc, i=i: e.dma_start(
                out=acc_ch(i), in_=xuscr[tc * 128:(tc + 1) * 128, 0:1024]), waits=wl,
                inc=sSD, amt=16)
            wt_sd[tc] = S.add("sync", lambda e, tc=tc: e.dma_start(
                out=wT_s[0:8, tc * 128:(tc + 1) * 128], in_=wtscr[tc * 8:tc * 8 + 8, 0:128]),
                waits=[(sACT, a_wt)], inc=sSD, amt=16)
            b2_bids = [("pb", 4), ("pb", 5)]
            S.add("tensor", lambda e, tc=tc: e.ldweights(wT_s[0:8, tc * 128:(tc + 1) * 128]),
                  waits=[(sSD, wt_sd[tc]), (sGD, GD_SMALL)] +
                        bank_w(b2_bids[0]) + bank_w(b2_bids[1]))
            for hs in range(2):
                te_b2 = S.add("tensor", lambda e, tc=tc, hs=hs:
                              e.matmul(PB[4 + hs][:, 0:512],
                                       wT_s[0:8, tc * 128:(tc + 1) * 128],
                                       b2_s[0:8, hs * 512:(hs + 1) * 512],
                                       start=True, stop=True), inc=sTE)
                wr[b2_bids[hs]] = (sTE, te_b2)

            def b2add(e, i=i):
                ac = acc_ch(i)
                e.drain()
                e.tensor_add(ac[:, 0:512], ac[:, 0:512], PB[4][:, 0:512])
                e.drain()
                return e.tensor_add(ac[:, 512:1024], ac[:, 512:1024], PB[5][:, 0:512])
            d_b2 = S.add("vector", b2add,
                         waits=[(sTE, te_b2), (sSD, acc_reload_sd[(th, i)])], inc=sDVE)
            rd[b2_bids[0]] = (sDVE, d_b2)
            rd[b2_bids[1]] = (sDVE, d_b2)
            acc_b2_dve = d_b2
            acc_reload_sd[(th, i)] = acc_reload_sd[(th, i)]
            acc_b2d[(th, i)] = acc_b2_dve

        gelu_act = {}
        final_store_sd = {}

        def emit_final(th, i):
            tc = th * THC + i
            par = i % 2
            so = par * 32
            x1r = x1r_t[:, par * 1024: par * 1024 + 1024]
            ob = outb_t[:, par * 1024: par * 1024 + 1024]
            if par == 0:
                wl = [(sDVE, final_read_dve.get((th, i - 2), 0)),
                      (sDVE, final_read_dve.get((th, i - 1), 0)),
                      (sDVE, final_read_dve.get((th - 1, THC - 2), 0)),
                      (sDVE, final_read_dve.get((th - 1, THC - 1), 0)),
                      (sSD, xu_spill_sd[TC - 1]), (sTE, zp_te.get(TC - 1, 0))]
                S.add("sync", lambda e, tc=tc: e.dma_start(
                    out=x1r_t[:, 0:2048].rearrange("p (two q) -> p two q", two=2),
                    in_=xuscr[tc * 128:(tc + 2) * 128, 1024:2048].rearrange(
                        "(two p) c -> p two c", p=128)), waits=wl, inc=sSD, amt=16)
            sd_r = sSD.n

            def fstat(e, i=i, so=so):
                ac = acc_ch(i)
                e.drain()
                e.bn_stats(stF[:, so:so + 6], ac[:, 0:512])
                e.bn_stats(stF[:, so + 6:so + 12], ac[:, 512:1024])
                e.drain()
                return e.bn_aggr(stF[:, so + 12:so + 14], stF[:, so:so + 12])
            d_fs = S.add("vector", fstat, inc=sDVE)

            def frsq(e, so=so):
                e.activation(stF[:, so + 14:so + 15], stF[:, so + 13:so + 14], AF.Ln,
                             bias=epsv[:, 0:1], scale=1.0)
                e.drain()
                e.activation(stF[:, so + 15:so + 16], stF[:, so + 14:so + 15], AF.Exp,
                             bias=0.0, scale=-0.5)
                e.drain()
                e.activation(stF[:, so + 16:so + 17], stF[:, so + 12:so + 13], AF.Copy,
                             bias=0.0, scale=-1.0)
                e.drain()
                return e.activation(stF[:, so + 17:so + 18], stF[:, so + 16:so + 17], AF.Copy,
                                    bias=0.0, scale=stF[:, so + 15:so + 16])
            a_fr = S.add("scalar", frsq, waits=[(sDVE, d_fs)], inc=sACT)

            def fout(e, i=i, x1r=x1r, ob=ob, so=so):
                ac = acc_ch(i)
                e.drain()
                e.scalar_tensor_tensor(ob, ac, stF[:, so + 15:so + 16], x1r, ALU.mult, ALU.add)
                e.drain()
                return e.tensor_scalar(ob, ob, stF[:, so + 17:so + 18], None, ALU.add)
            d_fo = S.add("vector", fout,
                         waits=[(sACT, a_fr), (sSD, sd_r),
                                (sSD, final_store_sd.get((th, i - 2), 0))], inc=sDVE)
            final_read_dve[(th, i)] = d_fo
            if par == 1:
                final_store_sd[(th, i - 1)] = final_store_sd[(th, i)] = S.add(
                    "sync", lambda e, tc=tc: e.dma_start(
                        out=y_out[(tc - 1) * 128:(tc + 1) * 128, :].rearrange(
                            "(two p) c -> p two c", p=128),
                        in_=outb_t[:, 0:2048].rearrange("p (two q) -> p two q", two=2)),
                    waits=[(sDVE, final_read_dve[(th, i - 1)]), (sDVE, d_fo)],
                    inc=sSD, amt=16)

        for th in range(2):
            for e_ in range(E):
                nth, ne = (th, e_ + 1) if e_ < E - 1 else (th + 1, 0)
                HL = T // 2
                n_tsl = max(1, HL // 512)
                tw = min(512, HL)
                w1_fc_te = {}
                for fc in range(FCH):
                    hf = fc // 8
                    k1 = (th, e_, 1, hf)
                    gd = gd_half_a[k1] if (fc % 8 < 4 and k1 in gd_half_a) else gd_half[k1]
                    slot = half_slot[k1]
                    bk = [2 * (fc % 2), 2 * (fc % 2) + 1]
                    w1_bids = [("pb", bk[0]), ("pb", bk[1])]
                    for cc in range(CCH):
                        lw = fsl[slot][:, cc * 1024 + (fc % 8) * 128:
                                       cc * 1024 + (fc % 8) * 128 + 128]
                        S.add("tensor", lambda e, lw=lw: e.ldweights(lw),
                              waits=[(sGD, gd), (sACT, trz_act.get(th * THC + THC - 1, 0))] +
                                    (bank_w(w1_bids[0]) + bank_w(w1_bids[1]) if cc == 0 else []))
                        for tsl in range(n_tsl):
                            rhs = Trz[:, cc * T + th * HL + tsl * tw:
                                      cc * T + th * HL + tsl * tw + tw]
                            te_mm = S.add("tensor", lambda e, b=PB[bk[tsl]], lw=lw, rhs=rhs, cc=cc,
                                          tw=tw: e.matmul(b[:, 0:tw], lw, rhs,
                                                          start=(cc == 0), stop=(cc == CCH - 1)),
                                          inc=sTE)
                    wr[w1_bids[0]] = (sTE, te_mm)
                    wr[w1_bids[1]] = (sTE, te_mm)
                    for tsl in range(n_tsl):
                        dst = hidT[:, fc * HL + tsl * tw: fc * HL + tsl * tw + tw]
                        a_g = S.add("scalar", lambda e, b=PB[bk[tsl]], dst=dst, e_=e_, fc=fc, tw=tw:
                                    e.activation(dst, b[:, 0:tw], AF.Gelu,
                                                 bias=b1_s[:, e_ * FCH + fc: e_ * FCH + fc + 1],
                                                 scale=1.0),
                                    waits=[(sTE, te_mm), (sTE, wo_last_te),
                                           (sSD, SD_CONST)], inc=sACT)
                        rd[w1_bids[tsl]] = (sACT, a_g)
                    gelu_act[fc] = a_g
                    w1_fc_te[fc] = sTE.n
                    if e_ == 0 and fc % 2 == 1 and fc // 2 < THC:
                        emit_acc_chunk(th, fc // 2)
                    if fc == 7:
                        ffn_w_load(th, e_, 2, 1)
                ffn_w_load(nth, ne, 1, 0, te_guard=w1_fc_te[11], te_guard2=w1_fc_te[15])
                for t8 in range(THC):
                    bk = [2 * (t8 % 2), 2 * (t8 % 2) + 1]
                    w2_bids = [("pb", bk[0]), ("pb", bk[1])]
                    for fc in range(FCH):
                        hf = fc // 8
                        gd = gd_half[(th, e_, 2, hf)]
                        slot = half_slot[(th, e_, 2, hf)]
                        lw = hidT[:, fc * (T // 2) + t8 * 128: fc * (T // 2) + t8 * 128 + 128]
                        S.add("tensor", lambda e, lw=lw: e.ldweights(lw),
                              waits=[(sGD, gd), (sACT, gelu_act[fc])] +
                                    (bank_w(w2_bids[0]) + bank_w(w2_bids[1]) if fc == 0 else []))
                        for hs in range(2):
                            rhs = fsl[slot][:, (fc % 8) * 1024 + hs * 512:
                                            (fc % 8) * 1024 + hs * 512 + 512]
                            te_mm = S.add("tensor", lambda e, b=PB[bk[hs]], lw=lw, rhs=rhs, fc=fc:
                                          e.matmul(b[:, 0:512], lw, rhs,
                                                   start=(fc == 0), stop=(fc == FCH - 1)),
                                          inc=sTE)
                    wr[w2_bids[0]] = (sTE, te_mm)
                    wr[w2_bids[1]] = (sTE, te_mm)
                    tc = th * THC + t8
                    wcol = wS[:, tc * E + e_: tc * E + e_ + 1]

                    def stt_fn(e, t8=t8, wcol=wcol, bk=bk):
                        ac = acc_ch(t8)
                        e.drain()
                        e.scalar_tensor_tensor(ac[:, 0:512], PB[bk[0]][:, 0:512], wcol,
                                               ac[:, 0:512], ALU.mult, ALU.add)
                        e.drain()
                        return e.scalar_tensor_tensor(ac[:, 512:1024], PB[bk[1]][:, 0:512], wcol,
                                                      ac[:, 512:1024], ALU.mult, ALU.add)
                    d_st = S.add("vector", stt_fn,
                                 waits=[(sTE, te_mm), (sSD, acc_reload_sd[(th, t8)]),
                                        (sDVE, max(acc_b2d[(th, t8)],
                                                   w_done_dve.get(TC - 1, 0)))], inc=sDVE)
                    rd[w2_bids[0]] = (sDVE, d_st)
                    rd[w2_bids[1]] = (sDVE, d_st)
                    if e_ == E - 1 and t8 % 2 == 1:
                        emit_final(th, t8 - 1)
                        emit_final(th, t8)
                ffn_w_load(nth, ne, 1, 1)
                ffn_w_load(nth, ne, 2, 0)

        SD_ALL = sSD.n
        GD_ALL = sGD.n
        S.add("sync", lambda e: e.nop(), waits=[(sSD, SD_ALL), (sGD, GD_ALL)])

        S.emit(block)
    return nc


def _host_consts(T):
    import ml_dtypes
    BF = ml_dtypes.bfloat16
    TC = T // 128
    sl = np.asarray(_slopes(NH), dtype=np.float32)
    amap, ncol = _acol_map(T)
    karr = np.arange(BLK, dtype=np.float32) - np.float32(BLK - 1)
    bias_ref = (sl[:, None] * karr[None, :]).astype(np.float32)
    alibi = np.zeros((128, ncol), np.float32)
    for (h, qb, kb), col in amap.items():
        qw = QW[h]
        qcen = qb * qw + qw // 2
        ks = kb * 128
        alibi[:, col] = (bias_ref[h, ks:ks + 128].astype(np.float64)
                         - np.float64(bias_ref[h, qcen])).astype(np.float32)
    masks = np.zeros((128, 4 * 512), np.float32)
    p = np.arange(128, dtype=np.float32)
    j = np.arange(512, dtype=np.float32)
    for oi in range(4):
        masks[:, oi * 512:(oi + 1) * 512] = (oi * 128 + p[:, None]) <= j[None, :]
    ident = np.eye(128, dtype=np.float32)
    bcon = np.zeros((128, 128), np.float32)
    bcon[32, :] = 1.0
    bcon[64, :] = 1.0
    return {
        "alibi": alibi,
        "masks": masks.astype(BF),
        "ident": ident.astype(BF),
        "bcon": bcon.astype(BF),
    }


def kernel(L0_attn_x, L0_attn_Wq, L0_attn_Wk, L0_attn_Wv, L0_attn_Wo,
           L0_ffn_ln_g, L0_ffn_ln_b, L0_ffn_rW, L0_ffn_rb, L0_ffn_W1,
           L0_ffn_b1, L0_ffn_W2, L0_ffn_b2):
    T = 2048
    x = np.ascontiguousarray(np.asarray(L0_attn_x, dtype=np.float32))
    consts = _host_consts(T)
    lg = np.asarray(L0_ffn_ln_g, np.float32).reshape(CCH, 128).T
    lb = np.asarray(L0_ffn_ln_b, np.float32).reshape(CCH, 128).T
    lgb = np.concatenate([lg, lb], axis=1)
    b1rr = (np.asarray(L0_ffn_b1, np.float32).reshape(E, FCH, 128)
            .transpose(2, 0, 1).reshape(128, E * FCH))
    rbc = np.broadcast_to(np.asarray(L0_ffn_rb, np.float32), (128, E))
    cf32 = np.ascontiguousarray(np.concatenate(
        [lgb, b1rr, rbc, consts["alibi"]], axis=1))
    cb16 = np.ascontiguousarray(np.concatenate(
        [consts["masks"], consts["bcon"]], axis=1))
    common = {
        "Wq": np.ascontiguousarray(L0_attn_Wq, np.float32),
        "Wk": np.ascontiguousarray(L0_attn_Wk, np.float32),
        "Wv": np.ascontiguousarray(L0_attn_Wv, np.float32),
        "Wo": np.ascontiguousarray(L0_attn_Wo, np.float32),
        "W1": np.ascontiguousarray(L0_ffn_W1, np.float32),
        "W2": np.ascontiguousarray(L0_ffn_W2, np.float32),
        "b2": np.ascontiguousarray(L0_ffn_b2, np.float32),
        "rW": np.ascontiguousarray(L0_ffn_rW, np.float32),
        "cf32": cf32, "cb16": cb16, "ident": consts["ident"],
    }
    nc = build_core_program(T)
    in_maps = [dict(common, x=np.ascontiguousarray(x[c])) for c in range(8)]
    res = run_bass_kernel_spmd(nc, in_maps, core_ids=list(range(8)))
    return np.stack([np.asarray(res.results[c]["y"]) for c in range(8)], axis=0)
